# version 03
# baseline (speedup 1.0000x reference)
import numpy as np
import ml_dtypes
from contextlib import ExitStack

import concourse.bass as bass
import concourse.mybir as mybir
from concourse.bass_utils import run_bass_kernel_spmd

F32 = mybir.dt.float32
BF16 = mybir.dt.bfloat16
AF = mybir.ActivationFunctionType
ALU = mybir.AluOpType
AX = mybir.AxisListType

S = 4096
DM = 4096
DH = 11008
NHC = DH // 128
RANK = 16
NT = 4
TQ = 512
SWEEPS = [8, 16, 24, 32]
ROWS_P = [0, 1536, 2048, 3584]
ROWS_Q = [512, 1024, 2560, 3072]
NROW = NT * TQ
EPS = 1e-5
EXPB = 18.0
QSCALE = float(1.0 / np.sqrt(128.0))
LN64 = float(np.log(64.0))
EPS_RAW = float(4096.0 * EPS)

_DBG = {}


class SemBank:
    def __init__(self, nc, es):
        self.nc, self.es = nc, es
        self.h = {}
        self.v = {}

    def new(self, name):
        self.h[name] = self.es.enter_context(self.nc.semaphore(name))
        self.v[name] = 0

    def bump(self, name, k=1):
        self.v[name] += k
        return self.h[name], self.v[name]

    def val(self, name):
        return self.v[name]

    def hnd(self, name):
        return self.h[name]


class Banks:
    def __init__(self, sb, n=8):
        self.sb = sb
        self.n = n
        for b in range(n):
            sb.new(f"bk{b}w")
            sb.new(f"bk{b}f")

    def acquire(self, eng, b):
        need = self.sb.val(f"bk{b}f")
        prev_w = self.sb.val(f"bk{b}w")
        if prev_w > 0:
            eng.wait_ge(self.sb.hnd(f"bk{b}f"), prev_w)

    def produced(self, inst, b):
        h, _ = self.sb.bump(f"bk{b}w")
        inst.then_inc(h, 1)
        return self.sb.val(f"bk{b}w")

    def consume_wait(self, eng, b, w=None):
        eng.wait_ge(self.sb.hnd(f"bk{b}w"), w if w is not None else self.sb.val(f"bk{b}w"))

    def freed(self, inst, b):
        h, _ = self.sb.bump(f"bk{b}f")
        inst.then_inc(h, 1)


def build_program():
    nc = bass.Bass("TRN2", target_bir_lowering=False)

    din = {}
    def dram_in(name, shape, dt=F32):
        din[name] = nc.dram_tensor(name, shape, dt, kind="ExternalInput").ap()
        return din[name]

    xb_perm = dram_in("xb_perm", [S, DM])
    x_own = dram_in("x_own", [NROW, DM])
    masks = dram_in("masks", [NT * 8 * 128, TQ], BF16)
    WqT = dram_in("WqT", [DM, DM])
    WkvT = dram_in("WkvT", [DM, 512])
    WkupT = dram_in("WkupT", [256, 1024])
    WvupT = dram_in("WvupT", [256, 1024])
    WoT = dram_in("WoT", [DM, DM])
    gateT = dram_in("gateT", [DM, DH])
    upT = dram_in("upT", [DM, DH])
    downT = dram_in("downT", [DH, DM])
    gateAT = dram_in("gateAT", [DM, RANK])
    gateBT = dram_in("gateBT", [RANK, DH])
    upAT = dram_in("upAT", [DM, RANK])
    upBT = dram_in("upBT", [RANK, DH])
    downAT = dram_in("downAT", [DH, RANK])
    downBT_ = dram_in("downBT", [RANK, DM])

    out = nc.dram_tensor("out", [NROW, DM], F32, kind="ExternalOutput").ap()

    q_sc = nc.dram_tensor("q_sc", [NT * 32 * 128, TQ], BF16, kind="Internal").ap()
    o_sc = nc.dram_tensor("o_sc", [32 * 128, NROW], BF16, kind="Internal").ap()
    kv_sc = nc.dram_tensor("kv_sc", [512, S], BF16, kind="Internal").ap()
    ln2_sc = nc.dram_tensor("ln2_sc", [32 * 128, NROW], BF16, kind="Internal").ap()
    ao_sc = nc.dram_tensor("ao_sc", [NROW, DM], F32, kind="Internal").ap()
    x1_sc = nc.dram_tensor("x1_sc", [NROW, DM], F32, kind="Internal").ap()
    h_sc = nc.dram_tensor("h_sc", [16 * NHC * 128, 128], BF16, kind="Internal").ap()
    d_sc = nc.dram_tensor("d_sc", [NROW, DM], F32, kind="Internal").ap()

    dbg_outs = {}
    for nm in _DBG.get("dumps", []):
        shp = {"kv": [512, S], "lnT": [128, 32 * NROW], "q": [NT * 32 * 128, TQ],
               "o": [32 * 128, NROW], "ao": [NROW, DM], "x1": [NROW, DM],
               "ln2": [32 * 128, NROW], "h": [16 * NHC * 128, 128],
               "d": [NROW, DM]}[nm]
        dt = BF16 if nm in ("kv", "lnT", "q", "o", "ln2", "h") else F32
        dbg_outs[nm] = nc.dram_tensor("dbg_" + nm, shp, dt, kind="ExternalOutput").ap()

    es = ExitStack()
    sb = SemBank(nc, es)
    for name in ["xld0", "xld1", "wkld0", "wkld1", "wkld2", "wkld3", "wkkup",
                 "pnl0", "pnl1", "qst0", "qst1", "qtl0", "qtl1", "mld0", "mld1",
                 "tdone", "mdone", "selc", "lr1", "bcd", "ost0", "ost1", "pvd",
                 "upe", "oscd", "lse", "lrc",
                 "consts", "sq", "var", "rstd", "negmr", "apply",
                 "tr", "trev", "ymm", "yev", "scr", "wk_cast",
                 "kvsp", "pan_cast", "qmm", "qev", "kv_ld", "pexp", "pmask",
                 "pj", "sq2", "var2", "rstd2", "negmr2", "ap2", "x1d",
                 "gsl", "hm", "dbc", "dbg"]:
        sb.new(name)
    bk = Banks(sb, 8)

    def phase_join(engs):
        for en in engs:
            def f(e):
                h_, _ = sb.bump("pj")
                e.drain().then_inc(h_, 1)
            getattr(blk, en)(f)

    PS = [es.enter_context(nc.psum_tensor(f"ps{b}", [128, 512], F32)) for b in range(8)]
    ident = es.enter_context(nc.sbuf_tensor("ident", [128, 128], BF16))
    onesb = es.enter_context(nc.sbuf_tensor("onesb", [128, 128], BF16))
    ones1 = es.enter_context(nc.sbuf_tensor("ones1", [1, 128], F32))
    inds = es.enter_context(nc.sbuf_tensor("inds", [128, 16], BF16))
    cst = es.enter_context(nc.sbuf_tensor("cst", [128, 8], F32))
    stt = es.enter_context(nc.sbuf_tensor("stt", [128, 16], F32))
    kv_es = ExitStack()
    kupb = kv_es.enter_context(nc.sbuf_tensor("kupb", [128, 2 * 1024], BF16))
    vupb = kv_es.enter_context(nc.sbuf_tensor("vupb", [128, 2 * 1024], BF16))

    blk = es.enter_context(nc.Block())

    NPH = _DBG.get("phases", 99)

    def ph_consts():
        def g(e):
            e.memset(onesb[:], 1.0)
            e.memset(inds[:], 0.0)
            e.drain()
            for h in range(4):
                e.memset(inds[:, 5 * h:5 * h + 1], 1.0)
            e.memset(ones1[:], 1.0)
            e.memset(cst[:, 0:1], EPS_RAW)
            e.memset(cst[:, 1:2], LN64)
            e.memset(cst[:, 2:3], -EXPB)
            e.memset(cst[:, 3:4], -0.5)
            e.memset(cst[:, 4:5], -1.0)
            e.drain()
            h_, v = sb.bump("consts")
            e.affine_select(out=ident[:], in_=onesb[:], pattern=[[1, 128]],
                            compare_op=ALU.is_equal, fill=0.0, base=0,
                            channel_multiplier=-1).then_inc(h_, 1)
        blk.gpsimd(g)
    ph_consts()
    CONSTS_DONE = sb.val("consts")


    def stats_vec_j(e, j, p, xrow_ap, base):
        if j >= 1:
            e.drain()
        e.wait_ge(sb.hnd(f"xld{j % 2}"), base["xld"][j % 2] + 16 * (j // 2 + 1))
        e.tensor_reduce(out=stt[:, 8 * p:8 * p + 1], in_=xrow_ap,
                        axis=AX.X, op=ALU.add)
        e.drain()
        e.wait_ge(sb.hnd("sq"), base["sq"] + j + 1)
        if j >= 1:
            e.wait_ge(sb.hnd("rstd"), base["rstd"] + j)
        e.tensor_scalar(out=stt[:, 2:3], in0=stt[:, 8 * p:8 * p + 1],
                        scalar1=stt[:, 8 * p:8 * p + 1], scalar2=float(1.0 / DM),
                        op0=ALU.mult, op1=ALU.mult)
        e.drain()
        h_, _ = sb.bump("var")
        e.tensor_tensor(out=stt[:, 2:3], in0=stt[:, 8 * p + 1:8 * p + 2],
                        in1=stt[:, 2:3], op=ALU.subtract).then_inc(h_, 1)
        e.wait_ge(sb.hnd("rstd"), base["rstd"] + j + 1)
        e.drain()
        h_, _ = sb.bump("negmr")
        e.tensor_scalar(out=stt[:, 4:5], in0=stt[:, 8 * p:8 * p + 1],
                        scalar1=stt[:, 3:4], scalar2=float(-1.0 / DM),
                        op0=ALU.mult, op1=ALU.mult).then_inc(h_, 1)

    def stats_act_j(e, j, p, xrow_ap, out_ap, base, extra_tr_wait=None,
                    extra_wait=None):
        if j >= 1:
            e.drain()
        e.wait_ge(sb.hnd(f"xld{j % 2}"), base["xld"][j % 2] + 16 * (j // 2 + 1))
        if j >= 2:
            e.wait_ge(sb.hnd("var"), base["var"] + j - 1)
        h_, _ = sb.bump("sq")
        e.activation(out=sqd_ref[0][:], in_=xrow_ap, func=AF.Square,
                     accum_out=stt[:, 8 * p + 1:8 * p + 2]).then_inc(h_, 1)
        e.wait_ge(sb.hnd("var"), base["var"] + j + 1)
        e.activation(out=stt[:, 2:3], in_=stt[:, 2:3], func=AF.Ln,
                     bias=cst[:, 0:1], scale=1.0)
        e.drain()
        h_, _ = sb.bump("rstd")
        e.activation(out=stt[:, 3:4], in_=stt[:, 2:3], func=AF.Exp,
                     bias=cst[:, 1:2], scale=cst[:, 3:4]).then_inc(h_, 1)
        e.wait_ge(sb.hnd("negmr"), base["negmr"] + j + 1)
        e.drain()
        if extra_tr_wait is not None:
            e.wait_ge(sb.hnd("tr"), extra_tr_wait)
        if extra_wait is not None:
            e.wait_ge(sb.hnd(extra_wait[0]), extra_wait[1])
        h_, _ = sb.bump("apply")
        e.activation(out=out_ap, in_=xrow_ap, func=AF.Identity,
                     bias=stt[:, 4:5], scale=stt[:, 3:4]).then_inc(h_, 1)

    sqd_ref = [None]

    def ph_p1():
        p1 = ExitStack()
        xrow = [p1.enter_context(nc.sbuf_tensor(f"xrow{i}", [128, DM], F32)) for i in range(2)]
        lnb = [p1.enter_context(nc.sbuf_tensor(f"lnb{i}", [128, DM], BF16)) for i in range(2)]
        sqd = p1.enter_context(nc.sbuf_tensor("sqd", [128, DM], BF16))
        lnT = p1.enter_context(nc.sbuf_tensor("lnT", [128, 32 * 512], BF16))
        ysb = p1.enter_context(nc.sbuf_tensor("ysb", [128, 4 * 512], BF16))
        kcmpT = p1.enter_context(nc.sbuf_tensor("kcmpT", [128, 2 * S], BF16))
        vcmpT = p1.enter_context(nc.sbuf_tensor("vcmpT", [128, 2 * S], BF16))
        wkst = [p1.enter_context(nc.sbuf_tensor(f"wkst{i}", [128, 512], F32)) for i in range(4)]
        wkvb = p1.enter_context(nc.sbuf_tensor("wkvb", [128, 32 * 512], BF16))
        kupst = p1.enter_context(nc.sbuf_tensor("kupst", [128, 4096], F32))

        NJ = 32

        def f_sync(e):
            for i in range(32):
                if i >= 4:
                    e.wait_ge(sb.hnd("wk_cast"), i - 3)
                h_, _ = sb.bump(f"wkld{i % 4}", 16)
                e.dma_start(out=wkst[i % 4][:],
                            in_=WkvT[128 * i:128 * (i + 1), :]).then_inc(h_, 16)
            h_, _ = sb.bump("wkkup", 16)
            e.dma_start(out=kupst[:, 0:2048].rearrange("p (c n) -> p c n", c=2),
                        in_=WkupT.rearrange("(c p) n -> p c n", p=128)).then_inc(h_, 16)
            h_, _ = sb.bump("wkkup", 16)
            e.dma_start(out=kupst[:, 2048:4096].rearrange("p (c n) -> p c n", c=2),
                        in_=WvupT.rearrange("(c p) n -> p c n", p=128)).then_inc(h_, 16)
            for j in range(NJ):
                if j >= 2:
                    e.wait_ge(sb.hnd("apply"), j - 1)
                h_, _ = sb.bump(f"xld{j % 2}", 16)
                e.dma_start(out=xrow[j % 2][:],
                            in_=xb_perm[128 * j:128 * (j + 1), :]).then_inc(h_, 16)

        def f_pool(e):
            for i in range(32):
                e.wait_ge(sb.hnd(f"wkld{i % 4}"), 16 * (i // 4 + 1))
                h_, _ = sb.bump("wk_cast")
                e.tensor_copy(out=wkvb[:, 512 * i:512 * (i + 1)],
                              in_=wkst[i % 4][:]).then_inc(h_, 1)
            e.wait_ge(sb.hnd("wkkup"), 32)
            e.drain()
            h_, _ = sb.bump("wk_cast")
            e.tensor_copy(out=kupb[:], in_=kupst[:, 0:2048]).then_inc(h_, 1)
            h_, _ = sb.bump("wk_cast")
            e.tensor_copy(out=vupb[:], in_=kupst[:, 2048:4096]).then_inc(h_, 1)

        def f_vec(e):
            base = {"xld": [0, 0], "sq": 0, "var": 0, "rstd": 0, "negmr": 0}
            for j in range(NJ):
                p = j % 2
                stats_vec_j(e, j, p, xrow[p][:], base)
                jj = j % 4
                if jj == 0 and j > 0:
                    e.wait_ge(sb.hnd("ymm"), 4 * (j // 4))
                for fb in range(4):
                    e.wait_ge(sb.hnd("tr"), 4 * j + fb + 1)
                    b = fb % 2
                    src = PS[b][:].bitcast(BF16)[:, 0:1024]
                    dst = lnT[:, :].rearrange("p (c r) -> p c r", r=512)
                    h_, _ = sb.bump("trev")
                    e.tensor_copy(out=dst[:, 8 * fb:8 * fb + 8, 128 * jj:128 * jj + 128],
                                  in_=src.rearrange("p (c r) -> p c r", r=128)).then_inc(h_, 1)
                if jj == 3:
                    g = j // 4
                    e.drain()
                    for kv in range(4):
                        e.wait_ge(sb.hnd("ymm"), 4 * g + kv + 1)
                        h_, _ = sb.bump("yev")
                        e.tensor_copy(out=ysb[:, 512 * kv:512 * (kv + 1)],
                                      in_=PS[4 + kv % 2][:]).then_inc(h_, 1)
                    e.drain()
                    dstT = kcmpT if g < 4 else vcmpT
                    rr = g % 4
                    pbase = 64 * (rr % 2)
                    fbase = S * (rr // 2)
                    for h in range(8):
                        sp = 64 * (h % 2)
                        sf = 512 * (h // 2)
                        h_, _ = sb.bump("scr")
                        e.tensor_copy(
                            out=dstT[pbase:pbase + 64, fbase + 512 * h:fbase + 512 * h + 512],
                            in_=ysb[sp:sp + 64, sf:sf + 512]).then_inc(h_, 1)

        def f_act(e):
            base = {"xld": [0, 0], "sq": 0, "var": 0, "rstd": 0, "negmr": 0}
            sqd_ref[0] = sqd
            for j in range(NJ):
                p = j % 2
                stats_act_j(e, j, p, xrow[p][:], lnb[p][:], base,
                            extra_tr_wait=(4 * (j - 1)) if j >= 2 else None)

        def f_pe(e):
            e.wait_ge(sb.hnd("consts"), CONSTS_DONE)
            for j in range(NJ):
                p = j % 2
                e.wait_ge(sb.hnd("apply"), j + 1)
                for fb in range(4):
                    b = fb % 2
                    nfill = 4 * j + fb
                    if nfill >= 2:
                        e.wait_ge(sb.hnd("trev"), nfill - 1)
                    for t8 in range(8):
                        c = 8 * fb + t8
                        inst = e.transpose(
                            PS[b][:].bitcast(BF16)[:, 128 * t8:128 * (t8 + 1)],
                            lnb[p][:, 128 * c:128 * (c + 1)], ident[:])
                    h_, _ = sb.bump("tr")
                    inst.then_inc(h_, 1)
                if j % 4 == 3:
                    g = j // 4
                    if g == 0:
                        e.wait_ge(sb.hnd("wk_cast"), 32)
                    e.wait_ge(sb.hnd("trev"), 16 * (g + 1))
                    for kv in range(4):
                        bb = 4 + kv % 2
                        nuse = 4 * g + kv
                        if nuse >= 2:
                            e.wait_ge(sb.hnd("yev"), nuse - 1)
                        for c in range(32):
                            inst = e.matmul(PS[bb][:],
                                            wkvb[:, 512 * c + 128 * kv:512 * c + 128 * (kv + 1)],
                                            lnT[:, 512 * c:512 * (c + 1)],
                                            start=(c == 0), stop=(c == 31))
                        h_, _ = sb.bump("ymm")
                        inst.then_inc(h_, 1)

        def f_store(e):
            e.wait_ge(sb.hnd("scr"), 64)
            h_, _ = sb.bump("kvsp", 16)
            e.dma_start(out=kv_sc[0:256, :].rearrange("(c p) n -> p c n", p=128),
                        in_=kcmpT[:].rearrange("p (c n) -> p c n", c=2)).then_inc(h_, 16)
            h_, _ = sb.bump("kvsp", 16)
            e.dma_start(out=kv_sc[256:512, :].rearrange("(c p) n -> p c n", p=128),
                        in_=vcmpT[:].rearrange("p (c n) -> p c n", c=2)).then_inc(h_, 16)
            if "kv" in dbg_outs:
                h_, _ = sb.bump("dbg", 32)
                e.dma_start(out=dbg_outs["kv"][0:256, :].rearrange("(c p) n -> p c n", p=128),
                            in_=kcmpT[:].rearrange("p (c n) -> p c n", c=2)).then_inc(h_, 16)
                e.dma_start(out=dbg_outs["kv"][256:512, :].rearrange("(c p) n -> p c n", p=128),
                            in_=vcmpT[:].rearrange("p (c n) -> p c n", c=2)).then_inc(h_, 16)
            e.wait_ge(sb.hnd("kvsp"), sb.val("kvsp"))
            if _DBG.get("dumps"):
                e.wait_ge(sb.hnd("dbg"), sb.val("dbg"))

        blk.sync(f_sync)
        blk.gpsimd(f_pool)
        blk.vector(f_vec)
        blk.scalar(f_act)
        blk.tensor(f_pe)
        blk.scalar(f_store)
        phase_join(["tensor", "vector", "scalar", "gpsimd"])
        p1.close()

    ph_p1()
    if NPH <= 1:
        es.close()
        return nc

    def ph_p2():
        p2o = ExitStack()
        lnTF = p2o.enter_context(nc.sbuf_tensor("lnTF", [128, 32 * NROW], BF16))

        NJ = 16
        b_xld = [sb.val("xld0"), sb.val("xld1")]
        b_ap = sb.val("apply"); b_sq = sb.val("sq")
        b_var = sb.val("var"); b_rstd = sb.val("rstd"); b_neg = sb.val("negmr")
        b_tr = sb.val("tr"); b_trev = sb.val("trev")

        p2 = ExitStack()
        xrow = [p2.enter_context(nc.sbuf_tensor(f"p2xr{i}", [128, DM], F32)) for i in range(2)]
        lnb = [p2.enter_context(nc.sbuf_tensor(f"p2ln{i}", [128, DM], BF16)) for i in range(2)]
        sqd = p2.enter_context(nc.sbuf_tensor("p2sqd", [128, DM], BF16))

        b_pj = sb.val("pj")

        def fa_sync(e):
            e.wait_ge(sb.hnd("pj"), b_pj)
            for j in range(NJ):
                e.wait_ge(sb.hnd("apply"), b_ap + max(j - 1, 0))
                h_, _ = sb.bump(f"xld{j % 2}", 16)
                e.dma_start(out=xrow[j % 2][:],
                            in_=x_own[128 * j:128 * (j + 1), :]).then_inc(h_, 16)

        def fa_vec(e):
            base = {"xld": b_xld, "sq": b_sq, "var": b_var, "rstd": b_rstd,
                    "negmr": b_neg}
            for j in range(NJ):
                p = j % 2
                stats_vec_j(e, j, p, xrow[p][:], base)
                for fb in range(4):
                    e.wait_ge(sb.hnd("tr"), b_tr + 4 * j + fb + 1)
                    b = fb % 2
                    src = PS[b][:].bitcast(BF16)[:, 0:1024]
                    dst = lnTF[:, :].rearrange("p (c r) -> p c r", r=NROW)
                    h_, _ = sb.bump("trev")
                    e.tensor_copy(out=dst[:, 8 * fb:8 * fb + 8, 128 * j:128 * j + 128],
                                  in_=src.rearrange("p (c r) -> p c r", r=128)).then_inc(h_, 1)

        def fa_act(e):
            base = {"xld": b_xld, "sq": b_sq, "var": b_var, "rstd": b_rstd,
                    "negmr": b_neg}
            sqd_ref[0] = sqd
            for j in range(NJ):
                p = j % 2
                stats_act_j(e, j, p, xrow[p][:], lnb[p][:], base,
                            extra_tr_wait=(b_tr + 4 * (j - 1)) if j >= 2 else None)

        def fa_pe(e):
            for j in range(NJ):
                p = j % 2
                e.wait_ge(sb.hnd("apply"), b_ap + j + 1)
                for fb in range(4):
                    b = fb % 2
                    nfill = 4 * j + fb
                    if nfill >= 2:
                        e.wait_ge(sb.hnd("trev"), b_trev + nfill - 1)
                    for t8 in range(8):
                        c = 8 * fb + t8
                        inst = e.transpose(
                            PS[b][:].bitcast(BF16)[:, 128 * t8:128 * (t8 + 1)],
                            lnb[p][:, 128 * c:128 * (c + 1)], ident[:])
                    h_, _ = sb.bump("tr")
                    inst.then_inc(h_, 1)

        blk.sync(fa_sync)
        blk.vector(fa_vec)
        blk.scalar(fa_act)
        blk.tensor(fa_pe)
        phase_join(["tensor", "vector", "scalar"])
        p2.close()

        p2 = ExitStack()
        wpan = [p2.enter_context(nc.sbuf_tensor(f"wpan{i}", [128, 32 * 128], F32)) for i in range(2)]
        wpanb = [p2.enter_context(nc.sbuf_tensor(f"wpanb{i}", [128, 32 * 128], BF16)) for i in range(2)]
        qstg = [p2.enter_context(nc.sbuf_tensor(f"qstg{i}", [128, 512], BF16)) for i in range(2)]

        b_pj2 = sb.val("pj")

        def fb_sync(e):
            e.wait_ge(sb.hnd("pj"), b_pj2)
            for qc in range(32):
                if qc >= 2:
                    e.wait_ge(sb.hnd("pan_cast"), qc - 1)
                h_, _ = sb.bump(f"pnl{qc % 2}", 16)
                e.dma_start(out=wpan[qc % 2][:].rearrange("p (c n) -> p c n", c=32),
                            in_=WqT[:, 128 * qc:128 * (qc + 1)]
                            .rearrange("(c p) n -> p c n", p=128)).then_inc(h_, 16)

        def fb_pool(e):
            for qc in range(32):
                e.wait_ge(sb.hnd(f"pnl{qc % 2}"), 16 * (qc // 2 + 1))
                if qc >= 2:
                    e.wait_ge(sb.hnd("qmm"), 4 * (qc - 1))
                h_, _ = sb.bump("pan_cast")
                e.tensor_copy(out=wpanb[qc % 2][:], in_=wpan[qc % 2][:]).then_inc(h_, 1)

        def fb_vec(e):
            for qc in range(32):
                for rb in range(4):
                    n = 4 * qc + rb
                    e.wait_ge(sb.hnd("qmm"), n + 1)
                    if n >= 2:
                        e.wait_ge(sb.hnd(f"qst{n % 2}"), 16 * (n // 2))
                    h_, _ = sb.bump("qev")
                    e.tensor_scalar(out=qstg[n % 2][:], in0=PS[rb % 2][:],
                                    scalar1=QSCALE, scalar2=None,
                                    op0=ALU.mult).then_inc(h_, 1)

        def fb_pe(e):
            e.wait_ge(sb.hnd("trev"), b_trev + 4 * NJ)
            for qc in range(32):
                e.wait_ge(sb.hnd("pan_cast"), qc + 1)
                for rb in range(4):
                    n = 4 * qc + rb
                    bb = rb % 2
                    if n >= 2:
                        e.wait_ge(sb.hnd("qev"), n - 1)
                    for c in range(32):
                        inst = e.matmul(PS[bb][:],
                                        wpanb[qc % 2][:, 128 * c:128 * (c + 1)],
                                        lnTF[:, NROW * c + 512 * rb:NROW * c + 512 * (rb + 1)],
                                        start=(c == 0), stop=(c == 31))
                    h_, _ = sb.bump("qmm")
                    inst.then_inc(h_, 1)

        def fb_store(e):
            for qc in range(32):
                for rb in range(4):
                    n = 4 * qc + rb
                    e.wait_ge(sb.hnd("qev"), n + 1)
                    h_, _ = sb.bump(f"qst{n % 2}", 16)
                    e.dma_start(out=q_sc[(rb * 32 + qc) * 128:(rb * 32 + qc) * 128 + 128, :],
                                in_=qstg[n % 2][:]).then_inc(h_, 16)
            e.wait_ge(sb.hnd("qst0"), sb.val("qst0"))
            e.wait_ge(sb.hnd("qst1"), sb.val("qst1"))

        blk.sync(fb_sync)
        blk.gpsimd(fb_pool)
        blk.vector(fb_vec)
        blk.tensor(fb_pe)
        blk.scalar(fb_store)
        phase_join(["tensor", "vector", "gpsimd", "scalar"])
        p2.close()
        p2o.close()

    ph_p2()
    if NPH <= 2:
        es.close()
        return nc


    def ph_p3():
        p3 = ExitStack()
        kcmpT = p3.enter_context(nc.sbuf_tensor("p3kc", [128, 2 * S], BF16))
        vcmpT = p3.enter_context(nc.sbuf_tensor("p3vc", [128, 2 * S], BF16))
        qbuf = [p3.enter_context(nc.sbuf_tensor(f"qb{i}", [128, 32 * 512], BF16)) for i in range(2)]
        mbuf = [p3.enter_context(nc.sbuf_tensor(f"mb{i}", [128, 8 * 512], BF16)) for i in range(2)]
        kseg = p3.enter_context(nc.sbuf_tensor("kseg", [128, 1024], BF16))
        vseg = p3.enter_context(nc.sbuf_tensor("vseg", [128, 8 * 128], BF16))
        pbuf = p3.enter_context(nc.sbuf_tensor("pbuf", [128, 8 * 512], BF16))
        lsb = p3.enter_context(nc.sbuf_tensor("lsb", [4, 512], F32))
        lrec = p3.enter_context(nc.sbuf_tensor("lrec", [4, 512], F32))
        selt = p3.enter_context(nc.sbuf_tensor("selt", [4, 512], F32))
        recb = p3.enter_context(nc.sbuf_tensor("recb", [128, 512], F32))
        osb = [p3.enter_context(nc.sbuf_tensor(f"osb{i}", [128, 512], BF16)) for i in range(2)]

        b_qev = sb.val("qev")
        b_qst = [sb.val("qst0"), sb.val("qst1")]
        b_kvsp = sb.val("kvsp")
        b_pj3 = sb.val("pj")

        freemap = {b: None for b in range(8)}
        nw = {b: 0 for b in range(8)}
        n_sop = [0]; n_msk = [0]; n_upe = [0]; n_osc = [0]; n_swp = [0]
        pvd_hist = [0]
        sched = []
        for t in range(NT):
            tg = []
            for g in range(8):
                n = SWEEPS[t]
                rot = [5, 6, 7]
                ri = [0]

                def nb():
                    b = rot[ri[0] % 3]
                    ri[0] += 1
                    nw[b] += 1
                    fw = freemap[b]
                    return b, nw[b], fw

                segs = []
                for s in range(n // 8):
                    ups = []
                    for half in range(2):
                        b, w, fw = nb()
                        n_upe[0] += 1
                        ups.append(dict(k="k", s=s, half=half, b=b, bw=w, fw=fw,
                                        upe=n_upe[0]))
                        freemap[b] = ("upe", n_upe[0])
                    for kc in range(8):
                        b, w, fw = nb()
                        n_upe[0] += 1
                        ups.append(dict(k="v", s=s, kc=kc, b=b, bw=w, fw=fw,
                                        upe=n_upe[0]))
                        freemap[b] = ("upe", n_upe[0])
                    sops = []
                    for c in range(8):
                        C = 8 * s + c
                        for h in range(4):
                            b, w, fw = nb()
                            n_sop[0] += 1
                            masked = C >= n - 8
                            if masked:
                                n_msk[0] += 1
                            sops.append(dict(s=s, C=C, h=h, b=b, bw=w, fw=fw,
                                             masked=masked, isop=n_sop[0],
                                             imsk=n_msk[0],
                                             slot=(n_sop[0] - 1) % 8))
                            freemap[b] = ("pexp", n_sop[0])
                    segs.append(dict(ups=ups, sops=sops,
                                     war_pvd=pvd_hist[-1],
                                     upe_done=n_upe[0]))
                    pvd_hist.append(n_sop[0])
                ofw = [freemap[hh] for hh in range(4)]
                lfw = freemap[4]
                olast = [0] * 4
                for op in segs[-1]["sops"]:
                    if op["C"] == n - 1:
                        olast[op["h"]] = op["isop"]
                lbwv = None
                nw[4] += 1
                lbwv = nw[4]
                bcb = []
                for hh in range(4):
                    b, w, fw = nb()
                    n_osc[0] += 1
                    bcb.append(dict(b=b, bw=w, fw=fw, oi=n_osc[0]))
                    freemap[b] = ("oscd", n_osc[0])
                    freemap[hh] = ("oscd", n_osc[0])
                n_swp[0] += 1
                freemap[4] = ("lse", n_swp[0])
                tg.append(dict(segs=segs, ofw=ofw, lfw=lfw, olast=olast,
                               lbw=lbwv, bcb=bcb, swp=n_swp[0]))
            sched.append(tg)

        def wait_fw(e, fw):
            if fw is not None:
                e.wait_ge(sb.hnd(fw[0]), fw[1])

        def emit_stores(e, t):
            for g in range(8):
                for hh in range(4):
                    oi = 32 * t + 4 * g + hh
                    e.wait_ge(sb.hnd("oscd"), oi + 1)
                    h_, _ = sb.bump(f"ost{oi % 2}", 16)
                    r0 = (4 * g + hh) * 128
                    e.dma_start(out=o_sc[r0:r0 + 128, 512 * t:512 * t + 512],
                                in_=osb[oi % 2][:]).then_inc(h_, 16)

        def f_sync(e):
            e.wait_ge(sb.hnd("pj"), b_pj3)
            e.wait_ge(sb.hnd("kvsp"), b_kvsp)
            h_, _ = sb.bump("kv_ld", 16)
            e.dma_start(out=kcmpT[:].rearrange("p (c n) -> p c n", c=2),
                        in_=kv_sc[0:256, :].rearrange("(c p) n -> p c n", p=128)).then_inc(h_, 16)
            h_, _ = sb.bump("kv_ld", 16)
            e.dma_start(out=vcmpT[:].rearrange("p (c n) -> p c n", c=2),
                        in_=kv_sc[256:512, :].rearrange("(c p) n -> p c n", p=128)).then_inc(h_, 16)
            e.wait_ge(sb.hnd("qst0"), b_qst[0])
            e.wait_ge(sb.hnd("qst1"), b_qst[1])
            for t in range(NT):
                if t >= 2:
                    e.wait_ge(sb.hnd("tdone"), t - 1)
                h_, _ = sb.bump(f"qtl{t % 2}", 16)
                e.dma_start(out=qbuf[t % 2][:].rearrange("p (c n) -> p c n", c=32),
                            in_=q_sc[4096 * t:4096 * (t + 1), :]
                            .rearrange("(c p) n -> p c n", p=128)).then_inc(h_, 16)
                if t >= 2:
                    e.wait_ge(sb.hnd("mdone"), t - 1)
                h_, _ = sb.bump(f"mld{t % 2}", 16)
                e.dma_start(out=mbuf[t % 2][:].rearrange("p (c n) -> p c n", c=8),
                            in_=masks[8 * 128 * t:8 * 128 * (t + 1), :]
                            .rearrange("(c p) n -> p c n", p=128)).then_inc(h_, 16)
                if t >= 1:
                    emit_stores(e, t - 1)
            emit_stores(e, NT - 1)
            e.wait_ge(sb.hnd("ost0"), sb.val("ost0"))
            e.wait_ge(sb.hnd("ost1"), sb.val("ost1"))

        def f_pool(e):
            e.wait_ge(sb.hnd("consts"), CONSTS_DONE)
            e.memset(selt[:], 1.0)
            e.drain()
            e.affine_select(out=selt[:], in_=selt[:], pattern=[[1, 512]],
                            compare_op=ALU.is_ge, fill=0.0, base=0,
                            channel_multiplier=-128)
            e.drain()
            h_, _ = sb.bump("selc")
            e.affine_select(out=selt[:], in_=selt[:], pattern=[[-1, 512]],
                            compare_op=ALU.is_ge, fill=0.0, base=127,
                            channel_multiplier=128).then_inc(h_, 1)

        def emit_lpv(e, n, ent, op):
            C, h = op["C"], op["h"]
            if op["masked"]:
                e.wait_ge(sb.hnd("pmask"), op["imsk"])
            else:
                e.wait_ge(sb.hnd("pexp"), op["isop"])
            first = (C == 0 and h == 0)
            firsth = (C == 0)
            last = (C == n - 1 and h == 3)
            lasth = (C == n - 1)
            pb = pbuf[:, 512 * op["slot"]:512 * op["slot"] + 512]
            if firsth:
                wait_fw(e, ent["ofw"][h])
            if first:
                wait_fw(e, ent["lfw"])
            inst = e.matmul(PS[4][0:4, :], inds[:, 4 * h:4 * h + 4], pb,
                            start=first, stop=last)
            if last:
                h_, _ = sb.bump("bk4w")
                inst.then_inc(h_, 1)
            inst2 = e.matmul(PS[h][:], vseg[:, 128 * (C % 8):128 * (C % 8) + 128],
                             pb, start=firsth, stop=lasth)
            h_, _ = sb.bump("pvd")
            inst2.then_inc(h_, 1)

        def f_pe(e):
            e.wait_ge(sb.hnd("qev"), b_qev)
            e.wait_ge(sb.hnd("kv_ld"), 32)
            e.wait_ge(sb.hnd("selc"), 1)
            for t in range(NT):
                e.wait_ge(sb.hnd(f"qtl{t % 2}"), 16 * (t // 2 + 1))
                n = SWEEPS[t]
                for g in range(8):
                    ent = sched[t][g]
                    for seg in ent["segs"]:
                        for op in seg["ups"]:
                            b = op["b"]
                            wait_fw(e, op["fw"])
                            if op["k"] == "k":
                                s, half = op["s"], op["half"]
                                for cc in range(2):
                                    inst = e.matmul(
                                        PS[b][:],
                                        kupb[:, 1024 * cc + 128 * g:1024 * cc + 128 * g + 128],
                                        kcmpT[:, 4096 * cc + 1024 * s + 512 * half:
                                              4096 * cc + 1024 * s + 512 * half + 512],
                                        start=(cc == 0), stop=(cc == 1))
                            else:
                                s, kc = op["s"], op["kc"]
                                for cc in range(2):
                                    inst = e.matmul(
                                        PS[b][:, 0:128],
                                        vcmpT[:, 4096 * cc + 1024 * s + 128 * kc:
                                              4096 * cc + 1024 * s + 128 * kc + 128],
                                        vupb[:, 1024 * cc + 128 * g:1024 * cc + 128 * g + 128],
                                        start=(cc == 0), stop=(cc == 1))
                            h_, _ = sb.bump(f"bk{b}w")
                            inst.then_inc(h_, 1)
                        pend = []
                        for i, op in enumerate(seg["sops"]):
                            b = op["b"]
                            wait_fw(e, op["fw"])
                            if i == 0:
                                e.wait_ge(sb.hnd("upe"), seg["upe_done"])
                            inst = e.matmul(
                                PS[b][:],
                                kseg[:, 128 * (op["C"] % 8):128 * (op["C"] % 8) + 128],
                                qbuf[t % 2][:, 512 * (4 * g + op["h"]):512 * (4 * g + op["h"]) + 512],
                                start=True, stop=True)
                            h_, _ = sb.bump(f"bk{b}w")
                            inst.then_inc(h_, 1)
                            pend.append(op)
                            if len(pend) > 2:
                                emit_lpv(e, n, ent, pend.pop(0))
                        for op in pend:
                            emit_lpv(e, n, ent, op)
                    e.wait_ge(sb.hnd("lrc"), 8 * t + g + 1)
                    for hh in range(4):
                        bc = ent["bcb"][hh]
                        wait_fw(e, bc["fw"])
                        inst = e.matmul(PS[bc["b"]][:], selt[:, 128 * hh:128 * hh + 128],
                                        lrec[:], start=True, stop=True)
                        h_, _ = sb.bump(f"bk{bc['b']}w")
                        inst.then_inc(h_, 1)
                    h_, _ = sb.bump("bcd")
                    e.drain().then_inc(h_, 1)
                h_, _ = sb.bump("tdone")
                e.drain().then_inc(h_, 1)

        def f_act(e):
            for t in range(NT):
                n = SWEEPS[t]
                for g in range(8):
                    ent = sched[t][g]
                    for seg in ent["segs"]:
                        for op in seg["sops"]:
                            b = op["b"]
                            e.wait_ge(sb.hnd(f"bk{b}w"), op["bw"])
                            if op["isop"] > 8:
                                e.wait_ge(sb.hnd("pvd"), op["isop"] - 8)
                            pb = pbuf[:, 512 * op["slot"]:512 * op["slot"] + 512]
                            h1, _ = sb.bump("pexp")
                            e.activation(out=pb, in_=PS[b][:], func=AF.Exp,
                                         bias=cst[:, 2:3],
                                         scale=1.0).then_inc(h1, 1)
                    e.wait_ge(sb.hnd("lse"), 8 * t + g + 1)
                    e.drain()
                    if 8 * t + g > 0:
                        e.wait_ge(sb.hnd("bcd"), 8 * t + g)
                    h_, _ = sb.bump("lr1")
                    e.activation(out=lrec[:], in_=lsb[:], func=AF.Ln,
                                 scale=1.0).then_inc(h_, 1)
                    e.drain()
                    h_, _ = sb.bump("lrc")
                    e.activation(out=lrec[:], in_=lrec[:], func=AF.Exp,
                                 scale=cst[0:4, 4:5]).then_inc(h_, 1)

        def f_vec(e):
            oi = 0
            for t in range(NT):
                n = SWEEPS[t]
                e.wait_ge(sb.hnd(f"mld{t % 2}"), 16 * (t // 2 + 1))
                for g in range(8):
                    ent = sched[t][g]
                    for seg in ent["segs"]:
                        if seg["war_pvd"] > 0:
                            e.wait_ge(sb.hnd("pvd"), seg["war_pvd"])
                        for op in seg["ups"]:
                            b = op["b"]
                            e.wait_ge(sb.hnd(f"bk{b}w"), op["bw"])
                            h2, _ = sb.bump("upe")
                            if op["k"] == "k":
                                e.tensor_copy(
                                    out=kseg[:, 512 * op["half"]:512 * op["half"] + 512],
                                    in_=PS[b][:]).then_inc(h2, 1)
                            else:
                                e.tensor_copy(
                                    out=vseg[:, 128 * op["kc"]:128 * op["kc"] + 128],
                                    in_=PS[b][:, 0:128]).then_inc(h2, 1)
                        for op in seg["sops"]:
                            if not op["masked"]:
                                continue
                            e.wait_ge(sb.hnd("pexp"), op["isop"])
                            pb = pbuf[:, 512 * op["slot"]:512 * op["slot"] + 512]
                            ci = op["C"] - (n - 8)
                            h_, _ = sb.bump("pmask")
                            e.tensor_tensor(out=pb, in0=pb,
                                            in1=mbuf[t % 2][:, 512 * ci:512 * ci + 512],
                                            op=ALU.mult).then_inc(h_, 1)
                    e.wait_ge(sb.hnd("bk4w"), ent["lbw"])
                    if 8 * t + g > 0:
                        e.wait_ge(sb.hnd("lr1"), 8 * t + g)
                    e.drain()
                    h2, _ = sb.bump("lse")
                    e.tensor_copy(out=lsb[:], in_=PS[4][0:4, :]).then_inc(h2, 1)
                    for hh in range(4):
                        bc = ent["bcb"][hh]
                        e.wait_ge(sb.hnd(f"bk{bc['b']}w"), bc["bw"])
                        e.drain()
                        e.tensor_copy(out=recb[:], in_=PS[bc["b"]][:])
                        e.wait_ge(sb.hnd("pvd"), ent["olast"][hh])
                        if oi >= 2:
                            e.wait_ge(sb.hnd(f"ost{oi % 2}"), 16 * (oi // 2))
                        e.drain()
                        h2, _ = sb.bump("oscd")
                        e.tensor_tensor(out=osb[oi % 2][:], in0=PS[hh][:],
                                        in1=recb[:],
                                        op=ALU.mult).then_inc(h2, 1)
                        oi += 1
                h_, _ = sb.bump("mdone")
                e.drain().then_inc(h_, 1)

        def f_store(e):
            if "o" in dbg_outs:
                e.wait_ge(sb.hnd("ost0"), sb.val("ost0"))
                e.wait_ge(sb.hnd("ost1"), sb.val("ost1"))
                h_, _ = sb.bump("dbg", 16)
                e.dma_start(out=dbg_outs["o"][:, :].rearrange("(c p) n -> p c n", p=128),
                            in_=o_sc[:, :].rearrange("(c p) n -> p c n", p=128)).then_inc(h_, 16)
                e.wait_ge(sb.hnd("dbg"), sb.val("dbg"))

        blk.sync(f_sync)
        blk.gpsimd(f_pool)
        blk.tensor(f_pe)
        blk.scalar(f_act)
        blk.vector(f_vec)
        blk.scalar(f_store)
        phase_join(["tensor", "vector", "scalar", "gpsimd"])
        p3.close()

    ph_p3()
    kv_es.close()
    if NPH <= 3:
        es.close()
        return nc


    def ph_g3():
        g3 = ExitStack()
        otb = g3.enter_context(nc.sbuf_tensor("otb", [128, 32 * 1024], BF16))
        wop = [g3.enter_context(nc.sbuf_tensor(f"wop{i}", [128, 32 * 256], F32)) for i in range(2)]
        wopb = [g3.enter_context(nc.sbuf_tensor(f"wopb{i}", [128, 32 * 256], BF16)) for i in range(2)]
        aostg = [g3.enter_context(nc.sbuf_tensor(f"aostg{i}", [128, 256], F32)) for i in range(2)]

        b_pj = sb.val("pj")
        b_kvld = sb.val("kv_ld")
        b_pnl = [sb.val("pnl0"), sb.val("pnl1")]
        b_cast = sb.val("pan_cast")
        b_mm = sb.val("qmm")
        b_ev = sb.val("qev")
        b_st = [sb.val("qst0"), sb.val("qst1")]
        b_ost = [sb.val("ost0"), sb.val("ost1")]

        def f_sync(e):
            e.wait_ge(sb.hnd("pj"), b_pj)
            e.wait_ge(sb.hnd("ost0"), b_ost[0])
            e.wait_ge(sb.hnd("ost1"), b_ost[1])
            for rg in range(2):
                if rg == 1:
                    e.wait_ge(sb.hnd("qev"), b_ev + 128)
                h_, _ = sb.bump("kv_ld", 16)
                e.dma_start(out=otb[:].rearrange("p (c n) -> p c n", c=32),
                            in_=o_sc[:, 1024 * rg:1024 * (rg + 1)]
                            .rearrange("(c p) n -> p c n", p=128)).then_inc(h_, 16)
                for db in range(16):
                    i = 16 * rg + db
                    if i >= 2:
                        e.wait_ge(sb.hnd("pan_cast"), b_cast + i - 1)
                    h_, _ = sb.bump(f"pnl{i % 2}", 16)
                    e.dma_start(out=wop[i % 2][:].rearrange("p (c n) -> p c n", c=32),
                                in_=WoT[:, 256 * db:256 * (db + 1)]
                                .rearrange("(c p) n -> p c n", p=128)).then_inc(h_, 16)

        def f_pool(e):
            for i in range(32):
                e.wait_ge(sb.hnd(f"pnl{i % 2}"), b_pnl[i % 2] + 16 * (i // 2 + 1))
                if i >= 2:
                    e.wait_ge(sb.hnd("qmm"), b_mm + 8 * (i - 1))
                h_, _ = sb.bump("pan_cast")
                e.tensor_copy(out=wopb[i % 2][:], in_=wop[i % 2][:]).then_inc(h_, 1)

        def f_pe(e):
            for rg in range(2):
                e.wait_ge(sb.hnd("kv_ld"), b_kvld + 16 * (rg + 1))
                for db in range(16):
                    i = 16 * rg + db
                    e.wait_ge(sb.hnd("pan_cast"), b_cast + i + 1)
                    for rs in range(8):
                        nn = 8 * i + rs
                        bb = nn % 2
                        if nn >= 2:
                            e.wait_ge(sb.hnd("qev"), b_ev + nn - 1)
                        for c in range(32):
                            inst = e.matmul(
                                PS[bb][:, 0:256],
                                otb[:, 1024 * c + 128 * rs:1024 * c + 128 * rs + 128],
                                wopb[i % 2][:, 256 * c:256 * (c + 1)],
                                start=(c == 0), stop=(c == 31))
                        h_, _ = sb.bump("qmm")
                        inst.then_inc(h_, 1)

        def f_vec(e):
            for nn in range(256):
                e.wait_ge(sb.hnd("qmm"), b_mm + nn + 1)
                if nn >= 2:
                    e.wait_ge(sb.hnd(f"qst{nn % 2}"), b_st[nn % 2] + 16 * (nn // 2))
                h_, _ = sb.bump("qev")
                e.tensor_copy(out=aostg[nn % 2][:],
                              in_=PS[nn % 2][:, 0:256]).then_inc(h_, 1)

        def f_store(e):
            for rg in range(2):
                for db in range(16):
                    for rs in range(8):
                        nn = 128 * rg + 8 * db + rs
                        e.wait_ge(sb.hnd("qev"), b_ev + nn + 1)
                        h_, _ = sb.bump(f"qst{nn % 2}", 16)
                        r0 = 1024 * rg + 128 * rs
                        e.dma_start(out=ao_sc[r0:r0 + 128, 256 * db:256 * (db + 1)],
                                    in_=aostg[nn % 2][:]).then_inc(h_, 16)
            e.wait_ge(sb.hnd("qst0"), sb.val("qst0"))
            e.wait_ge(sb.hnd("qst1"), sb.val("qst1"))

        blk.sync(f_sync)
        blk.gpsimd(f_pool)
        blk.tensor(f_pe)
        blk.vector(f_vec)
        blk.scalar(f_store)
        phase_join(["tensor", "vector", "gpsimd", "scalar"])
        g3.close()

    ph_g3()
    if NPH <= 4:
        es.close()
        return nc

    def ph_p3b():
        p4 = ExitStack()
        aor = [p4.enter_context(nc.sbuf_tensor(f"aor{i}", [128, DM], F32)) for i in range(2)]
        xor_ = [p4.enter_context(nc.sbuf_tensor(f"xor{i}", [128, DM], F32)) for i in range(2)]
        t1b = [p4.enter_context(nc.sbuf_tensor(f"t1b{i}", [128, DM], F32)) for i in range(2)]
        x1b = [p4.enter_context(nc.sbuf_tensor(f"x1b{i}", [128, DM], F32)) for i in range(2)]
        ln2b = [p4.enter_context(nc.sbuf_tensor(f"l2b{i}", [128, DM], BF16)) for i in range(2)]
        sqd = p4.enter_context(nc.sbuf_tensor("p4sqd", [128, DM], BF16))
        ln2stg = [p4.enter_context(nc.sbuf_tensor(f"ln2stg{i}", [128, 32 * 128], BF16)) for i in range(2)]

        NJ = 16
        b_pj = sb.val("pj")
        b = {k: sb.val(k) for k in ("qtl0", "qtl1", "xld0", "xld1", "sq", "var",
                                    "rstd", "negmr", "apply", "sq2", "var2",
                                    "rstd2", "negmr2", "ap2", "x1d", "tr",
                                    "trev", "ost0", "ost1", "mld0", "mld1",
                                    "qst0", "qst1")}

        def emit_stores_j(e, j):
            p = j % 2
            e.wait_ge(sb.hnd("x1d"), b["x1d"] + j + 1)
            h_, _ = sb.bump(f"ost{p}", 16)
            e.dma_start(out=x1_sc[128 * j:128 * (j + 1), :],
                        in_=x1b[p][:]).then_inc(h_, 16)
            e.wait_ge(sb.hnd("trev"), b["trev"] + 4 * (j + 1))
            h_, _ = sb.bump(f"mld{p}", 16)
            l2v = ln2_sc.rearrange("(c p) n -> p c n", p=128)
            srcv = ln2stg[p][:, :].rearrange("p (c r) -> p c r", r=128)
            e.dma_start(out=l2v[:, :, 128 * j:128 * j + 128],
                        in_=srcv).then_inc(h_, 16)

        def f_sync(e):
            e.wait_ge(sb.hnd("pj"), b_pj)
            e.wait_ge(sb.hnd("qst0"), b["qst0"])
            e.wait_ge(sb.hnd("qst1"), b["qst1"])
            for j in range(NJ):
                e.wait_ge(sb.hnd("apply"), b["apply"] + max(j - 1, 0))
                h_, _ = sb.bump(f"qtl{j % 2}", 16)
                e.dma_start(out=aor[j % 2][:],
                            in_=ao_sc[128 * j:128 * (j + 1), :]).then_inc(h_, 16)
                e.wait_ge(sb.hnd("x1d"), b["x1d"] + max(j - 1, 0))
                h_, _ = sb.bump(f"xld{j % 2}", 16)
                e.dma_start(out=xor_[j % 2][:],
                            in_=x_own[128 * j:128 * (j + 1), :]).then_inc(h_, 16)
                if j >= 1:
                    emit_stores_j(e, j - 1)
            emit_stores_j(e, NJ - 1)
            e.wait_ge(sb.hnd("ost0"), sb.val("ost0"))
            e.wait_ge(sb.hnd("ost1"), sb.val("ost1"))
            e.wait_ge(sb.hnd("mld0"), sb.val("mld0"))
            e.wait_ge(sb.hnd("mld1"), sb.val("mld1"))

        def f_vec(e):
            for j in range(NJ):
                p = j % 2
                e.drain()
                e.wait_ge(sb.hnd(f"qtl{p}"), b[f"qtl{p}"] + 16 * (j // 2 + 1))
                e.tensor_reduce(out=stt[:, 8 * p:8 * p + 1], in_=aor[p][:],
                                axis=AX.X, op=ALU.add)
                e.drain()
                e.wait_ge(sb.hnd("sq"), b["sq"] + j + 1)
                if j >= 1:
                    e.wait_ge(sb.hnd("rstd"), b["rstd"] + j)
                e.tensor_scalar(out=stt[:, 2:3], in0=stt[:, 8 * p:8 * p + 1],
                                scalar1=stt[:, 8 * p:8 * p + 1], scalar2=float(1.0 / DM),
                                op0=ALU.mult, op1=ALU.mult)
                e.drain()
                h_, _ = sb.bump("var")
                e.tensor_tensor(out=stt[:, 2:3], in0=stt[:, 8 * p + 1:8 * p + 2],
                                in1=stt[:, 2:3], op=ALU.subtract).then_inc(h_, 1)
                e.wait_ge(sb.hnd("rstd"), b["rstd"] + j + 1)
                e.drain()
                h_, _ = sb.bump("negmr")
                e.tensor_scalar(out=stt[:, 4:5], in0=stt[:, 8 * p:8 * p + 1],
                                scalar1=stt[:, 3:4], scalar2=float(-1.0 / DM),
                                op0=ALU.mult, op1=ALU.mult).then_inc(h_, 1)
                e.wait_ge(sb.hnd("apply"), b["apply"] + j + 1)
                e.wait_ge(sb.hnd(f"xld{p}"), b[f"xld{p}"] + 16 * (j // 2 + 1))
                if j >= 2:
                    e.wait_ge(sb.hnd("ap2"), b["ap2"] + j - 1)
                    e.wait_ge(sb.hnd(f"ost{p}"), b[f"ost{p}"] + 16 * (j // 2))
                e.drain()
                h_, _ = sb.bump("x1d")
                e.tensor_tensor(out=x1b[p][:], in0=t1b[p][:], in1=xor_[p][:],
                                op=ALU.add).then_inc(h_, 1)
                e.drain()
                e.tensor_reduce(out=stt[:, 8 * p + 5:8 * p + 6], in_=x1b[p][:],
                                axis=AX.X, op=ALU.add)
                e.drain()
                e.wait_ge(sb.hnd("sq2"), b["sq2"] + j + 1)
                if j >= 1:
                    e.wait_ge(sb.hnd("rstd2"), b["rstd2"] + j)
                e.tensor_scalar(out=stt[:, 7:8], in0=stt[:, 8 * p + 5:8 * p + 6],
                                scalar1=stt[:, 8 * p + 5:8 * p + 6], scalar2=float(1.0 / DM),
                                op0=ALU.mult, op1=ALU.mult)
                e.drain()
                h_, _ = sb.bump("var2")
                e.tensor_tensor(out=stt[:, 7:8], in0=stt[:, 8 * p + 6:8 * p + 7],
                                in1=stt[:, 7:8], op=ALU.subtract).then_inc(h_, 1)
                e.wait_ge(sb.hnd("rstd2"), b["rstd2"] + j + 1)
                e.drain()
                h_, _ = sb.bump("negmr2")
                e.tensor_scalar(out=stt[:, 11:12], in0=stt[:, 8 * p + 5:8 * p + 6],
                                scalar1=stt[:, 10:11], scalar2=float(-1.0 / DM),
                                op0=ALU.mult, op1=ALU.mult).then_inc(h_, 1)
                for fb in range(4):
                    e.wait_ge(sb.hnd("tr"), b["tr"] + 4 * j + fb + 1)
                    if fb == 0 and j >= 2:
                        e.wait_ge(sb.hnd(f"mld{p}"), b[f"mld{p}"] + 16 * (j // 2))
                    bb = fb % 2
                    src = PS[bb][:].bitcast(BF16)[:, 0:1024]
                    dst = ln2stg[p][:, :].rearrange("p (c r) -> p c r", r=128)
                    h_, _ = sb.bump("trev")
                    e.tensor_copy(out=dst[:, 8 * fb:8 * fb + 8, :],
                                  in_=src.rearrange("p (c r) -> p c r", r=128)).then_inc(h_, 1)

        def f_act(e):
            for j in range(NJ):
                p = j % 2
                e.drain()
                e.wait_ge(sb.hnd(f"qtl{p}"), b[f"qtl{p}"] + 16 * (j // 2 + 1))
                if j >= 2:
                    e.wait_ge(sb.hnd("var"), b["var"] + j - 1)
                h_, _ = sb.bump("sq")
                e.activation(out=sqd[:], in_=aor[p][:], func=AF.Square,
                             accum_out=stt[:, 8 * p + 1:8 * p + 2]).then_inc(h_, 1)
                e.wait_ge(sb.hnd("var"), b["var"] + j + 1)
                e.activation(out=stt[:, 2:3], in_=stt[:, 2:3], func=AF.Ln,
                             bias=cst[:, 0:1], scale=1.0)
                e.drain()
                h_, _ = sb.bump("rstd")
                e.activation(out=stt[:, 3:4], in_=stt[:, 2:3], func=AF.Exp,
                             bias=cst[:, 1:2], scale=cst[:, 3:4]).then_inc(h_, 1)
                e.wait_ge(sb.hnd("negmr"), b["negmr"] + j + 1)
                if j >= 2:
                    e.wait_ge(sb.hnd("x1d"), b["x1d"] + j - 1)
                e.drain()
                h_, _ = sb.bump("apply")
                e.activation(out=t1b[p][:], in_=aor[p][:], func=AF.Identity,
                             bias=stt[:, 4:5], scale=stt[:, 3:4]).then_inc(h_, 1)
                e.drain()
                e.wait_ge(sb.hnd("x1d"), b["x1d"] + j + 1)
                if j >= 2:
                    e.wait_ge(sb.hnd("var2"), b["var2"] + j - 1)
                h_, _ = sb.bump("sq2")
                e.activation(out=sqd[:], in_=x1b[p][:], func=AF.Square,
                             accum_out=stt[:, 8 * p + 6:8 * p + 7]).then_inc(h_, 1)
                e.wait_ge(sb.hnd("var2"), b["var2"] + j + 1)
                e.activation(out=stt[:, 7:8], in_=stt[:, 7:8], func=AF.Ln,
                             bias=cst[:, 0:1], scale=1.0)
                e.drain()
                h_, _ = sb.bump("rstd2")
                e.activation(out=stt[:, 10:11], in_=stt[:, 7:8], func=AF.Exp,
                             bias=cst[:, 1:2], scale=cst[:, 3:4]).then_inc(h_, 1)
                e.wait_ge(sb.hnd("negmr2"), b["negmr2"] + j + 1)
                e.drain()
                if j >= 2:
                    e.wait_ge(sb.hnd("tr"), b["tr"] + 4 * (j - 1))
                h_, _ = sb.bump("ap2")
                e.activation(out=ln2b[p][:], in_=x1b[p][:], func=AF.Identity,
                             bias=stt[:, 11:12], scale=stt[:, 10:11]).then_inc(h_, 1)

        def f_pe(e):
            for j in range(NJ):
                p = j % 2
                e.wait_ge(sb.hnd("ap2"), b["ap2"] + j + 1)
                for fb in range(4):
                    bb = fb % 2
                    nfill = 4 * j + fb
                    if nfill >= 2:
                        e.wait_ge(sb.hnd("trev"), b["trev"] + nfill - 1)
                    for t8 in range(8):
                        c = 8 * fb + t8
                        inst = e.transpose(
                            PS[bb][:].bitcast(BF16)[:, 128 * t8:128 * (t8 + 1)],
                            ln2b[p][:, 128 * c:128 * (c + 1)], ident[:])
                    h_, _ = sb.bump("tr")
                    inst.then_inc(h_, 1)

        blk.sync(f_sync)
        blk.vector(f_vec)
        blk.scalar(f_act)
        blk.tensor(f_pe)
        phase_join(["tensor", "vector", "scalar"])
        p4.close()

    ph_p3b()
    if NPH <= 5:
        es.close()
        return nc


    g45 = ExitStack()
    dA_sb = g45.enter_context(nc.sbuf_tensor("dA_sb", [16, 2048], BF16))

    def ph_g4():
        g4 = ExitStack()
        ln2TF = g4.enter_context(nc.sbuf_tensor("ln2TF", [128, 32 * NROW], BF16))
        gp = [g4.enter_context(nc.sbuf_tensor(f"gp{i}", [128, 32 * 64], F32)) for i in range(2)]
        gpb = [g4.enter_context(nc.sbuf_tensor(f"gpb{i}", [128, 32 * 128], BF16)) for i in range(2)]
        gsil = [g4.enter_context(nc.sbuf_tensor(f"gsil{i}", [128, 2048], BF16)) for i in range(2)]
        h_sb = [g4.enter_context(nc.sbuf_tensor(f"hsb{i}", [128, 2048], BF16)) for i in range(2)]
        gATb = g4.enter_context(nc.sbuf_tensor("gATb", [128, 32 * 16], BF16))
        uATb = g4.enter_context(nc.sbuf_tensor("uATb", [128, 32 * 16], BF16))
        dATb = g4.enter_context(nc.sbuf_tensor("dATb", [128, NHC * 16], BF16))
        gA_sb = g4.enter_context(nc.sbuf_tensor("gA_sb", [16, 2048], BF16))
        uA_sb = g4.enter_context(nc.sbuf_tensor("uA_sb", [16, 2048], BF16))
        smw = g4.enter_context(nc.sbuf_tensor("smw", [128, NHC * 16], F32))
        gBst = [g4.enter_context(nc.sbuf_tensor(f"gBst{i}", [16, 128], F32)) for i in range(2)]
        uBst = [g4.enter_context(nc.sbuf_tensor(f"uBst{i}", [16, 128], F32)) for i in range(2)]
        gBb = [g4.enter_context(nc.sbuf_tensor(f"gBb{i}", [16, 128], BF16)) for i in range(2)]
        uBb = [g4.enter_context(nc.sbuf_tensor(f"uBb{i}", [16, 128], BF16)) for i in range(2)]

        b_pj = sb.val("pj")
        b = {k: sb.val(k) for k in ("wkkup", "wk_cast", "kv_ld", "ymm", "yev",
                                    "pnl0", "pnl1", "pan_cast", "qmm", "qev",
                                    "qst0", "qst1", "pvd", "gsl", "hm",
                                    "mld0", "mld1", "wkld0", "wkld1",
                                    "wkld2", "wkld3")}

        def f_sync(e):
            e.wait_ge(sb.hnd("pj"), b_pj)
            e.wait_ge(sb.hnd("mld0"), b["mld0"])
            e.wait_ge(sb.hnd("mld1"), b["mld1"])
            h_, _ = sb.bump("kv_ld", 16)
            e.dma_start(out=ln2TF[:].rearrange("p (c n) -> p c n", c=32),
                        in_=ln2_sc[:, :].rearrange("(c p) n -> p c n", p=128)).then_inc(h_, 16)
            for i, (ncol, srcw) in enumerate([
                    (512, gateAT), (512, upAT), (NHC * 16, downAT)]):
                e.wait_ge(sb.hnd("wk_cast"), b["wk_cast"] + i)
                h_, _ = sb.bump("wkkup", 16)
                e.dma_start(out=smw[:, 0:ncol].rearrange("p (c n) -> p c n", n=16),
                            in_=srcw.rearrange("(c p) n -> p c n", p=128)).then_inc(h_, 16)
            for c in range(NHC):
                for w in range(2):
                    i = 2 * c + w
                    wt = gateT if w == 0 else upT
                    for hf in range(2):
                        hi = 2 * i + hf
                        if hi >= 2:
                            e.wait_ge(sb.hnd("pan_cast"), b["pan_cast"] + hi - 1)
                        h_, _ = sb.bump(f"pnl{hi % 2}", 16)
                        e.dma_start(out=gp[hi % 2][:].rearrange("p (c n) -> p c n", c=32),
                                    in_=wt[:, 128 * c + 64 * hf:128 * c + 64 * (hf + 1)]
                                    .rearrange("(c p) n -> p c n", p=128)).then_inc(h_, 16)
                if c >= 2:
                    e.wait_ge(sb.hnd("wk_cast"), b["wk_cast"] + 3 + 2 * (c - 1))
                h_, _ = sb.bump(f"wkld{c % 2}", 16)
                e.dma_start(out=gBst[c % 2][:],
                            in_=gateBT[:, 128 * c:128 * (c + 1)]).then_inc(h_, 16)
                h_, _ = sb.bump(f"wkld{2 + c % 2}", 16)
                e.dma_start(out=uBst[c % 2][:],
                            in_=upBT[:, 128 * c:128 * (c + 1)]).then_inc(h_, 16)
                if c >= 1:
                    cc = c - 1
                    e.wait_ge(sb.hnd("hm"), b["hm"] + 4 * (cc + 1))
                    h_, _ = sb.bump(f"qst{cc % 2}", 16)
                    hv = h_sc.rearrange("(a b p) n -> p a b n", a=16, b=NHC)
                    e.dma_start(out=hv[:, :, cc, :],
                                in_=h_sb[cc % 2][:].rearrange("p (a r) -> p a r", a=16)).then_inc(h_, 16)
            cc = NHC - 1
            e.wait_ge(sb.hnd("hm"), b["hm"] + 4 * (cc + 1))
            h_, _ = sb.bump(f"qst{cc % 2}", 16)
            hv = h_sc.rearrange("(a b p) n -> p a b n", a=16, b=NHC)
            e.dma_start(out=hv[:, :, cc, :],
                        in_=h_sb[cc % 2][:].rearrange("p (a r) -> p a r", a=16)).then_inc(h_, 16)
            e.wait_ge(sb.hnd("qst0"), sb.val("qst0"))
            e.wait_ge(sb.hnd("qst1"), sb.val("qst1"))

        def f_pool(e):
            for i, (dsb, ncol) in enumerate([(gATb, 512), (uATb, 512),
                                             (dATb, NHC * 16)]):
                e.wait_ge(sb.hnd("wkkup"), b["wkkup"] + 16 * (i + 1))
                h_, _ = sb.bump("wk_cast")
                e.tensor_copy(out=dsb[:, 0:ncol], in_=smw[:, 0:ncol]).then_inc(h_, 1)
            for c in range(NHC):
                for w in range(2):
                    i = 2 * c + w
                    for hf in range(2):
                        hi = 2 * i + hf
                        e.wait_ge(sb.hnd(f"pnl{hi % 2}"), b[f"pnl{hi % 2}"] + 16 * (hi // 2 + 1))
                        if i >= 2:
                            e.wait_ge(sb.hnd("qmm"), b["qmm"] + 4 * (i - 1))
                        h_, _ = sb.bump("pan_cast")
                        dstv = gpb[i % 2][:, :].rearrange("p (c h n) -> p c h n",
                                                          h=2, n=64)
                        e.tensor_copy(out=dstv[:, :, hf, :],
                                      in_=gp[hi % 2][:].rearrange("p (c n) -> p c n",
                                                                  n=64)).then_inc(h_, 1)
                e.wait_ge(sb.hnd(f"wkld{c % 2}"), b[f"wkld{c % 2}"] + 16 * (c // 2 + 1))
                e.wait_ge(sb.hnd(f"wkld{2 + c % 2}"), b[f"wkld{2 + c % 2}"] + 16 * (c // 2 + 1))
                if c >= 2:
                    e.wait_ge(sb.hnd("qmm"), b["qmm"] + 8 * (c - 1))
                h_, _ = sb.bump("wk_cast")
                e.tensor_copy(out=gBb[c % 2][:], in_=gBst[c % 2][:]).then_inc(h_, 1)
                h_, _ = sb.bump("wk_cast")
                e.tensor_copy(out=uBb[c % 2][:], in_=uBst[c % 2][:]).then_inc(h_, 1)

        def f_pe(e):
            e.wait_ge(sb.hnd("kv_ld"), b["kv_ld"] + 16)
            e.wait_ge(sb.hnd("wk_cast"), b["wk_cast"] + 3)
            for wi, ATb in enumerate([gATb, uATb]):
                for rb in range(4):
                    nn = 4 * wi + rb
                    bb = 4 + nn % 2
                    if nn >= 2:
                        e.wait_ge(sb.hnd("yev"), b["yev"] + nn - 1)
                    for c in range(32):
                        inst = e.matmul(PS[bb][0:16, :],
                                        ATb[:, 16 * c:16 * (c + 1)],
                                        ln2TF[:, NROW * c + 512 * rb:NROW * c + 512 * (rb + 1)],
                                        start=(c == 0), stop=(c == 31))
                    h_, _ = sb.bump("ymm")
                    inst.then_inc(h_, 1)
            pendDA = None
            for c in range(NHC):
                for w in range(2):
                    i = 2 * c + w
                    e.wait_ge(sb.hnd("pan_cast"), b["pan_cast"] + 2 * (i + 1))
                    e.wait_ge(sb.hnd("wk_cast"), b["wk_cast"] + 3 + 2 * (c + 1))
                    Bt = gBb[c % 2] if w == 0 else uBb[c % 2]
                    Asb = gA_sb if w == 0 else uA_sb
                    for rb in range(4):
                        bb = (2 * w) + rb % 2
                        nn = 8 * c + 4 * w + rb
                        if c >= 1 or rb >= 2:
                            if w == 0:
                                e.wait_ge(sb.hnd("gsl"), b["gsl"] + 4 * c + rb - 1)
                            else:
                                e.wait_ge(sb.hnd("hm"), b["hm"] + 4 * c + rb - 1)
                        if w == 0 and c == 0 and rb < 2:
                            e.wait_ge(sb.hnd("yev"), b["yev"] + 8)
                        for cc in range(32):
                            inst = e.matmul(
                                PS[bb][:],
                                gpb[i % 2][:, 128 * cc:128 * (cc + 1)],
                                ln2TF[:, NROW * cc + 512 * rb:NROW * cc + 512 * (rb + 1)],
                                start=(cc == 0), stop=False)
                        e.wait_ge(sb.hnd("yev"), b["yev"] + (1 if w == 0 else 2) * 4)
                        inst = e.matmul(PS[bb][:], Bt[:, :],
                                        Asb[:, 512 * rb:512 * (rb + 1)],
                                        start=False, stop=True)
                        h_, _ = sb.bump("qmm")
                        inst.then_inc(h_, 1)
                    if w == 0 and pendDA is not None:
                        cd = pendDA
                        for rb in range(4):
                            e.wait_ge(sb.hnd("hm"), b["hm"] + 4 * cd + rb + 1)
                            inst = e.matmul(PS[4 + rb][0:16, :],
                                            dATb[:, 16 * cd:16 * (cd + 1)],
                                            h_sb[cd % 2][:, 512 * rb:512 * (rb + 1)],
                                            start=(cd == 0), stop=(cd == NHC - 1))
                            h_, _ = sb.bump("pvd")
                            inst.then_inc(h_, 1)
                pendDA = c
            cd = pendDA
            for rb in range(4):
                e.wait_ge(sb.hnd("hm"), b["hm"] + 4 * cd + rb + 1)
                inst = e.matmul(PS[4 + rb][0:16, :],
                                dATb[:, 16 * cd:16 * (cd + 1)],
                                h_sb[cd % 2][:, 512 * rb:512 * (rb + 1)],
                                start=(cd == 0), stop=(cd == NHC - 1))
                h_, _ = sb.bump("pvd")
                inst.then_inc(h_, 1)

        def f_act(e):
            for c in range(NHC):
                for rb in range(4):
                    nn = 8 * c + rb
                    e.wait_ge(sb.hnd("qmm"), b["qmm"] + 8 * c + rb + 1)
                    if c >= 2:
                        e.wait_ge(sb.hnd("hm"), b["hm"] + 4 * (c - 2) + rb + 1)
                    h_, _ = sb.bump("gsl")
                    e.activation(out=gsil[c % 2][:, 512 * rb:512 * (rb + 1)],
                                 in_=PS[rb % 2][:], func=AF.Silu).then_inc(h_, 1)

        def f_vec(e):
            for wi, Asb in enumerate([gA_sb, uA_sb]):
                for rb in range(4):
                    nn = 4 * wi + rb
                    bb = 4 + nn % 2
                    e.wait_ge(sb.hnd("ymm"), b["ymm"] + nn + 1)
                    h_, _ = sb.bump("yev")
                    e.tensor_copy(out=Asb[:, 512 * rb:512 * (rb + 1)],
                                  in_=PS[bb][0:16, :]).then_inc(h_, 1)
            for c in range(NHC):
                for rb in range(4):
                    e.wait_ge(sb.hnd("qmm"), b["qmm"] + 8 * c + 4 + rb + 1)
                    e.wait_ge(sb.hnd("gsl"), b["gsl"] + 4 * c + rb + 1)
                    if c >= 2:
                        e.wait_ge(sb.hnd(f"qst{c % 2}"), b[f"qst{c % 2}"] + 16 * (c // 2))
                        e.wait_ge(sb.hnd("pvd"), b["pvd"] + 4 * (c - 1))
                    h_, _ = sb.bump("hm")
                    e.tensor_tensor(out=h_sb[c % 2][:, 512 * rb:512 * (rb + 1)],
                                    in0=PS[2 + rb % 2][:],
                                    in1=gsil[c % 2][:, 512 * rb:512 * (rb + 1)],
                                    op=ALU.mult).then_inc(h_, 1)
            e.wait_ge(sb.hnd("pvd"), b["pvd"] + 4 * NHC)
            for rb in range(4):
                h_, _ = sb.bump("yev")
                e.tensor_copy(out=dA_sb[:, 512 * rb:512 * (rb + 1)],
                              in_=PS[4 + rb][0:16, :]).then_inc(h_, 1)

        blk.sync(f_sync)
        blk.gpsimd(f_pool)
        blk.tensor(f_pe)
        blk.scalar(f_act)
        blk.vector(f_vec)
        phase_join(["tensor", "vector", "scalar", "gpsimd"])
        g4.close()

    ph_g4()
    if NPH <= 6:
        g45.close()
        es.close()
        return nc

    def ph_g5():
        g5 = ExitStack()
        panelb = [g5.enter_context(nc.sbuf_tensor(f"panelb{i}", [128, NHC * 256], BF16)) for i in range(2)]
        pn4 = [g5.enter_context(nc.sbuf_tensor(f"pn4{i}", [128, 256], F32)) for i in range(4)]
        ht = [g5.enter_context(nc.sbuf_tensor(f"ht{i}", [128, NHC * 128], BF16)) for i in range(2)]
        dstg = [g5.enter_context(nc.sbuf_tensor(f"dstg{i}", [128, 256], F32)) for i in range(2)]
        dBTb = g5.enter_context(nc.sbuf_tensor("dBTb", [16, DM], BF16))
        smw2 = g5.enter_context(nc.sbuf_tensor("g5smw", [16, DM], F32))

        b_pj = sb.val("pj")
        b = {k: sb.val(k) for k in ("wkkup", "wk_cast", "pan_cast", "qmm", "qev",
                                    "qst0", "qst1", "qtl0", "qtl1",
                                    "wkld0", "wkld1", "wkld2", "wkld3", "dbc")}
        ND = 16

        def f_sync(e):
            e.wait_ge(sb.hnd("pj"), b_pj)
            h_, _ = sb.bump("wkkup", 16)
            e.dma_start(out=smw2[:, :], in_=downBT_[:, :]).then_inc(h_, 16)
            for d in range(ND):
                for c in range(NHC):
                    i = NHC * d + c
                    if i >= 4:
                        e.wait_ge(sb.hnd("pan_cast"), b["pan_cast"] + i - 3)
                    h_, _ = sb.bump(f"wkld{i % 4}", 16)
                    e.dma_start(out=pn4[i % 4][:],
                                in_=downT[128 * c:128 * (c + 1), 256 * d:256 * (d + 1)]).then_inc(h_, 16)
                for rs in range(16):
                    n = 16 * d + rs
                    if n >= 2:
                        e.wait_ge(sb.hnd("qmm"), b["qmm"] + n - 1)
                    h_, _ = sb.bump(f"qtl{n % 2}", 16)
                    e.dma_start(out=ht[n % 2][:].rearrange("p (c n) -> p c n", c=NHC),
                                in_=h_sc[NHC * 128 * rs:NHC * 128 * (rs + 1), :]
                                .rearrange("(c p) n -> p c n", p=128)).then_inc(h_, 16)

        def f_pool(e):
            e.wait_ge(sb.hnd("wkkup"), b["wkkup"] + 16)
            h_, _ = sb.bump("wk_cast")
            e.tensor_copy(out=dBTb[:, :], in_=smw2[:, :]).then_inc(h_, 1)
            for d in range(ND):
                for c in range(NHC):
                    i = NHC * d + c
                    e.wait_ge(sb.hnd(f"wkld{i % 4}"), b[f"wkld{i % 4}"] + 16 * (i // 4 + 1))
                    if d >= 2:
                        e.wait_ge(sb.hnd("qmm"), b["qmm"] + 16 * (d - 1))
                    h_, _ = sb.bump("pan_cast")
                    e.tensor_copy(out=panelb[d % 2][:, 256 * c:256 * (c + 1)],
                                  in_=pn4[i % 4][:]).then_inc(h_, 1)
                h_, _ = sb.bump("dbc")
                e.drain().then_inc(h_, 1)

        def f_pe(e):
            e.wait_ge(sb.hnd("wk_cast"), b["wk_cast"] + 1)
            for d in range(ND):
                e.wait_ge(sb.hnd("dbc"), b["dbc"] + d + 1)
                for rs in range(16):
                    n = 16 * d + rs
                    bb = n % 2
                    e.wait_ge(sb.hnd(f"qtl{n % 2}"), b[f"qtl{n % 2}"] + 16 * (n // 2 + 1))
                    if n >= 2:
                        e.wait_ge(sb.hnd("qev"), b["qev"] + n - 1)
                    for c in range(NHC):
                        inst = e.matmul(PS[bb][:, 0:256],
                                        ht[n % 2][:, 128 * c:128 * (c + 1)],
                                        panelb[d % 2][:, 256 * c:256 * (c + 1)],
                                        start=(c == 0), stop=False)
                    inst = e.matmul(PS[bb][:, 0:256], dA_sb[:, 128 * rs:128 * (rs + 1)],
                                    dBTb[:, 256 * d:256 * (d + 1)],
                                    start=False, stop=True)
                    h_, _ = sb.bump("qmm")
                    inst.then_inc(h_, 1)

        def f_vec(e):
            for d in range(ND):
                for rs in range(16):
                    n = 16 * d + rs
                    e.wait_ge(sb.hnd("qmm"), b["qmm"] + n + 1)
                    if n >= 2:
                        e.wait_ge(sb.hnd(f"qst{n % 2}"), b[f"qst{n % 2}"] + 16 * (n // 2))
                    h_, _ = sb.bump("qev")
                    e.tensor_copy(out=dstg[n % 2][:], in_=PS[n % 2][:, 0:256]).then_inc(h_, 1)

        def f_store(e):
            for d in range(ND):
                for rs in range(16):
                    n = 16 * d + rs
                    e.wait_ge(sb.hnd("qev"), b["qev"] + n + 1)
                    h_, _ = sb.bump(f"qst{n % 2}", 16)
                    e.dma_start(out=d_sc[128 * rs:128 * (rs + 1), 256 * d:256 * (d + 1)],
                                in_=dstg[n % 2][:]).then_inc(h_, 16)
            e.wait_ge(sb.hnd("qst0"), sb.val("qst0"))
            e.wait_ge(sb.hnd("qst1"), sb.val("qst1"))

        blk.sync(f_sync)
        blk.gpsimd(f_pool)
        blk.tensor(f_pe)
        blk.vector(f_vec)
        blk.scalar(f_store)
        phase_join(["tensor", "vector", "gpsimd", "scalar"])
        g5.close()

    ph_g5()
    g45.close()
    if NPH <= 7:
        es.close()
        return nc

    def ph_p7():
        p7 = ExitStack()
        dr = [p7.enter_context(nc.sbuf_tensor(f"p7d{i}", [128, DM], F32)) for i in range(2)]
        x1r = [p7.enter_context(nc.sbuf_tensor(f"p7x{i}", [128, DM], F32)) for i in range(2)]
        t2b = [p7.enter_context(nc.sbuf_tensor(f"p7t{i}", [128, DM], F32)) for i in range(2)]
        outb = [p7.enter_context(nc.sbuf_tensor(f"p7o{i}", [128, DM], F32)) for i in range(2)]
        sqd = p7.enter_context(nc.sbuf_tensor("p7sqd", [128, DM], BF16))

        NJ = 16
        b_pj = sb.val("pj")
        b = {k: sb.val(k) for k in ("xld0", "xld1", "qtl0", "qtl1", "sq", "var",
                                    "rstd", "negmr", "apply", "x1d",
                                    "ost0", "ost1", "qst0", "qst1")}

        def emit_out_j(e, j):
            p = j % 2
            e.wait_ge(sb.hnd("x1d"), b["x1d"] + j + 1)
            h_, _ = sb.bump(f"ost{p}", 16)
            e.dma_start(out=out[128 * j:128 * (j + 1), :],
                        in_=outb[p][:]).then_inc(h_, 16)

        def f_sync(e):
            e.wait_ge(sb.hnd("pj"), b_pj)
            e.wait_ge(sb.hnd("qst0"), b["qst0"])
            e.wait_ge(sb.hnd("qst1"), b["qst1"])
            for j in range(NJ):
                e.wait_ge(sb.hnd("apply"), b["apply"] + max(j - 1, 0))
                h_, _ = sb.bump(f"xld{j % 2}", 16)
                e.dma_start(out=dr[j % 2][:],
                            in_=d_sc[128 * j:128 * (j + 1), :]).then_inc(h_, 16)
                e.wait_ge(sb.hnd("x1d"), b["x1d"] + max(j - 1, 0))
                h_, _ = sb.bump(f"qtl{j % 2}", 16)
                e.dma_start(out=x1r[j % 2][:],
                            in_=x1_sc[128 * j:128 * (j + 1), :]).then_inc(h_, 16)
                if j >= 1:
                    emit_out_j(e, j - 1)
            emit_out_j(e, NJ - 1)

        def f_vec(e):
            base = {"xld": [b["xld0"], b["xld1"]], "sq": b["sq"], "var": b["var"],
                    "rstd": b["rstd"], "negmr": b["negmr"]}
            for j in range(NJ):
                p = j % 2
                stats_vec_j(e, j, p, dr[p][:], base)
                e.wait_ge(sb.hnd("apply"), b["apply"] + j + 1)
                e.wait_ge(sb.hnd(f"qtl{p}"), b[f"qtl{p}"] + 16 * (j // 2 + 1))
                if j >= 2:
                    e.wait_ge(sb.hnd(f"ost{p}"), b[f"ost{p}"] + 16 * (j // 2))
                e.drain()
                h_, _ = sb.bump("x1d")
                e.tensor_tensor(out=outb[p][:], in0=t2b[p][:], in1=x1r[p][:],
                                op=ALU.add).then_inc(h_, 1)

        def f_act(e):
            base = {"xld": [b["xld0"], b["xld1"]], "sq": b["sq"], "var": b["var"],
                    "rstd": b["rstd"], "negmr": b["negmr"]}
            sqd_ref[0] = sqd
            for j in range(NJ):
                p = j % 2
                ew = ("x1d", b["x1d"] + j - 1) if j >= 2 else None
                stats_act_j(e, j, p, dr[p][:], t2b[p][:], base,
                            extra_tr_wait=None, extra_wait=ew)

        blk.sync(f_sync)
        blk.vector(f_vec)
        blk.scalar(f_act)

        def f_final(e):
            e.wait_ge(sb.hnd("ost0"), sb.val("ost0"))
            e.wait_ge(sb.hnd("ost1"), sb.val("ost1"))
        blk.sync(f_final)
        p7.close()

    ph_p7()

    dbg_map = {"ao": ao_sc, "x1": x1_sc, "ln2": ln2_sc, "h": h_sc, "d": d_sc}
    todump = [nm for nm in _DBG.get("dumps", []) if nm in dbg_map]
    if todump:
        def f_dbg(e):
            e.wait_ge(sb.hnd("pj"), sb.val("pj"))
            e.wait_ge(sb.hnd("ost0"), sb.val("ost0"))
            e.wait_ge(sb.hnd("ost1"), sb.val("ost1"))
            for nm in todump:
                h_, _ = sb.bump("dbg", 16)
                sc = dbg_map[nm]
                e.dma_start(out=dbg_outs[nm][:, :].rearrange("(c p) n -> p c n", p=128),
                            in_=sc[:, :].rearrange("(c p) n -> p c n", p=128)).then_inc(h_, 16)
            e.wait_ge(sb.hnd("dbg"), sb.val("dbg"))
        blk.scalar(f_dbg)

    es.close()
    return nc


_PERM = (np.arange(8)[:, None] + 8 * np.arange(512)[None, :]).reshape(-1)


def _build_masks(starts):
    m = np.zeros((NT, 8, 128, TQ), np.float32)
    for t in range(NT):
        n = SWEEPS[t]
        r0 = starts[t]
        for ci in range(8):
            c = n - 8 + ci
            k = 128 * c + np.arange(128)[:, None]
            q = r0 + np.arange(TQ)[None, :]
            m[t, ci] = (k <= q)
    return m.reshape(NT * 8 * 128, TQ).astype(ml_dtypes.bfloat16)


def kernel(**inp):
    x = np.ascontiguousarray(np.asarray(inp["L0_attn_hidden_states"], np.float32))
    B = x.shape[0]

    def T(a):
        return np.ascontiguousarray(np.asarray(a, np.float32).T)

    Wkd = np.asarray(inp["L0_attn_Wk_down"], np.float32)
    Wvd = np.asarray(inp["L0_attn_Wv_down"], np.float32)
    wkv = np.empty((512, DM), np.float32)
    for h in range(8):
        wkv[64 * h:64 * h + 32] = Wkd[32 * h:32 * h + 32]
        wkv[64 * h + 32:64 * h + 64] = Wvd[32 * h:32 * h + 32]

    weights = {
        "WqT": T(inp["L0_attn_Wq"]),
        "WkvT": np.ascontiguousarray(wkv.T),
        "WkupT": T(inp["L0_attn_Wk_up"]),
        "WvupT": T(inp["L0_attn_Wv_up"]),
        "WoT": T(inp["L0_attn_Wo"]),
        "gateT": T(inp["L0_ffn_gate_w"]),
        "upT": T(inp["L0_ffn_up_w"]),
        "downT": T(inp["L0_ffn_down_w"]),
        "gateAT": T(inp["L0_ffn_gate_a"]),
        "gateBT": T(inp["L0_ffn_gate_b"]),
        "upAT": T(inp["L0_ffn_up_a"]),
        "upBT": T(inp["L0_ffn_up_b"]),
        "downAT": T(inp["L0_ffn_down_a"]),
        "downBT": T(inp["L0_ffn_down_b"]),
    }
    mask_p = _build_masks(ROWS_P)
    mask_q = _build_masks(ROWS_Q)

    nc = build_program()
    in_maps = []
    for c in range(8):
        b = c // 2
        starts = ROWS_P if c % 2 == 0 else ROWS_Q
        xo = np.concatenate([x[b, s:s + TQ] for s in starts], axis=0)
        m = {"xb_perm": np.ascontiguousarray(x[b][_PERM]),
             "x_own": np.ascontiguousarray(xo),
             "masks": mask_p if c % 2 == 0 else mask_q}
        m.update(weights)
        in_maps.append(m)

    res = run_bass_kernel_spmd(nc, in_maps, core_ids=list(range(8)))
    outp = np.empty((B, S, DM), np.float32)
    for c in range(8):
        b = c // 2
        starts = ROWS_P if c % 2 == 0 else ROWS_Q
        oc = np.asarray(res.results[c]["out"], np.float32)
        for t, s in enumerate(starts):
            outp[b, s:s + TQ] = oc[TQ * t:TQ * (t + 1)]
    return outp


# version 04
# speedup vs baseline: 1.0072x; 1.0072x over previous
import numpy as np
import ml_dtypes
from contextlib import ExitStack

import concourse.bass as bass
import concourse.mybir as mybir
from concourse.bass_utils import run_bass_kernel_spmd

F32 = mybir.dt.float32
BF16 = mybir.dt.bfloat16
AF = mybir.ActivationFunctionType
ALU = mybir.AluOpType
AX = mybir.AxisListType

S = 4096
DM = 4096
DH = 11008
NHC = DH // 128
RANK = 16
NT = 4
TQ = 512
SWEEPS = [8, 16, 24, 32]
ROWS_P = [0, 1536, 2048, 3584]
ROWS_Q = [512, 1024, 2560, 3072]
NROW = NT * TQ
EPS = 1e-5
EXPB = 18.0
QSCALE = float(1.0 / np.sqrt(128.0))
LN64 = float(np.log(64.0))
EPS_RAW = float(4096.0 * EPS)

_DBG = {}


class SemBank:
    def __init__(self, nc, es):
        self.nc, self.es = nc, es
        self.h = {}
        self.v = {}

    def new(self, name):
        self.h[name] = self.es.enter_context(self.nc.semaphore(name))
        self.v[name] = 0

    def bump(self, name, k=1):
        self.v[name] += k
        return self.h[name], self.v[name]

    def val(self, name):
        return self.v[name]

    def hnd(self, name):
        return self.h[name]


class Banks:
    def __init__(self, sb, n=8):
        self.sb = sb
        self.n = n
        for b in range(n):
            sb.new(f"bk{b}w")
            sb.new(f"bk{b}f")

    def acquire(self, eng, b):
        need = self.sb.val(f"bk{b}f")
        prev_w = self.sb.val(f"bk{b}w")
        if prev_w > 0:
            eng.wait_ge(self.sb.hnd(f"bk{b}f"), prev_w)

    def produced(self, inst, b):
        h, _ = self.sb.bump(f"bk{b}w")
        inst.then_inc(h, 1)
        return self.sb.val(f"bk{b}w")

    def consume_wait(self, eng, b, w=None):
        eng.wait_ge(self.sb.hnd(f"bk{b}w"), w if w is not None else self.sb.val(f"bk{b}w"))

    def freed(self, inst, b):
        h, _ = self.sb.bump(f"bk{b}f")
        inst.then_inc(h, 1)


def build_program():
    nc = bass.Bass("TRN2", target_bir_lowering=False)

    din = {}
    def dram_in(name, shape, dt=F32):
        din[name] = nc.dram_tensor(name, shape, dt, kind="ExternalInput").ap()
        return din[name]

    xb_perm = dram_in("xb_perm", [S, DM])
    x_own = dram_in("x_own", [NROW, DM])
    masks = dram_in("masks", [NT * 8 * 128, TQ], BF16)
    WqT = dram_in("WqT", [DM, DM])
    WkvT = dram_in("WkvT", [DM, 512])
    WkupT = dram_in("WkupT", [256, 1024])
    WvupT = dram_in("WvupT", [256, 1024])
    WoT = dram_in("WoT", [DM, DM])
    gateT = dram_in("gateT", [DM, DH])
    upT = dram_in("upT", [DM, DH])
    downT = dram_in("downT", [DH, DM])
    gateAT = dram_in("gateAT", [DM, RANK])
    gateBT = dram_in("gateBT", [RANK, DH])
    upAT = dram_in("upAT", [DM, RANK])
    upBT = dram_in("upBT", [RANK, DH])
    downAT = dram_in("downAT", [DH, RANK])
    downBT_ = dram_in("downBT", [RANK, DM])

    out = nc.dram_tensor("out", [NROW, DM], F32, kind="ExternalOutput").ap()

    q_sc = nc.dram_tensor("q_sc", [NT * 32 * 128, TQ], BF16, kind="Internal").ap()
    o_sc = nc.dram_tensor("o_sc", [32 * 128, NROW], BF16, kind="Internal").ap()
    kv_sc = nc.dram_tensor("kv_sc", [512, S], BF16, kind="Internal").ap()
    ln2_sc = nc.dram_tensor("ln2_sc", [32 * 128, NROW], BF16, kind="Internal").ap()
    ao_sc = nc.dram_tensor("ao_sc", [NROW, DM], F32, kind="Internal").ap()
    x1_sc = nc.dram_tensor("x1_sc", [NROW, DM], F32, kind="Internal").ap()
    h_sc = nc.dram_tensor("h_sc", [16 * NHC * 128, 128], BF16, kind="Internal").ap()
    d_sc = nc.dram_tensor("d_sc", [NROW, DM], F32, kind="Internal").ap()

    dbg_outs = {}
    for nm in _DBG.get("dumps", []):
        shp = {"kv": [512, S], "lnT": [128, 32 * NROW], "q": [NT * 32 * 128, TQ],
               "o": [32 * 128, NROW], "ao": [NROW, DM], "x1": [NROW, DM],
               "ln2": [32 * 128, NROW], "h": [16 * NHC * 128, 128],
               "d": [NROW, DM]}[nm]
        dt = BF16 if nm in ("kv", "lnT", "q", "o", "ln2", "h") else F32
        dbg_outs[nm] = nc.dram_tensor("dbg_" + nm, shp, dt, kind="ExternalOutput").ap()

    es = ExitStack()
    sb = SemBank(nc, es)
    for name in ["xld0", "xld1", "wkld0", "wkld1", "wkld2", "wkld3", "wkkup",
                 "pnl0", "pnl1", "qst0", "qst1", "qtl0", "qtl1", "mld0", "mld1",
                 "tdone", "mdone", "selc", "lr1", "bcd", "ost0", "ost1", "pvd",
                 "upe", "oscd", "lse", "lrc",
                 "consts", "sq", "var", "rstd", "negmr", "apply",
                 "tr", "trev", "ymm", "yev", "scr", "wk_cast",
                 "kvsp", "pan_cast", "qmm", "qev", "kv_ld", "pexp", "pmask",
                 "pj", "sq2", "var2", "rstd2", "negmr2", "ap2", "x1d",
                 "gsl", "hm", "dbc", "dbg"]:
        sb.new(name)
    bk = Banks(sb, 8)

    def phase_join(engs):
        for en in engs:
            def f(e):
                h_, _ = sb.bump("pj")
                e.drain().then_inc(h_, 1)
            getattr(blk, en)(f)

    PS = [es.enter_context(nc.psum_tensor(f"ps{b}", [128, 512], F32)) for b in range(8)]
    ident = es.enter_context(nc.sbuf_tensor("ident", [128, 128], BF16))
    onesb = es.enter_context(nc.sbuf_tensor("onesb", [128, 128], BF16))
    ones1 = es.enter_context(nc.sbuf_tensor("ones1", [1, 128], F32))
    inds = es.enter_context(nc.sbuf_tensor("inds", [128, 16], BF16))
    cst = es.enter_context(nc.sbuf_tensor("cst", [128, 8], F32))
    stt = es.enter_context(nc.sbuf_tensor("stt", [128, 16], F32))
    kv_es = ExitStack()
    kupb = kv_es.enter_context(nc.sbuf_tensor("kupb", [128, 2 * 1024], BF16))
    vupb = kv_es.enter_context(nc.sbuf_tensor("vupb", [128, 2 * 1024], BF16))

    blk = es.enter_context(nc.Block())

    NPH = _DBG.get("phases", 99)

    def ph_consts():
        def g(e):
            e.memset(onesb[:], 1.0)
            e.memset(inds[:], 0.0)
            e.drain()
            for h in range(4):
                e.memset(inds[:, 5 * h:5 * h + 1], 1.0)
            e.memset(ones1[:], 1.0)
            e.memset(cst[:, 0:1], EPS_RAW)
            e.memset(cst[:, 1:2], LN64)
            e.memset(cst[:, 2:3], -EXPB)
            e.memset(cst[:, 3:4], -0.5)
            e.memset(cst[:, 4:5], -1.0)
            e.drain()
            h_, v = sb.bump("consts")
            e.affine_select(out=ident[:], in_=onesb[:], pattern=[[1, 128]],
                            compare_op=ALU.is_equal, fill=0.0, base=0,
                            channel_multiplier=-1).then_inc(h_, 1)
        blk.gpsimd(g)
    ph_consts()
    CONSTS_DONE = sb.val("consts")


    def stats_vec_j(e, j, p, xrow_ap, base):
        if j >= 1:
            e.drain()
        e.wait_ge(sb.hnd(f"xld{j % 2}"), base["xld"][j % 2] + 16 * (j // 2 + 1))
        e.tensor_reduce(out=stt[:, 8 * p:8 * p + 1], in_=xrow_ap,
                        axis=AX.X, op=ALU.add)
        e.drain()
        e.wait_ge(sb.hnd("sq"), base["sq"] + j + 1)
        if j >= 1:
            e.wait_ge(sb.hnd("rstd"), base["rstd"] + j)
        e.tensor_scalar(out=stt[:, 2:3], in0=stt[:, 8 * p:8 * p + 1],
                        scalar1=stt[:, 8 * p:8 * p + 1], scalar2=float(1.0 / DM),
                        op0=ALU.mult, op1=ALU.mult)
        e.drain()
        h_, _ = sb.bump("var")
        e.tensor_tensor(out=stt[:, 2:3], in0=stt[:, 8 * p + 1:8 * p + 2],
                        in1=stt[:, 2:3], op=ALU.subtract).then_inc(h_, 1)
        e.wait_ge(sb.hnd("rstd"), base["rstd"] + j + 1)
        e.drain()
        h_, _ = sb.bump("negmr")
        e.tensor_scalar(out=stt[:, 4:5], in0=stt[:, 8 * p:8 * p + 1],
                        scalar1=stt[:, 3:4], scalar2=float(-1.0 / DM),
                        op0=ALU.mult, op1=ALU.mult).then_inc(h_, 1)

    def stats_act_j(e, j, p, xrow_ap, out_ap, base, extra_tr_wait=None,
                    extra_wait=None):
        if j >= 1:
            e.drain()
        e.wait_ge(sb.hnd(f"xld{j % 2}"), base["xld"][j % 2] + 16 * (j // 2 + 1))
        if j >= 2:
            e.wait_ge(sb.hnd("var"), base["var"] + j - 1)
        h_, _ = sb.bump("sq")
        e.activation(out=sqd_ref[0][:], in_=xrow_ap, func=AF.Square,
                     accum_out=stt[:, 8 * p + 1:8 * p + 2]).then_inc(h_, 1)
        e.wait_ge(sb.hnd("var"), base["var"] + j + 1)
        e.activation(out=stt[:, 2:3], in_=stt[:, 2:3], func=AF.Ln,
                     bias=cst[:, 0:1], scale=1.0)
        e.drain()
        h_, _ = sb.bump("rstd")
        e.activation(out=stt[:, 3:4], in_=stt[:, 2:3], func=AF.Exp,
                     bias=cst[:, 1:2], scale=cst[:, 3:4]).then_inc(h_, 1)
        e.wait_ge(sb.hnd("negmr"), base["negmr"] + j + 1)
        e.drain()
        if extra_tr_wait is not None:
            e.wait_ge(sb.hnd("tr"), extra_tr_wait)
        if extra_wait is not None:
            e.wait_ge(sb.hnd(extra_wait[0]), extra_wait[1])
        h_, _ = sb.bump("apply")
        e.activation(out=out_ap, in_=xrow_ap, func=AF.Identity,
                     bias=stt[:, 4:5], scale=stt[:, 3:4]).then_inc(h_, 1)

    sqd_ref = [None]

    def ph_p1():
        p1 = ExitStack()
        xrow = [p1.enter_context(nc.sbuf_tensor(f"xrow{i}", [128, DM], F32)) for i in range(2)]
        lnb = [p1.enter_context(nc.sbuf_tensor(f"lnb{i}", [128, DM], BF16)) for i in range(2)]
        sqd = p1.enter_context(nc.sbuf_tensor("sqd", [128, DM], BF16))
        lnT = p1.enter_context(nc.sbuf_tensor("lnT", [128, 32 * 512], BF16))
        ysb = p1.enter_context(nc.sbuf_tensor("ysb", [128, 4 * 512], BF16))
        kcmpT = p1.enter_context(nc.sbuf_tensor("kcmpT", [128, 2 * S], BF16))
        vcmpT = p1.enter_context(nc.sbuf_tensor("vcmpT", [128, 2 * S], BF16))
        wkst = [p1.enter_context(nc.sbuf_tensor(f"wkst{i}", [128, 512], F32)) for i in range(4)]
        wkvb = p1.enter_context(nc.sbuf_tensor("wkvb", [128, 32 * 512], BF16))
        kupst = p1.enter_context(nc.sbuf_tensor("kupst", [128, 4096], F32))

        NJ = 32

        def f_sync(e):
            for i in range(32):
                if i >= 4:
                    e.wait_ge(sb.hnd("wk_cast"), i - 3)
                h_, _ = sb.bump(f"wkld{i % 4}", 16)
                e.dma_start(out=wkst[i % 4][:],
                            in_=WkvT[128 * i:128 * (i + 1), :]).then_inc(h_, 16)
            h_, _ = sb.bump("wkkup", 16)
            e.dma_start(out=kupst[:, 0:2048].rearrange("p (c n) -> p c n", c=2),
                        in_=WkupT.rearrange("(c p) n -> p c n", p=128)).then_inc(h_, 16)
            h_, _ = sb.bump("wkkup", 16)
            e.dma_start(out=kupst[:, 2048:4096].rearrange("p (c n) -> p c n", c=2),
                        in_=WvupT.rearrange("(c p) n -> p c n", p=128)).then_inc(h_, 16)
            for j in range(NJ):
                if j >= 2:
                    e.wait_ge(sb.hnd("apply"), j - 1)
                h_, _ = sb.bump(f"xld{j % 2}", 16)
                e.dma_start(out=xrow[j % 2][:],
                            in_=xb_perm[128 * j:128 * (j + 1), :]).then_inc(h_, 16)

        def f_pool(e):
            for i in range(32):
                e.wait_ge(sb.hnd(f"wkld{i % 4}"), 16 * (i // 4 + 1))
                h_, _ = sb.bump("wk_cast")
                e.tensor_copy(out=wkvb[:, 512 * i:512 * (i + 1)],
                              in_=wkst[i % 4][:]).then_inc(h_, 1)
            e.wait_ge(sb.hnd("wkkup"), 32)
            e.drain()
            h_, _ = sb.bump("wk_cast")
            e.tensor_copy(out=kupb[:], in_=kupst[:, 0:2048]).then_inc(h_, 1)
            h_, _ = sb.bump("wk_cast")
            e.tensor_copy(out=vupb[:], in_=kupst[:, 2048:4096]).then_inc(h_, 1)

        def f_vec(e):
            base = {"xld": [0, 0], "sq": 0, "var": 0, "rstd": 0, "negmr": 0}
            for j in range(NJ):
                p = j % 2
                stats_vec_j(e, j, p, xrow[p][:], base)
                jj = j % 4
                if jj == 0 and j > 0:
                    e.wait_ge(sb.hnd("ymm"), 4 * (j // 4))
                for fb in range(4):
                    e.wait_ge(sb.hnd("tr"), 4 * j + fb + 1)
                    b = fb % 2
                    src = PS[b][:].bitcast(BF16)[:, 0:1024]
                    dst = lnT[:, :].rearrange("p (c r) -> p c r", r=512)
                    h_, _ = sb.bump("trev")
                    e.tensor_copy(out=dst[:, 8 * fb:8 * fb + 8, 128 * jj:128 * jj + 128],
                                  in_=src.rearrange("p (c r) -> p c r", r=128)).then_inc(h_, 1)
                if jj == 3:
                    g = j // 4
                    e.drain()
                    for kv in range(4):
                        e.wait_ge(sb.hnd("ymm"), 4 * g + kv + 1)
                        h_, _ = sb.bump("yev")
                        e.tensor_copy(out=ysb[:, 512 * kv:512 * (kv + 1)],
                                      in_=PS[4 + kv % 2][:]).then_inc(h_, 1)
                    e.drain()
                    dstT = kcmpT if g < 4 else vcmpT
                    rr = g % 4
                    pbase = 64 * (rr % 2)
                    fbase = S * (rr // 2)
                    for h in range(8):
                        sp = 64 * (h % 2)
                        sf = 512 * (h // 2)
                        h_, _ = sb.bump("scr")
                        e.tensor_copy(
                            out=dstT[pbase:pbase + 64, fbase + 512 * h:fbase + 512 * h + 512],
                            in_=ysb[sp:sp + 64, sf:sf + 512]).then_inc(h_, 1)

        def f_act(e):
            base = {"xld": [0, 0], "sq": 0, "var": 0, "rstd": 0, "negmr": 0}
            sqd_ref[0] = sqd
            for j in range(NJ):
                p = j % 2
                stats_act_j(e, j, p, xrow[p][:], lnb[p][:], base,
                            extra_tr_wait=(4 * (j - 1)) if j >= 2 else None)

        def f_pe(e):
            e.wait_ge(sb.hnd("consts"), CONSTS_DONE)
            for j in range(NJ):
                p = j % 2
                e.wait_ge(sb.hnd("apply"), j + 1)
                for fb in range(4):
                    b = fb % 2
                    nfill = 4 * j + fb
                    if nfill >= 2:
                        e.wait_ge(sb.hnd("trev"), nfill - 1)
                    for t8 in range(8):
                        c = 8 * fb + t8
                        inst = e.transpose(
                            PS[b][:].bitcast(BF16)[:, 128 * t8:128 * (t8 + 1)],
                            lnb[p][:, 128 * c:128 * (c + 1)], ident[:])
                    h_, _ = sb.bump("tr")
                    inst.then_inc(h_, 1)
                if j % 4 == 3:
                    g = j // 4
                    if g == 0:
                        e.wait_ge(sb.hnd("wk_cast"), 32)
                    e.wait_ge(sb.hnd("trev"), 16 * (g + 1))
                    for kv in range(4):
                        bb = 4 + kv % 2
                        nuse = 4 * g + kv
                        if nuse >= 2:
                            e.wait_ge(sb.hnd("yev"), nuse - 1)
                        for c in range(32):
                            inst = e.matmul(PS[bb][:],
                                            wkvb[:, 512 * c + 128 * kv:512 * c + 128 * (kv + 1)],
                                            lnT[:, 512 * c:512 * (c + 1)],
                                            start=(c == 0), stop=(c == 31))
                        h_, _ = sb.bump("ymm")
                        inst.then_inc(h_, 1)

        def f_store(e):
            e.wait_ge(sb.hnd("scr"), 64)
            h_, _ = sb.bump("kvsp", 16)
            e.dma_start(out=kv_sc[0:256, :].rearrange("(c p) n -> p c n", p=128),
                        in_=kcmpT[:].rearrange("p (c n) -> p c n", c=2)).then_inc(h_, 16)
            h_, _ = sb.bump("kvsp", 16)
            e.dma_start(out=kv_sc[256:512, :].rearrange("(c p) n -> p c n", p=128),
                        in_=vcmpT[:].rearrange("p (c n) -> p c n", c=2)).then_inc(h_, 16)
            if "kv" in dbg_outs:
                h_, _ = sb.bump("dbg", 32)
                e.dma_start(out=dbg_outs["kv"][0:256, :].rearrange("(c p) n -> p c n", p=128),
                            in_=kcmpT[:].rearrange("p (c n) -> p c n", c=2)).then_inc(h_, 16)
                e.dma_start(out=dbg_outs["kv"][256:512, :].rearrange("(c p) n -> p c n", p=128),
                            in_=vcmpT[:].rearrange("p (c n) -> p c n", c=2)).then_inc(h_, 16)
            e.wait_ge(sb.hnd("kvsp"), sb.val("kvsp"))
            if _DBG.get("dumps"):
                e.wait_ge(sb.hnd("dbg"), sb.val("dbg"))

        blk.sync(f_sync)
        blk.gpsimd(f_pool)
        blk.vector(f_vec)
        blk.scalar(f_act)
        blk.tensor(f_pe)
        blk.scalar(f_store)
        phase_join(["tensor", "vector", "scalar", "gpsimd"])
        p1.close()

    ph_p1()
    if NPH <= 1:
        es.close()
        return nc

    def ph_p2():
        p2o = ExitStack()
        lnTF = p2o.enter_context(nc.sbuf_tensor("lnTF", [128, 32 * NROW], BF16))

        NJ = 16
        b_xld = [sb.val("xld0"), sb.val("xld1")]
        b_ap = sb.val("apply"); b_sq = sb.val("sq")
        b_var = sb.val("var"); b_rstd = sb.val("rstd"); b_neg = sb.val("negmr")
        b_tr = sb.val("tr"); b_trev = sb.val("trev")

        p2 = ExitStack()
        xrow = [p2.enter_context(nc.sbuf_tensor(f"p2xr{i}", [128, DM], F32)) for i in range(2)]
        lnb = [p2.enter_context(nc.sbuf_tensor(f"p2ln{i}", [128, DM], BF16)) for i in range(2)]
        sqd = p2.enter_context(nc.sbuf_tensor("p2sqd", [128, DM], BF16))

        b_pj = sb.val("pj")

        def fa_sync(e):
            e.wait_ge(sb.hnd("pj"), b_pj)
            for j in range(NJ):
                e.wait_ge(sb.hnd("apply"), b_ap + max(j - 1, 0))
                h_, _ = sb.bump(f"xld{j % 2}", 16)
                e.dma_start(out=xrow[j % 2][:],
                            in_=x_own[128 * j:128 * (j + 1), :]).then_inc(h_, 16)

        def fa_vec(e):
            base = {"xld": b_xld, "sq": b_sq, "var": b_var, "rstd": b_rstd,
                    "negmr": b_neg}
            for j in range(NJ):
                p = j % 2
                stats_vec_j(e, j, p, xrow[p][:], base)
                for fb in range(4):
                    e.wait_ge(sb.hnd("tr"), b_tr + 4 * j + fb + 1)
                    b = fb % 2
                    src = PS[b][:].bitcast(BF16)[:, 0:1024]
                    dst = lnTF[:, :].rearrange("p (c r) -> p c r", r=NROW)
                    h_, _ = sb.bump("trev")
                    e.tensor_copy(out=dst[:, 8 * fb:8 * fb + 8, 128 * j:128 * j + 128],
                                  in_=src.rearrange("p (c r) -> p c r", r=128)).then_inc(h_, 1)

        def fa_act(e):
            base = {"xld": b_xld, "sq": b_sq, "var": b_var, "rstd": b_rstd,
                    "negmr": b_neg}
            sqd_ref[0] = sqd
            for j in range(NJ):
                p = j % 2
                stats_act_j(e, j, p, xrow[p][:], lnb[p][:], base,
                            extra_tr_wait=(b_tr + 4 * (j - 1)) if j >= 2 else None)

        def fa_pe(e):
            for j in range(NJ):
                p = j % 2
                e.wait_ge(sb.hnd("apply"), b_ap + j + 1)
                for fb in range(4):
                    b = fb % 2
                    nfill = 4 * j + fb
                    if nfill >= 2:
                        e.wait_ge(sb.hnd("trev"), b_trev + nfill - 1)
                    for t8 in range(8):
                        c = 8 * fb + t8
                        inst = e.transpose(
                            PS[b][:].bitcast(BF16)[:, 128 * t8:128 * (t8 + 1)],
                            lnb[p][:, 128 * c:128 * (c + 1)], ident[:])
                    h_, _ = sb.bump("tr")
                    inst.then_inc(h_, 1)

        blk.sync(fa_sync)
        blk.vector(fa_vec)
        blk.scalar(fa_act)
        blk.tensor(fa_pe)
        phase_join(["tensor", "vector", "scalar"])
        p2.close()

        p2 = ExitStack()
        wpan = [p2.enter_context(nc.sbuf_tensor(f"wpan{i}", [128, 32 * 128], F32)) for i in range(2)]
        wpanb = [p2.enter_context(nc.sbuf_tensor(f"wpanb{i}", [128, 32 * 128], BF16)) for i in range(2)]
        qstg = [p2.enter_context(nc.sbuf_tensor(f"qstg{i}", [128, 512], BF16)) for i in range(2)]

        b_pj2 = sb.val("pj")

        def fb_sync(e):
            e.wait_ge(sb.hnd("pj"), b_pj2)
            for qc in range(32):
                if qc >= 2:
                    e.wait_ge(sb.hnd("pan_cast"), qc - 1)
                h_, _ = sb.bump(f"pnl{qc % 2}", 16)
                e.dma_start(out=wpan[qc % 2][:].rearrange("p (c n) -> p c n", c=32),
                            in_=WqT[:, 128 * qc:128 * (qc + 1)]
                            .rearrange("(c p) n -> p c n", p=128)).then_inc(h_, 16)

        def fb_pool(e):
            for qc in range(32):
                e.wait_ge(sb.hnd(f"pnl{qc % 2}"), 16 * (qc // 2 + 1))
                if qc >= 2:
                    e.wait_ge(sb.hnd("qmm"), 4 * (qc - 1))
                h_, _ = sb.bump("pan_cast")
                e.tensor_copy(out=wpanb[qc % 2][:], in_=wpan[qc % 2][:]).then_inc(h_, 1)

        def fb_vec(e):
            for qc in range(32):
                for rb in range(4):
                    n = 4 * qc + rb
                    e.wait_ge(sb.hnd("qmm"), n + 1)
                    if n >= 2:
                        e.wait_ge(sb.hnd(f"qst{n % 2}"), 16 * (n // 2))
                    h_, _ = sb.bump("qev")
                    e.tensor_scalar(out=qstg[n % 2][:], in0=PS[rb % 2][:],
                                    scalar1=QSCALE, scalar2=None,
                                    op0=ALU.mult).then_inc(h_, 1)

        def fb_pe(e):
            e.wait_ge(sb.hnd("trev"), b_trev + 4 * NJ)
            for qc in range(32):
                e.wait_ge(sb.hnd("pan_cast"), qc + 1)
                for rb in range(4):
                    n = 4 * qc + rb
                    bb = rb % 2
                    if n >= 2:
                        e.wait_ge(sb.hnd("qev"), n - 1)
                    for c in range(32):
                        inst = e.matmul(PS[bb][:],
                                        wpanb[qc % 2][:, 128 * c:128 * (c + 1)],
                                        lnTF[:, NROW * c + 512 * rb:NROW * c + 512 * (rb + 1)],
                                        start=(c == 0), stop=(c == 31))
                    h_, _ = sb.bump("qmm")
                    inst.then_inc(h_, 1)

        def fb_store(e):
            for qc in range(32):
                for rb in range(4):
                    n = 4 * qc + rb
                    e.wait_ge(sb.hnd("qev"), n + 1)
                    h_, _ = sb.bump(f"qst{n % 2}", 16)
                    e.dma_start(out=q_sc[(rb * 32 + qc) * 128:(rb * 32 + qc) * 128 + 128, :],
                                in_=qstg[n % 2][:]).then_inc(h_, 16)
            e.wait_ge(sb.hnd("qst0"), sb.val("qst0"))
            e.wait_ge(sb.hnd("qst1"), sb.val("qst1"))

        blk.sync(fb_sync)
        blk.gpsimd(fb_pool)
        blk.vector(fb_vec)
        blk.tensor(fb_pe)
        blk.scalar(fb_store)
        phase_join(["tensor", "vector", "gpsimd", "scalar"])
        p2.close()
        p2o.close()

    ph_p2()
    if NPH <= 2:
        es.close()
        return nc


    def ph_p3():
        p3 = ExitStack()
        kcmpT = p3.enter_context(nc.sbuf_tensor("p3kc", [128, 2 * S], BF16))
        vcmpT = p3.enter_context(nc.sbuf_tensor("p3vc", [128, 2 * S], BF16))
        qbuf = [p3.enter_context(nc.sbuf_tensor(f"qb{i}", [128, 32 * 512], BF16)) for i in range(2)]
        mbuf = [p3.enter_context(nc.sbuf_tensor(f"mb{i}", [128, 8 * 512], BF16)) for i in range(2)]
        kseg = p3.enter_context(nc.sbuf_tensor("kseg", [128, 1024], BF16))
        vseg = p3.enter_context(nc.sbuf_tensor("vseg", [128, 8 * 128], BF16))
        pbuf = p3.enter_context(nc.sbuf_tensor("pbuf", [128, 8 * 512], BF16))
        lsb = p3.enter_context(nc.sbuf_tensor("lsb", [4, 512], F32))
        lrec = p3.enter_context(nc.sbuf_tensor("lrec", [4, 512], F32))
        selt = p3.enter_context(nc.sbuf_tensor("selt", [4, 512], F32))
        recb = p3.enter_context(nc.sbuf_tensor("recb", [128, 512], F32))
        osb = [p3.enter_context(nc.sbuf_tensor(f"osb{i}", [128, 512], BF16)) for i in range(2)]

        b_qev = sb.val("qev")
        b_qst = [sb.val("qst0"), sb.val("qst1")]
        b_kvsp = sb.val("kvsp")
        b_pj3 = sb.val("pj")

        freemap = {b: None for b in range(8)}
        nw = {b: 0 for b in range(8)}
        n_sop = [0]; n_msk = [0]; n_upe = [0]; n_osc = [0]; n_swp = [0]
        pvd_hist = [0]
        sched = []
        for t in range(NT):
            tg = []
            for g in range(8):
                n = SWEEPS[t]
                rot = [5, 6, 7]
                ri = [0]

                def nb():
                    b = rot[ri[0] % 3]
                    ri[0] += 1
                    nw[b] += 1
                    fw = freemap[b]
                    return b, nw[b], fw

                segs = []
                for s in range(n // 8):
                    ups = []
                    for half in range(2):
                        b, w, fw = nb()
                        n_upe[0] += 1
                        ups.append(dict(k="k", s=s, half=half, b=b, bw=w, fw=fw,
                                        upe=n_upe[0]))
                        freemap[b] = ("upe", n_upe[0])
                    for kc in range(8):
                        b, w, fw = nb()
                        n_upe[0] += 1
                        ups.append(dict(k="v", s=s, kc=kc, b=b, bw=w, fw=fw,
                                        upe=n_upe[0]))
                        freemap[b] = ("upe", n_upe[0])
                    sops = []
                    for c in range(8):
                        C = 8 * s + c
                        for h in range(4):
                            b, w, fw = nb()
                            n_sop[0] += 1
                            masked = C >= n - 8
                            if masked:
                                n_msk[0] += 1
                            sops.append(dict(s=s, C=C, h=h, b=b, bw=w, fw=fw,
                                             masked=masked, isop=n_sop[0],
                                             imsk=n_msk[0],
                                             slot=(n_sop[0] - 1) % 8))
                            freemap[b] = ("pexp", n_sop[0])
                    segs.append(dict(ups=ups, sops=sops,
                                     war_pvd=pvd_hist[-1],
                                     upe_done=n_upe[0]))
                    pvd_hist.append(n_sop[0])
                ofw = [freemap[hh] for hh in range(4)]
                lfw = freemap[4]
                olast = [0] * 4
                for op in segs[-1]["sops"]:
                    if op["C"] == n - 1:
                        olast[op["h"]] = op["isop"]
                lbwv = None
                nw[4] += 1
                lbwv = nw[4]
                bcb = []
                for hh in range(4):
                    b, w, fw = nb()
                    n_osc[0] += 1
                    bcb.append(dict(b=b, bw=w, fw=fw, oi=n_osc[0]))
                    freemap[b] = ("oscd", n_osc[0])
                    freemap[hh] = ("oscd", n_osc[0])
                n_swp[0] += 1
                freemap[4] = ("lse", n_swp[0])
                tg.append(dict(segs=segs, ofw=ofw, lfw=lfw, olast=olast,
                               lbw=lbwv, bcb=bcb, swp=n_swp[0]))
            sched.append(tg)

        def wait_fw(e, fw):
            if fw is not None:
                e.wait_ge(sb.hnd(fw[0]), fw[1])

        def emit_stores(e, t):
            for g in range(8):
                for hh in range(4):
                    oi = 32 * t + 4 * g + hh
                    e.wait_ge(sb.hnd("oscd"), oi + 1)
                    h_, _ = sb.bump(f"ost{oi % 2}", 16)
                    r0 = (4 * g + hh) * 128
                    e.dma_start(out=o_sc[r0:r0 + 128, 512 * t:512 * t + 512],
                                in_=osb[oi % 2][:]).then_inc(h_, 16)

        def f_sync(e):
            e.wait_ge(sb.hnd("pj"), b_pj3)
            e.wait_ge(sb.hnd("kvsp"), b_kvsp)
            h_, _ = sb.bump("kv_ld", 16)
            e.dma_start(out=kcmpT[:].rearrange("p (c n) -> p c n", c=2),
                        in_=kv_sc[0:256, :].rearrange("(c p) n -> p c n", p=128)).then_inc(h_, 16)
            h_, _ = sb.bump("kv_ld", 16)
            e.dma_start(out=vcmpT[:].rearrange("p (c n) -> p c n", c=2),
                        in_=kv_sc[256:512, :].rearrange("(c p) n -> p c n", p=128)).then_inc(h_, 16)
            e.wait_ge(sb.hnd("qst0"), b_qst[0])
            e.wait_ge(sb.hnd("qst1"), b_qst[1])
            for t in range(NT):
                if t >= 2:
                    e.wait_ge(sb.hnd("tdone"), t - 1)
                h_, _ = sb.bump(f"qtl{t % 2}", 16)
                e.dma_start(out=qbuf[t % 2][:].rearrange("p (c n) -> p c n", c=32),
                            in_=q_sc[4096 * t:4096 * (t + 1), :]
                            .rearrange("(c p) n -> p c n", p=128)).then_inc(h_, 16)
                if t >= 2:
                    e.wait_ge(sb.hnd("mdone"), t - 1)
                h_, _ = sb.bump(f"mld{t % 2}", 16)
                e.dma_start(out=mbuf[t % 2][:].rearrange("p (c n) -> p c n", c=8),
                            in_=masks[8 * 128 * t:8 * 128 * (t + 1), :]
                            .rearrange("(c p) n -> p c n", p=128)).then_inc(h_, 16)
                if t >= 1:
                    emit_stores(e, t - 1)
            emit_stores(e, NT - 1)
            e.wait_ge(sb.hnd("ost0"), sb.val("ost0"))
            e.wait_ge(sb.hnd("ost1"), sb.val("ost1"))

        def f_pool(e):
            e.wait_ge(sb.hnd("consts"), CONSTS_DONE)
            e.memset(selt[:], 1.0)
            e.drain()
            e.affine_select(out=selt[:], in_=selt[:], pattern=[[1, 512]],
                            compare_op=ALU.is_ge, fill=0.0, base=0,
                            channel_multiplier=-128)
            e.drain()
            h_, _ = sb.bump("selc")
            e.affine_select(out=selt[:], in_=selt[:], pattern=[[-1, 512]],
                            compare_op=ALU.is_ge, fill=0.0, base=127,
                            channel_multiplier=128).then_inc(h_, 1)

        def emit_lpv(e, n, ent, op):
            C, h = op["C"], op["h"]
            if op["masked"]:
                e.wait_ge(sb.hnd("pmask"), op["imsk"])
            else:
                e.wait_ge(sb.hnd("pexp"), op["isop"])
            first = (C == 0 and h == 0)
            firsth = (C == 0)
            last = (C == n - 1 and h == 3)
            lasth = (C == n - 1)
            pb = pbuf[:, 512 * op["slot"]:512 * op["slot"] + 512]
            if firsth:
                wait_fw(e, ent["ofw"][h])
            if first:
                wait_fw(e, ent["lfw"])
            inst = e.matmul(PS[4][0:4, :], inds[:, 4 * h:4 * h + 4], pb,
                            start=first, stop=last)
            if last:
                h_, _ = sb.bump("bk4w")
                inst.then_inc(h_, 1)
            inst2 = e.matmul(PS[h][:], vseg[:, 128 * (C % 8):128 * (C % 8) + 128],
                             pb, start=firsth, stop=lasth)
            h_, _ = sb.bump("pvd")
            inst2.then_inc(h_, 1)

        def f_pe(e):
            e.wait_ge(sb.hnd("qev"), b_qev)
            e.wait_ge(sb.hnd("kv_ld"), 32)
            e.wait_ge(sb.hnd("selc"), 1)
            for t in range(NT):
                e.wait_ge(sb.hnd(f"qtl{t % 2}"), 16 * (t // 2 + 1))
                n = SWEEPS[t]
                for g in range(8):
                    ent = sched[t][g]
                    for seg in ent["segs"]:
                        for op in seg["ups"]:
                            b = op["b"]
                            wait_fw(e, op["fw"])
                            if op["k"] == "k":
                                s, half = op["s"], op["half"]
                                for cc in range(2):
                                    inst = e.matmul(
                                        PS[b][:],
                                        kupb[:, 1024 * cc + 128 * g:1024 * cc + 128 * g + 128],
                                        kcmpT[:, 4096 * cc + 1024 * s + 512 * half:
                                              4096 * cc + 1024 * s + 512 * half + 512],
                                        start=(cc == 0), stop=(cc == 1))
                            else:
                                s, kc = op["s"], op["kc"]
                                for cc in range(2):
                                    inst = e.matmul(
                                        PS[b][:, 0:128],
                                        vcmpT[:, 4096 * cc + 1024 * s + 128 * kc:
                                              4096 * cc + 1024 * s + 128 * kc + 128],
                                        vupb[:, 1024 * cc + 128 * g:1024 * cc + 128 * g + 128],
                                        start=(cc == 0), stop=(cc == 1))
                            h_, _ = sb.bump(f"bk{b}w")
                            inst.then_inc(h_, 1)
                        pend = []
                        for i, op in enumerate(seg["sops"]):
                            b = op["b"]
                            wait_fw(e, op["fw"])
                            if i == 0:
                                e.wait_ge(sb.hnd("upe"), seg["upe_done"])
                            inst = e.matmul(
                                PS[b][:],
                                kseg[:, 128 * (op["C"] % 8):128 * (op["C"] % 8) + 128],
                                qbuf[t % 2][:, 512 * (4 * g + op["h"]):512 * (4 * g + op["h"]) + 512],
                                start=True, stop=True)
                            h_, _ = sb.bump(f"bk{b}w")
                            inst.then_inc(h_, 1)
                            pend.append(op)
                            if len(pend) > 2:
                                emit_lpv(e, n, ent, pend.pop(0))
                        for op in pend:
                            emit_lpv(e, n, ent, op)
                    e.wait_ge(sb.hnd("lrc"), 8 * t + g + 1)
                    for hh in range(4):
                        bc = ent["bcb"][hh]
                        wait_fw(e, bc["fw"])
                        inst = e.matmul(PS[bc["b"]][:], selt[:, 128 * hh:128 * hh + 128],
                                        lrec[:], start=True, stop=True)
                        h_, _ = sb.bump(f"bk{bc['b']}w")
                        inst.then_inc(h_, 1)
                    h_, _ = sb.bump("bcd")
                    e.drain().then_inc(h_, 1)
                h_, _ = sb.bump("tdone")
                e.drain().then_inc(h_, 1)

        def f_act(e):
            for t in range(NT):
                n = SWEEPS[t]
                for g in range(8):
                    ent = sched[t][g]
                    for seg in ent["segs"]:
                        for op in seg["sops"]:
                            b = op["b"]
                            e.wait_ge(sb.hnd(f"bk{b}w"), op["bw"])
                            if op["isop"] > 8:
                                e.wait_ge(sb.hnd("pvd"), op["isop"] - 8)
                            pb = pbuf[:, 512 * op["slot"]:512 * op["slot"] + 512]
                            h1, _ = sb.bump("pexp")
                            e.activation(out=pb, in_=PS[b][:], func=AF.Exp,
                                         bias=cst[:, 2:3],
                                         scale=1.0).then_inc(h1, 1)
                    e.wait_ge(sb.hnd("lse"), 8 * t + g + 1)
                    e.drain()
                    if 8 * t + g > 0:
                        e.wait_ge(sb.hnd("bcd"), 8 * t + g)
                    h_, _ = sb.bump("lr1")
                    e.activation(out=lrec[:], in_=lsb[:], func=AF.Ln,
                                 scale=1.0).then_inc(h_, 1)
                    e.drain()
                    h_, _ = sb.bump("lrc")
                    e.activation(out=lrec[:], in_=lrec[:], func=AF.Exp,
                                 scale=cst[0:4, 4:5]).then_inc(h_, 1)

        def f_vec(e):
            oi = 0
            for t in range(NT):
                n = SWEEPS[t]
                e.wait_ge(sb.hnd(f"mld{t % 2}"), 16 * (t // 2 + 1))
                for g in range(8):
                    ent = sched[t][g]
                    for seg in ent["segs"]:
                        if seg["war_pvd"] > 0:
                            e.wait_ge(sb.hnd("pvd"), seg["war_pvd"])
                        for op in seg["ups"]:
                            b = op["b"]
                            e.wait_ge(sb.hnd(f"bk{b}w"), op["bw"])
                            h2, _ = sb.bump("upe")
                            if op["k"] == "k":
                                e.tensor_copy(
                                    out=kseg[:, 512 * op["half"]:512 * op["half"] + 512],
                                    in_=PS[b][:]).then_inc(h2, 1)
                            else:
                                e.tensor_copy(
                                    out=vseg[:, 128 * op["kc"]:128 * op["kc"] + 128],
                                    in_=PS[b][:, 0:128]).then_inc(h2, 1)
                        for op in seg["sops"]:
                            if not op["masked"]:
                                continue
                            e.wait_ge(sb.hnd("pexp"), op["isop"])
                            pb = pbuf[:, 512 * op["slot"]:512 * op["slot"] + 512]
                            ci = op["C"] - (n - 8)
                            h_, _ = sb.bump("pmask")
                            e.tensor_tensor(out=pb, in0=pb,
                                            in1=mbuf[t % 2][:, 512 * ci:512 * ci + 512],
                                            op=ALU.mult).then_inc(h_, 1)
                    e.wait_ge(sb.hnd("bk4w"), ent["lbw"])
                    if 8 * t + g > 0:
                        e.wait_ge(sb.hnd("lr1"), 8 * t + g)
                    e.drain()
                    h2, _ = sb.bump("lse")
                    e.tensor_copy(out=lsb[:], in_=PS[4][0:4, :]).then_inc(h2, 1)
                    for hh in range(4):
                        bc = ent["bcb"][hh]
                        e.wait_ge(sb.hnd(f"bk{bc['b']}w"), bc["bw"])
                        e.drain()
                        e.tensor_copy(out=recb[:], in_=PS[bc["b"]][:])
                        e.wait_ge(sb.hnd("pvd"), ent["olast"][hh])
                        if oi >= 2:
                            e.wait_ge(sb.hnd(f"ost{oi % 2}"), 16 * (oi // 2))
                        e.drain()
                        h2, _ = sb.bump("oscd")
                        e.tensor_tensor(out=osb[oi % 2][:], in0=PS[hh][:],
                                        in1=recb[:],
                                        op=ALU.mult).then_inc(h2, 1)
                        oi += 1
                h_, _ = sb.bump("mdone")
                e.drain().then_inc(h_, 1)

        def f_store(e):
            if "o" in dbg_outs:
                e.wait_ge(sb.hnd("ost0"), sb.val("ost0"))
                e.wait_ge(sb.hnd("ost1"), sb.val("ost1"))
                h_, _ = sb.bump("dbg", 16)
                e.dma_start(out=dbg_outs["o"][:, :].rearrange("(c p) n -> p c n", p=128),
                            in_=o_sc[:, :].rearrange("(c p) n -> p c n", p=128)).then_inc(h_, 16)
                e.wait_ge(sb.hnd("dbg"), sb.val("dbg"))

        blk.sync(f_sync)
        blk.gpsimd(f_pool)
        blk.tensor(f_pe)
        blk.scalar(f_act)
        blk.vector(f_vec)
        blk.scalar(f_store)
        phase_join(["tensor", "vector", "scalar", "gpsimd"])
        p3.close()

    ph_p3()
    kv_es.close()
    if NPH <= 3:
        es.close()
        return nc


    def ph_g3():
        g3 = ExitStack()
        otb = g3.enter_context(nc.sbuf_tensor("otb", [128, 32 * 1024], BF16))
        wop = [g3.enter_context(nc.sbuf_tensor(f"wop{i}", [128, 32 * 256], F32)) for i in range(2)]
        wopb = [g3.enter_context(nc.sbuf_tensor(f"wopb{i}", [128, 32 * 256], BF16)) for i in range(2)]
        aostg = [g3.enter_context(nc.sbuf_tensor(f"aostg{i}", [128, 256], F32)) for i in range(2)]

        b_pj = sb.val("pj")
        b_kvld = sb.val("kv_ld")
        b_pnl = [sb.val("pnl0"), sb.val("pnl1")]
        b_cast = sb.val("pan_cast")
        b_mm = sb.val("qmm")
        b_ev = sb.val("qev")
        b_st = [sb.val("qst0"), sb.val("qst1")]
        b_ost = [sb.val("ost0"), sb.val("ost1")]

        def f_sync(e):
            e.wait_ge(sb.hnd("pj"), b_pj)
            e.wait_ge(sb.hnd("ost0"), b_ost[0])
            e.wait_ge(sb.hnd("ost1"), b_ost[1])
            for rg in range(2):
                if rg == 1:
                    e.wait_ge(sb.hnd("qev"), b_ev + 128)
                h_, _ = sb.bump("kv_ld", 16)
                e.dma_start(out=otb[:].rearrange("p (c n) -> p c n", c=32),
                            in_=o_sc[:, 1024 * rg:1024 * (rg + 1)]
                            .rearrange("(c p) n -> p c n", p=128)).then_inc(h_, 16)
                for db in range(16):
                    i = 16 * rg + db
                    if i >= 2:
                        e.wait_ge(sb.hnd("pan_cast"), b_cast + i - 1)
                    h_, _ = sb.bump(f"pnl{i % 2}", 16)
                    e.dma_start(out=wop[i % 2][:].rearrange("p (c n) -> p c n", c=32),
                                in_=WoT[:, 256 * db:256 * (db + 1)]
                                .rearrange("(c p) n -> p c n", p=128)).then_inc(h_, 16)

        def f_pool(e):
            for i in range(32):
                e.wait_ge(sb.hnd(f"pnl{i % 2}"), b_pnl[i % 2] + 16 * (i // 2 + 1))
                if i >= 2:
                    e.wait_ge(sb.hnd("qmm"), b_mm + 8 * (i - 1))
                h_, _ = sb.bump("pan_cast")
                e.tensor_copy(out=wopb[i % 2][:], in_=wop[i % 2][:]).then_inc(h_, 1)

        def f_pe(e):
            for rg in range(2):
                e.wait_ge(sb.hnd("kv_ld"), b_kvld + 16 * (rg + 1))
                for db in range(16):
                    i = 16 * rg + db
                    e.wait_ge(sb.hnd("pan_cast"), b_cast + i + 1)
                    for rs in range(8):
                        nn = 8 * i + rs
                        bb = nn % 2
                        if nn >= 2:
                            e.wait_ge(sb.hnd("qev"), b_ev + nn - 1)
                        for c in range(32):
                            inst = e.matmul(
                                PS[bb][:, 0:256],
                                otb[:, 1024 * c + 128 * rs:1024 * c + 128 * rs + 128],
                                wopb[i % 2][:, 256 * c:256 * (c + 1)],
                                start=(c == 0), stop=(c == 31))
                        h_, _ = sb.bump("qmm")
                        inst.then_inc(h_, 1)

        def f_vec(e):
            for nn in range(256):
                e.wait_ge(sb.hnd("qmm"), b_mm + nn + 1)
                if nn >= 2:
                    e.wait_ge(sb.hnd(f"qst{nn % 2}"), b_st[nn % 2] + 16 * (nn // 2))
                h_, _ = sb.bump("qev")
                e.tensor_copy(out=aostg[nn % 2][:],
                              in_=PS[nn % 2][:, 0:256]).then_inc(h_, 1)

        def f_store(e):
            for rg in range(2):
                for db in range(16):
                    for rs in range(8):
                        nn = 128 * rg + 8 * db + rs
                        e.wait_ge(sb.hnd("qev"), b_ev + nn + 1)
                        h_, _ = sb.bump(f"qst{nn % 2}", 16)
                        r0 = 1024 * rg + 128 * rs
                        e.dma_start(out=ao_sc[r0:r0 + 128, 256 * db:256 * (db + 1)],
                                    in_=aostg[nn % 2][:]).then_inc(h_, 16)
            e.wait_ge(sb.hnd("qst0"), sb.val("qst0"))
            e.wait_ge(sb.hnd("qst1"), sb.val("qst1"))

        blk.sync(f_sync)
        blk.gpsimd(f_pool)
        blk.tensor(f_pe)
        blk.vector(f_vec)
        blk.scalar(f_store)
        phase_join(["tensor", "vector", "gpsimd", "scalar"])
        g3.close()

    ph_g3()
    if NPH <= 4:
        es.close()
        return nc

    def ph_p3b():
        p4 = ExitStack()
        aor = [p4.enter_context(nc.sbuf_tensor(f"aor{i}", [128, DM], F32)) for i in range(2)]
        xor_ = [p4.enter_context(nc.sbuf_tensor(f"xor{i}", [128, DM], F32)) for i in range(2)]
        t1b = [p4.enter_context(nc.sbuf_tensor(f"t1b{i}", [128, DM], F32)) for i in range(2)]
        x1b = [p4.enter_context(nc.sbuf_tensor(f"x1b{i}", [128, DM], F32)) for i in range(2)]
        ln2b = [p4.enter_context(nc.sbuf_tensor(f"l2b{i}", [128, DM], BF16)) for i in range(2)]
        sqd = p4.enter_context(nc.sbuf_tensor("p4sqd", [128, DM], BF16))
        ln2stg = [p4.enter_context(nc.sbuf_tensor(f"ln2stg{i}", [128, 32 * 128], BF16)) for i in range(2)]

        NJ = 16
        b_pj = sb.val("pj")
        b = {k: sb.val(k) for k in ("qtl0", "qtl1", "xld0", "xld1", "sq", "var",
                                    "rstd", "negmr", "apply", "sq2", "var2",
                                    "rstd2", "negmr2", "ap2", "x1d", "tr",
                                    "trev", "ost0", "ost1", "mld0", "mld1",
                                    "qst0", "qst1")}

        def emit_stores_j(e, j):
            p = j % 2
            e.wait_ge(sb.hnd("x1d"), b["x1d"] + j + 1)
            h_, _ = sb.bump(f"ost{p}", 16)
            e.dma_start(out=x1_sc[128 * j:128 * (j + 1), :],
                        in_=x1b[p][:]).then_inc(h_, 16)
            e.wait_ge(sb.hnd("trev"), b["trev"] + 4 * (j + 1))
            h_, _ = sb.bump(f"mld{p}", 16)
            l2v = ln2_sc.rearrange("(c p) n -> p c n", p=128)
            srcv = ln2stg[p][:, :].rearrange("p (c r) -> p c r", r=128)
            e.dma_start(out=l2v[:, :, 128 * j:128 * j + 128],
                        in_=srcv).then_inc(h_, 16)

        def f_sync(e):
            e.wait_ge(sb.hnd("pj"), b_pj)
            e.wait_ge(sb.hnd("qst0"), b["qst0"])
            e.wait_ge(sb.hnd("qst1"), b["qst1"])
            for j in range(NJ):
                e.wait_ge(sb.hnd("apply"), b["apply"] + max(j - 1, 0))
                h_, _ = sb.bump(f"qtl{j % 2}", 16)
                e.dma_start(out=aor[j % 2][:],
                            in_=ao_sc[128 * j:128 * (j + 1), :]).then_inc(h_, 16)
                e.wait_ge(sb.hnd("x1d"), b["x1d"] + max(j - 1, 0))
                h_, _ = sb.bump(f"xld{j % 2}", 16)
                e.dma_start(out=xor_[j % 2][:],
                            in_=x_own[128 * j:128 * (j + 1), :]).then_inc(h_, 16)
                if j >= 1:
                    emit_stores_j(e, j - 1)
            emit_stores_j(e, NJ - 1)
            e.wait_ge(sb.hnd("ost0"), sb.val("ost0"))
            e.wait_ge(sb.hnd("ost1"), sb.val("ost1"))
            e.wait_ge(sb.hnd("mld0"), sb.val("mld0"))
            e.wait_ge(sb.hnd("mld1"), sb.val("mld1"))

        def f_vec(e):
            for j in range(NJ):
                p = j % 2
                e.drain()
                e.wait_ge(sb.hnd(f"qtl{p}"), b[f"qtl{p}"] + 16 * (j // 2 + 1))
                e.tensor_reduce(out=stt[:, 8 * p:8 * p + 1], in_=aor[p][:],
                                axis=AX.X, op=ALU.add)
                e.drain()
                e.wait_ge(sb.hnd("sq"), b["sq"] + j + 1)
                if j >= 1:
                    e.wait_ge(sb.hnd("rstd"), b["rstd"] + j)
                e.tensor_scalar(out=stt[:, 2:3], in0=stt[:, 8 * p:8 * p + 1],
                                scalar1=stt[:, 8 * p:8 * p + 1], scalar2=float(1.0 / DM),
                                op0=ALU.mult, op1=ALU.mult)
                e.drain()
                h_, _ = sb.bump("var")
                e.tensor_tensor(out=stt[:, 2:3], in0=stt[:, 8 * p + 1:8 * p + 2],
                                in1=stt[:, 2:3], op=ALU.subtract).then_inc(h_, 1)
                e.wait_ge(sb.hnd("rstd"), b["rstd"] + j + 1)
                e.drain()
                h_, _ = sb.bump("negmr")
                e.tensor_scalar(out=stt[:, 4:5], in0=stt[:, 8 * p:8 * p + 1],
                                scalar1=stt[:, 3:4], scalar2=float(-1.0 / DM),
                                op0=ALU.mult, op1=ALU.mult).then_inc(h_, 1)
                e.wait_ge(sb.hnd("apply"), b["apply"] + j + 1)
                e.wait_ge(sb.hnd(f"xld{p}"), b[f"xld{p}"] + 16 * (j // 2 + 1))
                if j >= 2:
                    e.wait_ge(sb.hnd("ap2"), b["ap2"] + j - 1)
                    e.wait_ge(sb.hnd(f"ost{p}"), b[f"ost{p}"] + 16 * (j // 2))
                e.drain()
                h_, _ = sb.bump("x1d")
                e.tensor_tensor(out=x1b[p][:], in0=t1b[p][:], in1=xor_[p][:],
                                op=ALU.add).then_inc(h_, 1)
                e.drain()
                e.tensor_reduce(out=stt[:, 8 * p + 5:8 * p + 6], in_=x1b[p][:],
                                axis=AX.X, op=ALU.add)
                e.drain()
                e.wait_ge(sb.hnd("sq2"), b["sq2"] + j + 1)
                if j >= 1:
                    e.wait_ge(sb.hnd("rstd2"), b["rstd2"] + j)
                e.tensor_scalar(out=stt[:, 7:8], in0=stt[:, 8 * p + 5:8 * p + 6],
                                scalar1=stt[:, 8 * p + 5:8 * p + 6], scalar2=float(1.0 / DM),
                                op0=ALU.mult, op1=ALU.mult)
                e.drain()
                h_, _ = sb.bump("var2")
                e.tensor_tensor(out=stt[:, 7:8], in0=stt[:, 8 * p + 6:8 * p + 7],
                                in1=stt[:, 7:8], op=ALU.subtract).then_inc(h_, 1)
                e.wait_ge(sb.hnd("rstd2"), b["rstd2"] + j + 1)
                e.drain()
                h_, _ = sb.bump("negmr2")
                e.tensor_scalar(out=stt[:, 11:12], in0=stt[:, 8 * p + 5:8 * p + 6],
                                scalar1=stt[:, 10:11], scalar2=float(-1.0 / DM),
                                op0=ALU.mult, op1=ALU.mult).then_inc(h_, 1)
                for fb in range(4):
                    e.wait_ge(sb.hnd("tr"), b["tr"] + 4 * j + fb + 1)
                    if fb == 0 and j >= 2:
                        e.wait_ge(sb.hnd(f"mld{p}"), b[f"mld{p}"] + 16 * (j // 2))
                    bb = fb % 2
                    src = PS[bb][:].bitcast(BF16)[:, 0:1024]
                    dst = ln2stg[p][:, :].rearrange("p (c r) -> p c r", r=128)
                    h_, _ = sb.bump("trev")
                    e.tensor_copy(out=dst[:, 8 * fb:8 * fb + 8, :],
                                  in_=src.rearrange("p (c r) -> p c r", r=128)).then_inc(h_, 1)

        def f_act(e):
            for j in range(NJ):
                p = j % 2
                e.drain()
                e.wait_ge(sb.hnd(f"qtl{p}"), b[f"qtl{p}"] + 16 * (j // 2 + 1))
                if j >= 2:
                    e.wait_ge(sb.hnd("var"), b["var"] + j - 1)
                h_, _ = sb.bump("sq")
                e.activation(out=sqd[:], in_=aor[p][:], func=AF.Square,
                             accum_out=stt[:, 8 * p + 1:8 * p + 2]).then_inc(h_, 1)
                e.wait_ge(sb.hnd("var"), b["var"] + j + 1)
                e.activation(out=stt[:, 2:3], in_=stt[:, 2:3], func=AF.Ln,
                             bias=cst[:, 0:1], scale=1.0)
                e.drain()
                h_, _ = sb.bump("rstd")
                e.activation(out=stt[:, 3:4], in_=stt[:, 2:3], func=AF.Exp,
                             bias=cst[:, 1:2], scale=cst[:, 3:4]).then_inc(h_, 1)
                e.wait_ge(sb.hnd("negmr"), b["negmr"] + j + 1)
                if j >= 2:
                    e.wait_ge(sb.hnd("x1d"), b["x1d"] + j - 1)
                e.drain()
                h_, _ = sb.bump("apply")
                e.activation(out=t1b[p][:], in_=aor[p][:], func=AF.Identity,
                             bias=stt[:, 4:5], scale=stt[:, 3:4]).then_inc(h_, 1)
                e.drain()
                e.wait_ge(sb.hnd("x1d"), b["x1d"] + j + 1)
                if j >= 2:
                    e.wait_ge(sb.hnd("var2"), b["var2"] + j - 1)
                h_, _ = sb.bump("sq2")
                e.activation(out=sqd[:], in_=x1b[p][:], func=AF.Square,
                             accum_out=stt[:, 8 * p + 6:8 * p + 7]).then_inc(h_, 1)
                e.wait_ge(sb.hnd("var2"), b["var2"] + j + 1)
                e.activation(out=stt[:, 7:8], in_=stt[:, 7:8], func=AF.Ln,
                             bias=cst[:, 0:1], scale=1.0)
                e.drain()
                h_, _ = sb.bump("rstd2")
                e.activation(out=stt[:, 10:11], in_=stt[:, 7:8], func=AF.Exp,
                             bias=cst[:, 1:2], scale=cst[:, 3:4]).then_inc(h_, 1)
                e.wait_ge(sb.hnd("negmr2"), b["negmr2"] + j + 1)
                e.drain()
                if j >= 2:
                    e.wait_ge(sb.hnd("tr"), b["tr"] + 4 * (j - 1))
                h_, _ = sb.bump("ap2")
                e.activation(out=ln2b[p][:], in_=x1b[p][:], func=AF.Identity,
                             bias=stt[:, 11:12], scale=stt[:, 10:11]).then_inc(h_, 1)

        def f_pe(e):
            for j in range(NJ):
                p = j % 2
                e.wait_ge(sb.hnd("ap2"), b["ap2"] + j + 1)
                for fb in range(4):
                    bb = fb % 2
                    nfill = 4 * j + fb
                    if nfill >= 2:
                        e.wait_ge(sb.hnd("trev"), b["trev"] + nfill - 1)
                    for t8 in range(8):
                        c = 8 * fb + t8
                        inst = e.transpose(
                            PS[bb][:].bitcast(BF16)[:, 128 * t8:128 * (t8 + 1)],
                            ln2b[p][:, 128 * c:128 * (c + 1)], ident[:])
                    h_, _ = sb.bump("tr")
                    inst.then_inc(h_, 1)

        blk.sync(f_sync)
        blk.vector(f_vec)
        blk.scalar(f_act)
        blk.tensor(f_pe)
        phase_join(["tensor", "vector", "scalar"])
        p4.close()

    ph_p3b()
    if NPH <= 5:
        es.close()
        return nc


    g45 = ExitStack()
    dA_sb = g45.enter_context(nc.sbuf_tensor("dA_sb", [16, 2048], BF16))

    def ph_g4():
        g4 = ExitStack()
        ln2TF = g4.enter_context(nc.sbuf_tensor("ln2TF", [128, 32 * NROW], BF16))
        gp = [g4.enter_context(nc.sbuf_tensor(f"gp{i}", [128, 32 * 64], F32)) for i in range(2)]
        gpb = [g4.enter_context(nc.sbuf_tensor(f"gpb{i}", [128, 32 * 128], BF16)) for i in range(2)]
        gsil = [g4.enter_context(nc.sbuf_tensor(f"gsil{i}", [128, 2048], BF16)) for i in range(2)]
        h_sb = [g4.enter_context(nc.sbuf_tensor(f"hsb{i}", [128, 2048], BF16)) for i in range(2)]
        gATb = g4.enter_context(nc.sbuf_tensor("gATb", [128, 32 * 16], BF16))
        uATb = g4.enter_context(nc.sbuf_tensor("uATb", [128, 32 * 16], BF16))
        dATb = g4.enter_context(nc.sbuf_tensor("dATb", [128, NHC * 16], BF16))
        gA_sb = g4.enter_context(nc.sbuf_tensor("gA_sb", [16, 2048], BF16))
        uA_sb = g4.enter_context(nc.sbuf_tensor("uA_sb", [16, 2048], BF16))
        smw = g4.enter_context(nc.sbuf_tensor("smw", [128, NHC * 16], F32))
        gBst = [g4.enter_context(nc.sbuf_tensor(f"gBst{i}", [16, 128], F32)) for i in range(2)]
        uBst = [g4.enter_context(nc.sbuf_tensor(f"uBst{i}", [16, 128], F32)) for i in range(2)]
        gBb = [g4.enter_context(nc.sbuf_tensor(f"gBb{i}", [16, 128], BF16)) for i in range(2)]
        uBb = [g4.enter_context(nc.sbuf_tensor(f"uBb{i}", [16, 128], BF16)) for i in range(2)]

        b_pj = sb.val("pj")
        b = {k: sb.val(k) for k in ("wkkup", "wk_cast", "kv_ld", "ymm", "yev",
                                    "pnl0", "pnl1", "pan_cast", "qmm", "qev",
                                    "qst0", "qst1", "pvd", "gsl", "hm",
                                    "mld0", "mld1", "wkld0", "wkld1",
                                    "wkld2", "wkld3")}

        def f_sync(e):
            e.wait_ge(sb.hnd("pj"), b_pj)
            for q in range(4):
                e.wait_ge(sb.hnd(f"wkld{q}"), b[f"wkld{q}"])
            e.wait_ge(sb.hnd("mld0"), b["mld0"])
            e.wait_ge(sb.hnd("mld1"), b["mld1"])
            h_, _ = sb.bump("kv_ld", 16)
            e.dma_start(out=ln2TF[:].rearrange("p (c n) -> p c n", c=32),
                        in_=ln2_sc[:, :].rearrange("(c p) n -> p c n", p=128)).then_inc(h_, 16)
            for i, (ncol, srcw) in enumerate([
                    (512, gateAT), (512, upAT), (NHC * 16, downAT)]):
                e.wait_ge(sb.hnd("wk_cast"), b["wk_cast"] + i)
                h_, _ = sb.bump("wkkup", 16)
                e.dma_start(out=smw[:, 0:ncol].rearrange("p (c n) -> p c n", n=16),
                            in_=srcw.rearrange("(c p) n -> p c n", p=128)).then_inc(h_, 16)
            for c in range(NHC):
                for w in range(2):
                    i = 2 * c + w
                    wt = gateT if w == 0 else upT
                    for hf in range(2):
                        hi = 2 * i + hf
                        if hi >= 2:
                            e.wait_ge(sb.hnd("pan_cast"), b["pan_cast"] + hi - 1)
                        h_, _ = sb.bump(f"pnl{hi % 2}", 16)
                        e.dma_start(out=gp[hi % 2][:].rearrange("p (c n) -> p c n", c=32),
                                    in_=wt[:, 128 * c + 64 * hf:128 * c + 64 * (hf + 1)]
                                    .rearrange("(c p) n -> p c n", p=128)).then_inc(h_, 16)
                if c >= 2:
                    e.wait_ge(sb.hnd("wk_cast"), b["wk_cast"] + 3 + 2 * (c - 1))
                h_, _ = sb.bump(f"wkld{c % 2}", 16)
                e.dma_start(out=gBst[c % 2][:],
                            in_=gateBT[:, 128 * c:128 * (c + 1)]).then_inc(h_, 16)
                h_, _ = sb.bump(f"wkld{2 + c % 2}", 16)
                e.dma_start(out=uBst[c % 2][:],
                            in_=upBT[:, 128 * c:128 * (c + 1)]).then_inc(h_, 16)
                if c >= 1:
                    cc = c - 1
                    e.wait_ge(sb.hnd("hm"), b["hm"] + 4 * (cc + 1))
                    h_, _ = sb.bump(f"qst{cc % 2}", 16)
                    hv = h_sc.rearrange("(a b p) n -> p a b n", a=16, b=NHC)
                    e.dma_start(out=hv[:, :, cc, :],
                                in_=h_sb[cc % 2][:].rearrange("p (a r) -> p a r", a=16)).then_inc(h_, 16)
            cc = NHC - 1
            e.wait_ge(sb.hnd("hm"), b["hm"] + 4 * (cc + 1))
            h_, _ = sb.bump(f"qst{cc % 2}", 16)
            hv = h_sc.rearrange("(a b p) n -> p a b n", a=16, b=NHC)
            e.dma_start(out=hv[:, :, cc, :],
                        in_=h_sb[cc % 2][:].rearrange("p (a r) -> p a r", a=16)).then_inc(h_, 16)
            e.wait_ge(sb.hnd("qst0"), sb.val("qst0"))
            e.wait_ge(sb.hnd("qst1"), sb.val("qst1"))

        def f_pool(e):
            for i, (dsb, ncol) in enumerate([(gATb, 512), (uATb, 512),
                                             (dATb, NHC * 16)]):
                e.wait_ge(sb.hnd("wkkup"), b["wkkup"] + 16 * (i + 1))
                h_, _ = sb.bump("wk_cast")
                e.tensor_copy(out=dsb[:, 0:ncol], in_=smw[:, 0:ncol]).then_inc(h_, 1)
            for c in range(NHC):
                for w in range(2):
                    i = 2 * c + w
                    for hf in range(2):
                        hi = 2 * i + hf
                        e.wait_ge(sb.hnd(f"pnl{hi % 2}"), b[f"pnl{hi % 2}"] + 16 * (hi // 2 + 1))
                        if i >= 2:
                            e.wait_ge(sb.hnd("qmm"), b["qmm"] + 4 * (i - 1))
                        h_, _ = sb.bump("pan_cast")
                        dstv = gpb[i % 2][:, :].rearrange("p (c h n) -> p c h n",
                                                          h=2, n=64)
                        e.tensor_copy(out=dstv[:, :, hf, :],
                                      in_=gp[hi % 2][:].rearrange("p (c n) -> p c n",
                                                                  n=64)).then_inc(h_, 1)
                e.wait_ge(sb.hnd(f"wkld{c % 2}"), b[f"wkld{c % 2}"] + 16 * (c // 2 + 1))
                e.wait_ge(sb.hnd(f"wkld{2 + c % 2}"), b[f"wkld{2 + c % 2}"] + 16 * (c // 2 + 1))
                if c >= 2:
                    e.wait_ge(sb.hnd("qmm"), b["qmm"] + 8 * (c - 1))
                h_, _ = sb.bump("wk_cast")
                e.tensor_copy(out=gBb[c % 2][:], in_=gBst[c % 2][:]).then_inc(h_, 1)
                h_, _ = sb.bump("wk_cast")
                e.tensor_copy(out=uBb[c % 2][:], in_=uBst[c % 2][:]).then_inc(h_, 1)

        def f_pe(e):
            e.wait_ge(sb.hnd("kv_ld"), b["kv_ld"] + 16)
            e.wait_ge(sb.hnd("wk_cast"), b["wk_cast"] + 3)
            for wi, ATb in enumerate([gATb, uATb]):
                for rb in range(4):
                    nn = 4 * wi + rb
                    bb = 4 + nn % 2
                    if nn >= 2:
                        e.wait_ge(sb.hnd("yev"), b["yev"] + nn - 1)
                    for c in range(32):
                        inst = e.matmul(PS[bb][0:16, :],
                                        ATb[:, 16 * c:16 * (c + 1)],
                                        ln2TF[:, NROW * c + 512 * rb:NROW * c + 512 * (rb + 1)],
                                        start=(c == 0), stop=(c == 31))
                    h_, _ = sb.bump("ymm")
                    inst.then_inc(h_, 1)
            pendDA = None
            for c in range(NHC):
                for w in range(2):
                    i = 2 * c + w
                    e.wait_ge(sb.hnd("pan_cast"), b["pan_cast"] + 2 * (i + 1))
                    e.wait_ge(sb.hnd("wk_cast"), b["wk_cast"] + 3 + 2 * (c + 1))
                    Bt = gBb[c % 2] if w == 0 else uBb[c % 2]
                    Asb = gA_sb if w == 0 else uA_sb
                    for rb in range(4):
                        bb = (2 * w) + rb % 2
                        nn = 8 * c + 4 * w + rb
                        if c >= 1 or rb >= 2:
                            if w == 0:
                                e.wait_ge(sb.hnd("gsl"), b["gsl"] + 4 * c + rb - 1)
                            else:
                                e.wait_ge(sb.hnd("hm"), b["hm"] + 4 * c + rb - 1)
                        if w == 0 and c == 0 and rb < 2:
                            e.wait_ge(sb.hnd("yev"), b["yev"] + 8)
                        for cc in range(32):
                            inst = e.matmul(
                                PS[bb][:],
                                gpb[i % 2][:, 128 * cc:128 * (cc + 1)],
                                ln2TF[:, NROW * cc + 512 * rb:NROW * cc + 512 * (rb + 1)],
                                start=(cc == 0), stop=False)
                        e.wait_ge(sb.hnd("yev"), b["yev"] + (1 if w == 0 else 2) * 4)
                        inst = e.matmul(PS[bb][:], Bt[:, :],
                                        Asb[:, 512 * rb:512 * (rb + 1)],
                                        start=False, stop=True)
                        h_, _ = sb.bump("qmm")
                        inst.then_inc(h_, 1)
                    if w == 0 and pendDA is not None:
                        cd = pendDA
                        for rb in range(4):
                            e.wait_ge(sb.hnd("hm"), b["hm"] + 4 * cd + rb + 1)
                            inst = e.matmul(PS[4 + rb][0:16, :],
                                            dATb[:, 16 * cd:16 * (cd + 1)],
                                            h_sb[cd % 2][:, 512 * rb:512 * (rb + 1)],
                                            start=(cd == 0), stop=(cd == NHC - 1))
                            h_, _ = sb.bump("pvd")
                            inst.then_inc(h_, 1)
                pendDA = c
            cd = pendDA
            for rb in range(4):
                e.wait_ge(sb.hnd("hm"), b["hm"] + 4 * cd + rb + 1)
                inst = e.matmul(PS[4 + rb][0:16, :],
                                dATb[:, 16 * cd:16 * (cd + 1)],
                                h_sb[cd % 2][:, 512 * rb:512 * (rb + 1)],
                                start=(cd == 0), stop=(cd == NHC - 1))
                h_, _ = sb.bump("pvd")
                inst.then_inc(h_, 1)

        def f_act(e):
            for c in range(NHC):
                for rb in range(4):
                    nn = 8 * c + rb
                    e.wait_ge(sb.hnd("qmm"), b["qmm"] + 8 * c + rb + 1)
                    if c >= 2:
                        e.wait_ge(sb.hnd("hm"), b["hm"] + 4 * (c - 2) + rb + 1)
                    h_, _ = sb.bump("gsl")
                    e.activation(out=gsil[c % 2][:, 512 * rb:512 * (rb + 1)],
                                 in_=PS[rb % 2][:], func=AF.Silu).then_inc(h_, 1)

        def f_vec(e):
            for wi, Asb in enumerate([gA_sb, uA_sb]):
                for rb in range(4):
                    nn = 4 * wi + rb
                    bb = 4 + nn % 2
                    e.wait_ge(sb.hnd("ymm"), b["ymm"] + nn + 1)
                    h_, _ = sb.bump("yev")
                    e.tensor_copy(out=Asb[:, 512 * rb:512 * (rb + 1)],
                                  in_=PS[bb][0:16, :]).then_inc(h_, 1)
            for c in range(NHC):
                for rb in range(4):
                    e.wait_ge(sb.hnd("qmm"), b["qmm"] + 8 * c + 4 + rb + 1)
                    e.wait_ge(sb.hnd("gsl"), b["gsl"] + 4 * c + rb + 1)
                    if c >= 2:
                        e.wait_ge(sb.hnd(f"qst{c % 2}"), b[f"qst{c % 2}"] + 16 * (c // 2))
                        e.wait_ge(sb.hnd("pvd"), b["pvd"] + 4 * (c - 1))
                    h_, _ = sb.bump("hm")
                    e.tensor_tensor(out=h_sb[c % 2][:, 512 * rb:512 * (rb + 1)],
                                    in0=PS[2 + rb % 2][:],
                                    in1=gsil[c % 2][:, 512 * rb:512 * (rb + 1)],
                                    op=ALU.mult).then_inc(h_, 1)
            e.wait_ge(sb.hnd("pvd"), b["pvd"] + 4 * NHC)
            for rb in range(4):
                h_, _ = sb.bump("yev")
                e.tensor_copy(out=dA_sb[:, 512 * rb:512 * (rb + 1)],
                              in_=PS[4 + rb][0:16, :]).then_inc(h_, 1)

        blk.sync(f_sync)
        blk.gpsimd(f_pool)
        blk.tensor(f_pe)
        blk.scalar(f_act)
        blk.vector(f_vec)
        phase_join(["tensor", "vector", "scalar", "gpsimd"])
        g4.close()

    ph_g4()
    if NPH <= 6:
        g45.close()
        es.close()
        return nc

    def ph_g5():
        g5 = ExitStack()
        panelb = g5.enter_context(nc.sbuf_tensor("panelb", [128, NHC * 512], BF16))
        pn4 = [g5.enter_context(nc.sbuf_tensor(f"pn4{i}", [128, 512], F32)) for i in range(4)]
        ht = [g5.enter_context(nc.sbuf_tensor(f"ht{i}", [128, NHC * 128], BF16)) for i in range(2)]
        dstg = [g5.enter_context(nc.sbuf_tensor(f"dstg{i}", [128, 512], F32)) for i in range(2)]
        dBTb = g5.enter_context(nc.sbuf_tensor("dBTb", [16, DM], BF16))
        smw2 = g5.enter_context(nc.sbuf_tensor("g5smw", [16, DM], F32))

        b_pj = sb.val("pj")
        b = {k: sb.val(k) for k in ("wkkup", "wk_cast", "pan_cast", "qmm", "qev",
                                    "qst0", "qst1", "qtl0", "qtl1", "gsl",
                                    "wkld0", "wkld1", "wkld2", "wkld3", "dbc")}

        def f_sync(e):
            e.wait_ge(sb.hnd("pj"), b_pj)
            for q in range(4):
                e.wait_ge(sb.hnd(f"wkld{q}"), b[f"wkld{q}"])
            h_, _ = sb.bump("wkkup", 16)
            e.dma_start(out=smw2[:, :], in_=downBT_[:, :]).then_inc(h_, 16)
            for d in range(8):
                for c in range(NHC):
                    i = NHC * d + c
                    if i >= 4:
                        e.wait_ge(sb.hnd("pan_cast"), b["pan_cast"] + i - 3)
                    h_, _ = sb.bump(f"wkld{i % 4}", 16)
                    e.dma_start(out=pn4[i % 4][:],
                                in_=downT[128 * c:128 * (c + 1), 512 * d:512 * (d + 1)]).then_inc(h_, 16)
                for rs in range(16):
                    n = 16 * d + rs
                    if n >= 2:
                        e.wait_ge(sb.hnd("qmm"), b["qmm"] + n - 1)
                    h_, _ = sb.bump(f"qtl{n % 2}", 16)
                    e.dma_start(out=ht[n % 2][:].rearrange("p (c n) -> p c n", c=NHC),
                                in_=h_sc[NHC * 128 * rs:NHC * 128 * (rs + 1), :]
                                .rearrange("(c p) n -> p c n", p=128)).then_inc(h_, 16)

        def f_pool(e):
            e.wait_ge(sb.hnd("wkkup"), b["wkkup"] + 16)
            h_, _ = sb.bump("wk_cast")
            e.tensor_copy(out=dBTb[:, :], in_=smw2[:, :]).then_inc(h_, 1)
            for d in range(8):
                for c in range(NHC):
                    i = NHC * d + c
                    e.wait_ge(sb.hnd(f"wkld{i % 4}"), b[f"wkld{i % 4}"] + 16 * (i // 4 + 1))
                    if d >= 1:
                        e.wait_ge(sb.hnd("gsl"), b["gsl"] + 11 * (d - 1) + min(c // 8 + 1, 11))
                    h_, _ = sb.bump("pan_cast")
                    e.tensor_copy(out=panelb[:, 512 * c:512 * (c + 1)],
                                  in_=pn4[i % 4][:]).then_inc(h_, 1)
                h_, _ = sb.bump("dbc")
                e.drain().then_inc(h_, 1)

        def f_pe(e):
            e.wait_ge(sb.hnd("wk_cast"), b["wk_cast"] + 1)
            for d in range(8):
                for rs in range(16):
                    n = 16 * d + rs
                    bb = n % 2
                    e.wait_ge(sb.hnd(f"qtl{n % 2}"), b[f"qtl{n % 2}"] + 16 * (n // 2 + 1))
                    if n >= 2:
                        e.wait_ge(sb.hnd("qev"), b["qev"] + n - 1)
                    if rs == 1:
                        e.wait_ge(sb.hnd("dbc"), b["dbc"] + d + 1)
                    for c in range(NHC):
                        if rs == 0:
                            e.wait_ge(sb.hnd("pan_cast"), b["pan_cast"] + NHC * d + c + 1)
                        inst = e.matmul(PS[bb][:],
                                        ht[n % 2][:, 128 * c:128 * (c + 1)],
                                        panelb[:, 512 * c:512 * (c + 1)],
                                        start=(c == 0), stop=False)
                        if rs == 15 and (c % 8 == 7 or c == NHC - 1):
                            h_, _ = sb.bump("gsl")
                            inst.then_inc(h_, 1)
                    inst = e.matmul(PS[bb][:], dA_sb[:, 128 * rs:128 * (rs + 1)],
                                    dBTb[:, 512 * d:512 * (d + 1)],
                                    start=False, stop=True)
                    h_, _ = sb.bump("qmm")
                    inst.then_inc(h_, 1)

        def f_vec(e):
            for d in range(8):
                for rs in range(16):
                    n = 16 * d + rs
                    e.wait_ge(sb.hnd("qmm"), b["qmm"] + n + 1)
                    if n >= 2:
                        e.wait_ge(sb.hnd(f"qst{n % 2}"), b[f"qst{n % 2}"] + 16 * (n // 2))
                    h_, _ = sb.bump("qev")
                    e.tensor_copy(out=dstg[n % 2][:], in_=PS[n % 2][:]).then_inc(h_, 1)

        def f_store(e):
            for d in range(8):
                for rs in range(16):
                    n = 16 * d + rs
                    e.wait_ge(sb.hnd("qev"), b["qev"] + n + 1)
                    h_, _ = sb.bump(f"qst{n % 2}", 16)
                    e.dma_start(out=d_sc[128 * rs:128 * (rs + 1), 512 * d:512 * (d + 1)],
                                in_=dstg[n % 2][:]).then_inc(h_, 16)
            e.wait_ge(sb.hnd("qst0"), sb.val("qst0"))
            e.wait_ge(sb.hnd("qst1"), sb.val("qst1"))

        blk.sync(f_sync)
        blk.gpsimd(f_pool)
        blk.tensor(f_pe)
        blk.vector(f_vec)
        blk.scalar(f_store)
        phase_join(["tensor", "vector", "gpsimd", "scalar"])
        g5.close()

    ph_g5()
    g45.close()
    if NPH <= 7:
        es.close()
        return nc

    def ph_p7():
        p7 = ExitStack()
        dr = [p7.enter_context(nc.sbuf_tensor(f"p7d{i}", [128, DM], F32)) for i in range(2)]
        x1r = [p7.enter_context(nc.sbuf_tensor(f"p7x{i}", [128, DM], F32)) for i in range(2)]
        t2b = [p7.enter_context(nc.sbuf_tensor(f"p7t{i}", [128, DM], F32)) for i in range(2)]
        outb = [p7.enter_context(nc.sbuf_tensor(f"p7o{i}", [128, DM], F32)) for i in range(2)]
        sqd = p7.enter_context(nc.sbuf_tensor("p7sqd", [128, DM], BF16))

        NJ = 16
        b_pj = sb.val("pj")
        b = {k: sb.val(k) for k in ("xld0", "xld1", "qtl0", "qtl1", "sq", "var",
                                    "rstd", "negmr", "apply", "x1d",
                                    "ost0", "ost1", "qst0", "qst1")}

        def emit_out_j(e, j):
            p = j % 2
            e.wait_ge(sb.hnd("x1d"), b["x1d"] + j + 1)
            h_, _ = sb.bump(f"ost{p}", 16)
            e.dma_start(out=out[128 * j:128 * (j + 1), :],
                        in_=outb[p][:]).then_inc(h_, 16)

        def f_sync(e):
            e.wait_ge(sb.hnd("pj"), b_pj)
            e.wait_ge(sb.hnd("qst0"), b["qst0"])
            e.wait_ge(sb.hnd("qst1"), b["qst1"])
            for j in range(NJ):
                e.wait_ge(sb.hnd("apply"), b["apply"] + max(j - 1, 0))
                h_, _ = sb.bump(f"xld{j % 2}", 16)
                e.dma_start(out=dr[j % 2][:],
                            in_=d_sc[128 * j:128 * (j + 1), :]).then_inc(h_, 16)
                e.wait_ge(sb.hnd("x1d"), b["x1d"] + max(j - 1, 0))
                h_, _ = sb.bump(f"qtl{j % 2}", 16)
                e.dma_start(out=x1r[j % 2][:],
                            in_=x1_sc[128 * j:128 * (j + 1), :]).then_inc(h_, 16)
                if j >= 1:
                    emit_out_j(e, j - 1)
            emit_out_j(e, NJ - 1)

        def f_vec(e):
            base = {"xld": [b["xld0"], b["xld1"]], "sq": b["sq"], "var": b["var"],
                    "rstd": b["rstd"], "negmr": b["negmr"]}
            for j in range(NJ):
                p = j % 2
                stats_vec_j(e, j, p, dr[p][:], base)
                e.wait_ge(sb.hnd("apply"), b["apply"] + j + 1)
                e.wait_ge(sb.hnd(f"qtl{p}"), b[f"qtl{p}"] + 16 * (j // 2 + 1))
                if j >= 2:
                    e.wait_ge(sb.hnd(f"ost{p}"), b[f"ost{p}"] + 16 * (j // 2))
                e.drain()
                h_, _ = sb.bump("x1d")
                e.tensor_tensor(out=outb[p][:], in0=t2b[p][:], in1=x1r[p][:],
                                op=ALU.add).then_inc(h_, 1)

        def f_act(e):
            base = {"xld": [b["xld0"], b["xld1"]], "sq": b["sq"], "var": b["var"],
                    "rstd": b["rstd"], "negmr": b["negmr"]}
            sqd_ref[0] = sqd
            for j in range(NJ):
                p = j % 2
                ew = ("x1d", b["x1d"] + j - 1) if j >= 2 else None
                stats_act_j(e, j, p, dr[p][:], t2b[p][:], base,
                            extra_tr_wait=None, extra_wait=ew)

        blk.sync(f_sync)
        blk.vector(f_vec)
        blk.scalar(f_act)

        def f_final(e):
            e.wait_ge(sb.hnd("ost0"), sb.val("ost0"))
            e.wait_ge(sb.hnd("ost1"), sb.val("ost1"))
        blk.sync(f_final)
        p7.close()

    ph_p7()

    dbg_map = {"ao": ao_sc, "x1": x1_sc, "ln2": ln2_sc, "h": h_sc, "d": d_sc}
    todump = [nm for nm in _DBG.get("dumps", []) if nm in dbg_map]
    if todump:
        def f_dbg(e):
            e.wait_ge(sb.hnd("pj"), sb.val("pj"))
            e.wait_ge(sb.hnd("ost0"), sb.val("ost0"))
            e.wait_ge(sb.hnd("ost1"), sb.val("ost1"))
            for nm in todump:
                h_, _ = sb.bump("dbg", 16)
                sc = dbg_map[nm]
                e.dma_start(out=dbg_outs[nm][:, :].rearrange("(c p) n -> p c n", p=128),
                            in_=sc[:, :].rearrange("(c p) n -> p c n", p=128)).then_inc(h_, 16)
            e.wait_ge(sb.hnd("dbg"), sb.val("dbg"))
        blk.scalar(f_dbg)

    es.close()
    return nc


_PERM = (np.arange(8)[:, None] + 8 * np.arange(512)[None, :]).reshape(-1)


def _build_masks(starts):
    m = np.zeros((NT, 8, 128, TQ), np.float32)
    for t in range(NT):
        n = SWEEPS[t]
        r0 = starts[t]
        for ci in range(8):
            c = n - 8 + ci
            k = 128 * c + np.arange(128)[:, None]
            q = r0 + np.arange(TQ)[None, :]
            m[t, ci] = (k <= q)
    return m.reshape(NT * 8 * 128, TQ).astype(ml_dtypes.bfloat16)


def kernel(**inp):
    x = np.ascontiguousarray(np.asarray(inp["L0_attn_hidden_states"], np.float32))
    B = x.shape[0]

    def T(a):
        return np.ascontiguousarray(np.asarray(a, np.float32).T)

    Wkd = np.asarray(inp["L0_attn_Wk_down"], np.float32)
    Wvd = np.asarray(inp["L0_attn_Wv_down"], np.float32)
    wkv = np.empty((512, DM), np.float32)
    for h in range(8):
        wkv[64 * h:64 * h + 32] = Wkd[32 * h:32 * h + 32]
        wkv[64 * h + 32:64 * h + 64] = Wvd[32 * h:32 * h + 32]

    weights = {
        "WqT": T(inp["L0_attn_Wq"]),
        "WkvT": np.ascontiguousarray(wkv.T),
        "WkupT": T(inp["L0_attn_Wk_up"]),
        "WvupT": T(inp["L0_attn_Wv_up"]),
        "WoT": T(inp["L0_attn_Wo"]),
        "gateT": T(inp["L0_ffn_gate_w"]),
        "upT": T(inp["L0_ffn_up_w"]),
        "downT": T(inp["L0_ffn_down_w"]),
        "gateAT": T(inp["L0_ffn_gate_a"]),
        "gateBT": T(inp["L0_ffn_gate_b"]),
        "upAT": T(inp["L0_ffn_up_a"]),
        "upBT": T(inp["L0_ffn_up_b"]),
        "downAT": T(inp["L0_ffn_down_a"]),
        "downBT": T(inp["L0_ffn_down_b"]),
    }
    mask_p = _build_masks(ROWS_P)
    mask_q = _build_masks(ROWS_Q)

    nc = build_program()
    in_maps = []
    for c in range(8):
        b = c // 2
        starts = ROWS_P if c % 2 == 0 else ROWS_Q
        xo = np.concatenate([x[b, s:s + TQ] for s in starts], axis=0)
        m = {"xb_perm": np.ascontiguousarray(x[b][_PERM]),
             "x_own": np.ascontiguousarray(xo),
             "masks": mask_p if c % 2 == 0 else mask_q}
        m.update(weights)
        in_maps.append(m)

    res = run_bass_kernel_spmd(nc, in_maps, core_ids=list(range(8)))
    outp = np.empty((B, S, DM), np.float32)
    for c in range(8):
        b = c // 2
        starts = ROWS_P if c % 2 == 0 else ROWS_Q
        oc = np.asarray(res.results[c]["out"], np.float32)
        for t, s in enumerate(starts):
            outp[b, s:s + TQ] = oc[TQ * t:TQ * (t + 1)]
    return outp


# version 05
# speedup vs baseline: 1.1323x; 1.1242x over previous
import numpy as np
import ml_dtypes
from contextlib import ExitStack

import concourse.bass as bass
import concourse.mybir as mybir
from concourse.bass_utils import run_bass_kernel_spmd

F32 = mybir.dt.float32
BF16 = mybir.dt.bfloat16
AF = mybir.ActivationFunctionType
ALU = mybir.AluOpType
AX = mybir.AxisListType

S = 4096
DM = 4096
DH = 11008
NHC = DH // 128
RANK = 16
NT = 4
TQ = 512
SWEEPS = [8, 16, 24, 32]
ROWS_P = [0, 1536, 2048, 3584]
ROWS_Q = [512, 1024, 2560, 3072]
NROW = NT * TQ
EPS = 1e-5
EXPB = 18.0
QSCALE = float(1.0 / np.sqrt(128.0))
LN64 = float(np.log(64.0))
EPS_RAW = float(4096.0 * EPS)

_DBG = {}


class SemBank:
    def __init__(self, nc, es):
        self.nc, self.es = nc, es
        self.h = {}
        self.v = {}

    def new(self, name):
        self.h[name] = self.es.enter_context(self.nc.semaphore(name))
        self.v[name] = 0

    def bump(self, name, k=1):
        self.v[name] += k
        return self.h[name], self.v[name]

    def val(self, name):
        return self.v[name]

    def hnd(self, name):
        return self.h[name]


class Banks:
    def __init__(self, sb, n=8):
        self.sb = sb
        self.n = n
        for b in range(n):
            sb.new(f"bk{b}w")
            sb.new(f"bk{b}f")

    def acquire(self, eng, b):
        need = self.sb.val(f"bk{b}f")
        prev_w = self.sb.val(f"bk{b}w")
        if prev_w > 0:
            eng.wait_ge(self.sb.hnd(f"bk{b}f"), prev_w)

    def produced(self, inst, b):
        h, _ = self.sb.bump(f"bk{b}w")
        inst.then_inc(h, 1)
        return self.sb.val(f"bk{b}w")

    def consume_wait(self, eng, b, w=None):
        eng.wait_ge(self.sb.hnd(f"bk{b}w"), w if w is not None else self.sb.val(f"bk{b}w"))

    def freed(self, inst, b):
        h, _ = self.sb.bump(f"bk{b}f")
        inst.then_inc(h, 1)


def build_program():
    nc = bass.Bass("TRN2", target_bir_lowering=False)

    din = {}
    def dram_in(name, shape, dt=F32):
        din[name] = nc.dram_tensor(name, shape, dt, kind="ExternalInput").ap()
        return din[name]

    xb_perm = dram_in("xb_perm", [S, DM])
    x_own = dram_in("x_own", [NROW, DM])
    masks = dram_in("masks", [NT * 8 * 128, TQ], BF16)
    WqT = dram_in("WqT", [DM, DM])
    WkvT = dram_in("WkvT", [DM, 512])
    WkupT = dram_in("WkupT", [256, 1024])
    WvupT = dram_in("WvupT", [256, 1024])
    WoT = dram_in("WoT", [DM, DM])
    gateT = dram_in("gateT", [DM, DH])
    upT = dram_in("upT", [DM, DH])
    downT = dram_in("downT", [DH, DM])
    gateAT = dram_in("gateAT", [DM, RANK])
    gateBT = dram_in("gateBT", [RANK, DH])
    upAT = dram_in("upAT", [DM, RANK])
    upBT = dram_in("upBT", [RANK, DH])
    downAT = dram_in("downAT", [DH, RANK])
    downBT_ = dram_in("downBT", [RANK, DM])

    out = nc.dram_tensor("out", [NROW, DM], F32, kind="ExternalOutput").ap()

    q_sc = nc.dram_tensor("q_sc", [NT * 32 * 128, TQ], BF16, kind="Internal").ap()
    o_sc = nc.dram_tensor("o_sc", [32 * 128, NROW], BF16, kind="Internal").ap()
    kv_sc = nc.dram_tensor("kv_sc", [512, S], BF16, kind="Internal").ap()
    ln2_sc = nc.dram_tensor("ln2_sc", [32 * 128, NROW], BF16, kind="Internal").ap()
    ao_sc = nc.dram_tensor("ao_sc", [NROW, DM], F32, kind="Internal").ap()
    x1_sc = nc.dram_tensor("x1_sc", [NROW, DM], F32, kind="Internal").ap()
    h_sc = nc.dram_tensor("h_sc", [16 * NHC * 128, 128], BF16, kind="Internal").ap()
    d_sc = nc.dram_tensor("d_sc", [NROW, DM], F32, kind="Internal").ap()

    dbg_outs = {}
    for nm in _DBG.get("dumps", []):
        shp = {"kv": [512, S], "lnT": [128, 32 * NROW], "q": [NT * 32 * 128, TQ],
               "o": [32 * 128, NROW], "ao": [NROW, DM], "x1": [NROW, DM],
               "ln2": [32 * 128, NROW], "h": [16 * NHC * 128, 128],
               "d": [NROW, DM]}[nm]
        dt = BF16 if nm in ("kv", "lnT", "q", "o", "ln2", "h") else F32
        dbg_outs[nm] = nc.dram_tensor("dbg_" + nm, shp, dt, kind="ExternalOutput").ap()

    es = ExitStack()
    sb = SemBank(nc, es)
    for name in ["xld0", "xld1", "wkld0", "wkld1", "wkld2", "wkld3", "wkkup",
                 "pnl0", "pnl1", "qst0", "qst1", "qtl0", "qtl1", "mld0", "mld1",
                 "tdone", "mdone", "selc", "lr1", "bcd", "ost0", "ost1", "pvd",
                 "upe", "oscd", "lse", "lrc",
                 "consts", "sq", "var", "rstd", "negmr", "apply",
                 "tr", "trev", "ymm", "yev", "scr", "wk_cast",
                 "kvsp", "pan_cast", "qmm", "qev", "kv_ld", "pexp", "pmask",
                 "pj", "sq2", "var2", "rstd2", "negmr2", "ap2", "x1d",
                 "gsl", "hm", "dbc", "dbg",
                 "sqB", "varB", "rstdB", "negmrB", "applyB",
                 "sq2B", "var2B", "rstd2B", "negmr2B", "ap2B"]:
        sb.new(name)
    bk = Banks(sb, 8)

    def phase_join(engs):
        for en in engs:
            def f(e):
                h_, _ = sb.bump("pj")
                e.drain().then_inc(h_, 1)
            getattr(blk, en)(f)

    PS = [es.enter_context(nc.psum_tensor(f"ps{b}", [128, 512], F32)) for b in range(8)]
    ident = es.enter_context(nc.sbuf_tensor("ident", [128, 128], BF16))
    onesb = es.enter_context(nc.sbuf_tensor("onesb", [128, 128], BF16))
    ones1 = es.enter_context(nc.sbuf_tensor("ones1", [1, 128], F32))
    inds = es.enter_context(nc.sbuf_tensor("inds", [128, 16], BF16))
    cst = es.enter_context(nc.sbuf_tensor("cst", [128, 8], F32))
    stt = es.enter_context(nc.sbuf_tensor("stt", [128, 32], F32))
    kv_es = ExitStack()
    kupb = kv_es.enter_context(nc.sbuf_tensor("kupb", [128, 2 * 1024], BF16))
    vupb = kv_es.enter_context(nc.sbuf_tensor("vupb", [128, 2 * 1024], BF16))

    blk = es.enter_context(nc.Block())

    NPH = _DBG.get("phases", 99)

    def ph_consts():
        def g(e):
            e.memset(onesb[:], 1.0)
            e.memset(inds[:], 0.0)
            e.drain()
            for h in range(4):
                e.memset(inds[:, 5 * h:5 * h + 1], 1.0)
            e.memset(ones1[:], 1.0)
            e.memset(cst[:, 0:1], EPS_RAW)
            e.memset(cst[:, 1:2], LN64)
            e.memset(cst[:, 2:3], -EXPB)
            e.memset(cst[:, 3:4], -0.5)
            e.memset(cst[:, 4:5], -1.0)
            e.drain()
            h_, v = sb.bump("consts")
            e.affine_select(out=ident[:], in_=onesb[:], pattern=[[1, 128]],
                            compare_op=ALU.is_equal, fill=0.0, base=0,
                            channel_multiplier=-1).then_inc(h_, 1)
        blk.gpsimd(g)
    ph_consts()
    CONSTS_DONE = sb.val("consts")


    def _ps(name, p):
        return name if p == 0 else {"sq": "sqB", "var": "varB", "rstd": "rstdB",
                                    "negmr": "negmrB", "apply": "applyB",
                                    "sq2": "sq2B", "var2": "var2B",
                                    "rstd2": "rstd2B", "negmr2": "negmr2B",
                                    "ap2": "ap2B"}[name]

    def stats_vec_j(e, j, p, xrow_ap, base):
        jp = j // 2 + 1
        o = 16 * p
        if j >= 2:
            e.drain()
        e.wait_ge(sb.hnd(f"xld{p}"), base["xld"][p] + 16 * jp)
        e.tensor_reduce(out=stt[:, o:o + 1], in_=xrow_ap,
                        axis=AX.X, op=ALU.add)
        e.drain()
        e.wait_ge(sb.hnd(_ps("sq", p)), base[_ps("sq", p)] + jp)
        if j >= 2:
            e.wait_ge(sb.hnd(_ps("rstd", p)), base[_ps("rstd", p)] + jp - 1)
        e.tensor_scalar(out=stt[:, o + 2:o + 3], in0=stt[:, o:o + 1],
                        scalar1=stt[:, o:o + 1], scalar2=float(1.0 / DM),
                        op0=ALU.mult, op1=ALU.mult)
        e.drain()
        h_, _ = sb.bump(_ps("var", p))
        e.tensor_tensor(out=stt[:, o + 2:o + 3], in0=stt[:, o + 1:o + 2],
                        in1=stt[:, o + 2:o + 3], op=ALU.subtract).then_inc(h_, 1)
        e.wait_ge(sb.hnd(_ps("rstd", p)), base[_ps("rstd", p)] + jp)
        e.drain()
        h_, _ = sb.bump(_ps("negmr", p))
        e.tensor_scalar(out=stt[:, o + 4:o + 5], in0=stt[:, o:o + 1],
                        scalar1=stt[:, o + 3:o + 4], scalar2=float(-1.0 / DM),
                        op0=ALU.mult, op1=ALU.mult).then_inc(h_, 1)

    def stats_act_j(e, j, p, xrow_ap, out_ap, base, extra_tr_wait=None,
                    extra_wait=None):
        jp = j // 2 + 1
        o = 16 * p
        if j >= 2:
            e.drain()
        e.wait_ge(sb.hnd(f"xld{p}"), base["xld"][p] + 16 * jp)
        if j >= 2:
            e.wait_ge(sb.hnd(_ps("var", p)), base[_ps("var", p)] + jp - 1)
        h_, _ = sb.bump(_ps("sq", p))
        e.activation(out=sqd_ref[0][:], in_=xrow_ap, func=AF.Square,
                     accum_out=stt[:, o + 1:o + 2]).then_inc(h_, 1)
        e.wait_ge(sb.hnd(_ps("var", p)), base[_ps("var", p)] + jp)
        e.activation(out=stt[:, o + 2:o + 3], in_=stt[:, o + 2:o + 3], func=AF.Ln,
                     bias=cst[:, 0:1], scale=1.0)
        e.drain()
        h_, _ = sb.bump(_ps("rstd", p))
        e.activation(out=stt[:, o + 3:o + 4], in_=stt[:, o + 2:o + 3], func=AF.Exp,
                     bias=cst[:, 1:2], scale=cst[:, 3:4]).then_inc(h_, 1)
        e.wait_ge(sb.hnd(_ps("negmr", p)), base[_ps("negmr", p)] + jp)
        e.drain()
        if extra_tr_wait is not None:
            e.wait_ge(sb.hnd("tr"), extra_tr_wait)
        if extra_wait is not None:
            e.wait_ge(sb.hnd(extra_wait[0]), extra_wait[1])
        h_, _ = sb.bump(_ps("apply", p))
        e.activation(out=out_ap, in_=xrow_ap, func=AF.Identity,
                     bias=stt[:, o + 4:o + 5], scale=stt[:, o + 3:o + 4]).then_inc(h_, 1)

    sqd_ref = [None]

    def ph_p1():
        p1 = ExitStack()
        xrow = [p1.enter_context(nc.sbuf_tensor(f"xrow{i}", [128, DM], F32)) for i in range(2)]
        lnb = [p1.enter_context(nc.sbuf_tensor(f"lnb{i}", [128, DM], BF16)) for i in range(2)]
        sqd = p1.enter_context(nc.sbuf_tensor("sqd", [128, DM], BF16))
        lnT = p1.enter_context(nc.sbuf_tensor("lnT", [128, 32 * 512], BF16))
        ysb = p1.enter_context(nc.sbuf_tensor("ysb", [128, 4 * 512], BF16))
        kcmpT = p1.enter_context(nc.sbuf_tensor("kcmpT", [128, 2 * S], BF16))
        vcmpT = p1.enter_context(nc.sbuf_tensor("vcmpT", [128, 2 * S], BF16))
        wkst = [p1.enter_context(nc.sbuf_tensor(f"wkst{i}", [128, 512], F32)) for i in range(4)]
        wkvb = p1.enter_context(nc.sbuf_tensor("wkvb", [128, 32 * 512], BF16))
        kupst = p1.enter_context(nc.sbuf_tensor("kupst", [128, 4096], F32))

        NJ = 32

        def f_sync(e):
            for i in range(32):
                if i >= 4:
                    e.wait_ge(sb.hnd("wk_cast"), i - 3)
                h_, _ = sb.bump(f"wkld{i % 4}", 16)
                e.dma_start(out=wkst[i % 4][:],
                            in_=WkvT[128 * i:128 * (i + 1), :]).then_inc(h_, 16)
            h_, _ = sb.bump("wkkup", 16)
            e.dma_start(out=kupst[:, 0:2048].rearrange("p (c n) -> p c n", c=2),
                        in_=WkupT.rearrange("(c p) n -> p c n", p=128)).then_inc(h_, 16)
            h_, _ = sb.bump("wkkup", 16)
            e.dma_start(out=kupst[:, 2048:4096].rearrange("p (c n) -> p c n", c=2),
                        in_=WvupT.rearrange("(c p) n -> p c n", p=128)).then_inc(h_, 16)
            for j in range(NJ):
                if j >= 2:
                    e.wait_ge(sb.hnd(_ps("apply", j % 2)), j // 2)
                h_, _ = sb.bump(f"xld{j % 2}", 16)
                e.dma_start(out=xrow[j % 2][:],
                            in_=xb_perm[128 * j:128 * (j + 1), :]).then_inc(h_, 16)

        def f_pool(e):
            for i in range(32):
                e.wait_ge(sb.hnd(f"wkld{i % 4}"), 16 * (i // 4 + 1))
                h_, _ = sb.bump("wk_cast")
                e.tensor_copy(out=wkvb[:, 512 * i:512 * (i + 1)],
                              in_=wkst[i % 4][:]).then_inc(h_, 1)
            e.wait_ge(sb.hnd("wkkup"), 32)
            e.drain()
            h_, _ = sb.bump("wk_cast")
            e.tensor_copy(out=kupb[:], in_=kupst[:, 0:2048]).then_inc(h_, 1)
            h_, _ = sb.bump("wk_cast")
            e.tensor_copy(out=vupb[:], in_=kupst[:, 2048:4096]).then_inc(h_, 1)

        def f_vec(e):
            base = {"xld": [0, 0], "sq": 0, "sqB": 0, "var": 0, "varB": 0,
                    "rstd": 0, "rstdB": 0, "negmr": 0, "negmrB": 0}
            for j in range(NJ):
                p = j % 2
                stats_vec_j(e, j, p, xrow[p][:], base)
                jj = j % 4
                if jj == 0 and j > 0:
                    e.wait_ge(sb.hnd("ymm"), 4 * (j // 4))
                for fb in range(4):
                    e.wait_ge(sb.hnd("tr"), 4 * j + fb + 1)
                    b = fb % 2
                    src = PS[b][:].bitcast(BF16)[:, 0:1024]
                    dst = lnT[:, :].rearrange("p (c r) -> p c r", r=512)
                    h_, _ = sb.bump("trev")
                    e.tensor_copy(out=dst[:, 8 * fb:8 * fb + 8, 128 * jj:128 * jj + 128],
                                  in_=src.rearrange("p (c r) -> p c r", r=128)).then_inc(h_, 1)
                if jj == 3:
                    g = j // 4
                    e.drain()
                    for kv in range(4):
                        e.wait_ge(sb.hnd("ymm"), 4 * g + kv + 1)
                        h_, _ = sb.bump("yev")
                        e.tensor_copy(out=ysb[:, 512 * kv:512 * (kv + 1)],
                                      in_=PS[4 + kv % 2][:]).then_inc(h_, 1)
                    e.drain()
                    dstT = kcmpT if g < 4 else vcmpT
                    rr = g % 4
                    pbase = 64 * (rr % 2)
                    fbase = S * (rr // 2)
                    for h in range(8):
                        sp = 64 * (h % 2)
                        sf = 512 * (h // 2)
                        h_, _ = sb.bump("scr")
                        e.tensor_copy(
                            out=dstT[pbase:pbase + 64, fbase + 512 * h:fbase + 512 * h + 512],
                            in_=ysb[sp:sp + 64, sf:sf + 512]).then_inc(h_, 1)

        def f_act(e):
            base = {"xld": [0, 0], "sq": 0, "sqB": 0, "var": 0, "varB": 0,
                    "rstd": 0, "rstdB": 0, "negmr": 0, "negmrB": 0}
            sqd_ref[0] = sqd
            for j in range(NJ):
                p = j % 2
                stats_act_j(e, j, p, xrow[p][:], lnb[p][:], base,
                            extra_tr_wait=(4 * (j - 1)) if j >= 2 else None)

        def f_pe(e):
            e.wait_ge(sb.hnd("consts"), CONSTS_DONE)
            for j in range(NJ):
                p = j % 2
                e.wait_ge(sb.hnd(_ps("apply", p)), j // 2 + 1)
                for fb in range(4):
                    b = fb % 2
                    nfill = 4 * j + fb
                    if nfill >= 2:
                        e.wait_ge(sb.hnd("trev"), nfill - 1)
                    for t8 in range(8):
                        c = 8 * fb + t8
                        inst = e.transpose(
                            PS[b][:].bitcast(BF16)[:, 128 * t8:128 * (t8 + 1)],
                            lnb[p][:, 128 * c:128 * (c + 1)], ident[:])
                    h_, _ = sb.bump("tr")
                    inst.then_inc(h_, 1)
                if j % 4 == 3:
                    g = j // 4
                    if g == 0:
                        e.wait_ge(sb.hnd("wk_cast"), 32)
                    e.wait_ge(sb.hnd("trev"), 16 * (g + 1))
                    for kv in range(4):
                        bb = 4 + kv % 2
                        nuse = 4 * g + kv
                        if nuse >= 2:
                            e.wait_ge(sb.hnd("yev"), nuse - 1)
                        for c in range(32):
                            inst = e.matmul(PS[bb][:],
                                            wkvb[:, 512 * c + 128 * kv:512 * c + 128 * (kv + 1)],
                                            lnT[:, 512 * c:512 * (c + 1)],
                                            start=(c == 0), stop=(c == 31))
                        h_, _ = sb.bump("ymm")
                        inst.then_inc(h_, 1)

        def f_store(e):
            e.wait_ge(sb.hnd("scr"), 64)
            h_, _ = sb.bump("kvsp", 16)
            e.dma_start(out=kv_sc[0:256, :].rearrange("(c p) n -> p c n", p=128),
                        in_=kcmpT[:].rearrange("p (c n) -> p c n", c=2)).then_inc(h_, 16)
            h_, _ = sb.bump("kvsp", 16)
            e.dma_start(out=kv_sc[256:512, :].rearrange("(c p) n -> p c n", p=128),
                        in_=vcmpT[:].rearrange("p (c n) -> p c n", c=2)).then_inc(h_, 16)
            if "kv" in dbg_outs:
                h_, _ = sb.bump("dbg", 32)
                e.dma_start(out=dbg_outs["kv"][0:256, :].rearrange("(c p) n -> p c n", p=128),
                            in_=kcmpT[:].rearrange("p (c n) -> p c n", c=2)).then_inc(h_, 16)
                e.dma_start(out=dbg_outs["kv"][256:512, :].rearrange("(c p) n -> p c n", p=128),
                            in_=vcmpT[:].rearrange("p (c n) -> p c n", c=2)).then_inc(h_, 16)
            e.wait_ge(sb.hnd("kvsp"), sb.val("kvsp"))
            if _DBG.get("dumps"):
                e.wait_ge(sb.hnd("dbg"), sb.val("dbg"))

        blk.sync(f_sync)
        blk.gpsimd(f_pool)
        blk.vector(f_vec)
        blk.scalar(f_act)
        blk.tensor(f_pe)
        blk.scalar(f_store)
        phase_join(["tensor", "vector", "scalar", "gpsimd"])
        p1.close()

    ph_p1()
    if NPH <= 1:
        es.close()
        return nc

    def ph_p2():
        p2o = ExitStack()
        lnTF = p2o.enter_context(nc.sbuf_tensor("lnTF", [128, 32 * NROW], BF16))

        NJ = 16
        b_xld = [sb.val("xld0"), sb.val("xld1")]
        bst = {k: sb.val(k) for k in ("sq", "sqB", "var", "varB", "rstd",
                                      "rstdB", "negmr", "negmrB", "apply",
                                      "applyB")}
        b_tr = sb.val("tr"); b_trev = sb.val("trev")

        p2 = ExitStack()
        xrow = [p2.enter_context(nc.sbuf_tensor(f"p2xr{i}", [128, DM], F32)) for i in range(2)]
        lnb = [p2.enter_context(nc.sbuf_tensor(f"p2ln{i}", [128, DM], BF16)) for i in range(2)]
        sqd = p2.enter_context(nc.sbuf_tensor("p2sqd", [128, DM], BF16))

        b_pj = sb.val("pj")

        def fa_sync(e):
            e.wait_ge(sb.hnd("pj"), b_pj)
            for j in range(NJ):
                p = j % 2
                e.wait_ge(sb.hnd(_ps("apply", p)),
                          bst[_ps("apply", p)] + max(j // 2, 0) if j >= 2
                          else bst[_ps("apply", p)])
                h_, _ = sb.bump(f"xld{j % 2}", 16)
                e.dma_start(out=xrow[j % 2][:],
                            in_=x_own[128 * j:128 * (j + 1), :]).then_inc(h_, 16)

        def fa_vec(e):
            base = dict(bst); base["xld"] = b_xld
            for j in range(NJ):
                p = j % 2
                stats_vec_j(e, j, p, xrow[p][:], base)
                for fb in range(4):
                    e.wait_ge(sb.hnd("tr"), b_tr + 4 * j + fb + 1)
                    b = fb % 2
                    src = PS[b][:].bitcast(BF16)[:, 0:1024]
                    dst = lnTF[:, :].rearrange("p (c r) -> p c r", r=NROW)
                    h_, _ = sb.bump("trev")
                    e.tensor_copy(out=dst[:, 8 * fb:8 * fb + 8, 128 * j:128 * j + 128],
                                  in_=src.rearrange("p (c r) -> p c r", r=128)).then_inc(h_, 1)

        def fa_act(e):
            base = dict(bst); base["xld"] = b_xld
            sqd_ref[0] = sqd
            for j in range(NJ):
                p = j % 2
                stats_act_j(e, j, p, xrow[p][:], lnb[p][:], base,
                            extra_tr_wait=(b_tr + 4 * (j - 1)) if j >= 2 else None)

        def fa_pe(e):
            for j in range(NJ):
                p = j % 2
                e.wait_ge(sb.hnd(_ps("apply", p)), bst[_ps("apply", p)] + j // 2 + 1)
                for fb in range(4):
                    b = fb % 2
                    nfill = 4 * j + fb
                    if nfill >= 2:
                        e.wait_ge(sb.hnd("trev"), b_trev + nfill - 1)
                    for t8 in range(8):
                        c = 8 * fb + t8
                        inst = e.transpose(
                            PS[b][:].bitcast(BF16)[:, 128 * t8:128 * (t8 + 1)],
                            lnb[p][:, 128 * c:128 * (c + 1)], ident[:])
                    h_, _ = sb.bump("tr")
                    inst.then_inc(h_, 1)

        blk.sync(fa_sync)
        blk.vector(fa_vec)
        blk.scalar(fa_act)
        blk.tensor(fa_pe)
        phase_join(["tensor", "vector", "scalar"])
        p2.close()

        p2 = ExitStack()
        wpan = [p2.enter_context(nc.sbuf_tensor(f"wpan{i}", [128, 32 * 128], F32)) for i in range(2)]
        wpanb = [p2.enter_context(nc.sbuf_tensor(f"wpanb{i}", [128, 32 * 128], BF16)) for i in range(2)]
        qstg = [p2.enter_context(nc.sbuf_tensor(f"qstg{i}", [128, 512], BF16)) for i in range(2)]

        b_pj2 = sb.val("pj")

        def fb_sync(e):
            e.wait_ge(sb.hnd("pj"), b_pj2)
            for qc in range(32):
                if qc >= 2:
                    e.wait_ge(sb.hnd("pan_cast"), qc - 1)
                h_, _ = sb.bump(f"pnl{qc % 2}", 16)
                e.dma_start(out=wpan[qc % 2][:].rearrange("p (c n) -> p c n", c=32),
                            in_=WqT[:, 128 * qc:128 * (qc + 1)]
                            .rearrange("(c p) n -> p c n", p=128)).then_inc(h_, 16)

        def fb_pool(e):
            for qc in range(32):
                e.wait_ge(sb.hnd(f"pnl{qc % 2}"), 16 * (qc // 2 + 1))
                if qc >= 2:
                    e.wait_ge(sb.hnd("qmm"), 4 * (qc - 1))
                h_, _ = sb.bump("pan_cast")
                e.tensor_copy(out=wpanb[qc % 2][:], in_=wpan[qc % 2][:]).then_inc(h_, 1)

        def fb_vec(e):
            for qc in range(32):
                for rb in range(4):
                    n = 4 * qc + rb
                    e.wait_ge(sb.hnd("qmm"), n + 1)
                    if n >= 2:
                        e.wait_ge(sb.hnd(f"qst{n % 2}"), 16 * (n // 2))
                    h_, _ = sb.bump("qev")
                    e.tensor_scalar(out=qstg[n % 2][:], in0=PS[rb % 2][:],
                                    scalar1=QSCALE, scalar2=None,
                                    op0=ALU.mult).then_inc(h_, 1)

        def fb_pe(e):
            e.wait_ge(sb.hnd("trev"), b_trev + 4 * NJ)
            for qc in range(32):
                e.wait_ge(sb.hnd("pan_cast"), qc + 1)
                for rb in range(4):
                    n = 4 * qc + rb
                    bb = rb % 2
                    if n >= 2:
                        e.wait_ge(sb.hnd("qev"), n - 1)
                    for c in range(32):
                        inst = e.matmul(PS[bb][:],
                                        wpanb[qc % 2][:, 128 * c:128 * (c + 1)],
                                        lnTF[:, NROW * c + 512 * rb:NROW * c + 512 * (rb + 1)],
                                        start=(c == 0), stop=(c == 31))
                    h_, _ = sb.bump("qmm")
                    inst.then_inc(h_, 1)

        def fb_store(e):
            for qc in range(32):
                for rb in range(4):
                    n = 4 * qc + rb
                    e.wait_ge(sb.hnd("qev"), n + 1)
                    h_, _ = sb.bump(f"qst{n % 2}", 16)
                    e.dma_start(out=q_sc[(rb * 32 + qc) * 128:(rb * 32 + qc) * 128 + 128, :],
                                in_=qstg[n % 2][:]).then_inc(h_, 16)
            e.wait_ge(sb.hnd("qst0"), sb.val("qst0"))
            e.wait_ge(sb.hnd("qst1"), sb.val("qst1"))

        blk.sync(fb_sync)
        blk.gpsimd(fb_pool)
        blk.vector(fb_vec)
        blk.tensor(fb_pe)
        blk.scalar(fb_store)
        phase_join(["tensor", "vector", "gpsimd", "scalar"])
        p2.close()
        p2o.close()

    ph_p2()
    if NPH <= 2:
        es.close()
        return nc


    def ph_p3():
        p3 = ExitStack()
        kcmpT = p3.enter_context(nc.sbuf_tensor("p3kc", [128, 2 * S], BF16))
        vcmpT = p3.enter_context(nc.sbuf_tensor("p3vc", [128, 2 * S], BF16))
        qbuf = [p3.enter_context(nc.sbuf_tensor(f"qb{i}", [128, 32 * 512], BF16)) for i in range(2)]
        mbuf = [p3.enter_context(nc.sbuf_tensor(f"mb{i}", [128, 8 * 512], BF16)) for i in range(2)]
        kseg = p3.enter_context(nc.sbuf_tensor("kseg", [128, 1024], BF16))
        vseg = p3.enter_context(nc.sbuf_tensor("vseg", [128, 8 * 128], BF16))
        pbuf = p3.enter_context(nc.sbuf_tensor("pbuf", [128, 8 * 512], BF16))
        lsb = p3.enter_context(nc.sbuf_tensor("lsb", [4, 512], F32))
        lrec = p3.enter_context(nc.sbuf_tensor("lrec", [4, 512], F32))
        selt = p3.enter_context(nc.sbuf_tensor("selt", [4, 512], F32))
        recb = p3.enter_context(nc.sbuf_tensor("recb", [128, 512], F32))
        osb = [p3.enter_context(nc.sbuf_tensor(f"osb{i}", [128, 512], BF16)) for i in range(2)]

        b_qev = sb.val("qev")
        b_qst = [sb.val("qst0"), sb.val("qst1")]
        b_kvsp = sb.val("kvsp")
        b_pj3 = sb.val("pj")

        freemap = {b: None for b in range(8)}
        nw = {b: 0 for b in range(8)}
        n_sop = [0]; n_msk = [0]; n_upe = [0]; n_osc = [0]; n_swp = [0]
        pvd_hist = [0]
        sched = []
        for t in range(NT):
            tg = []
            for g in range(8):
                n = SWEEPS[t]
                rot = [5, 6, 7]
                ri = [0]

                def nb():
                    b = rot[ri[0] % 3]
                    ri[0] += 1
                    nw[b] += 1
                    fw = freemap[b]
                    return b, nw[b], fw

                segs = []
                for s in range(n // 8):
                    ups = []
                    for half in range(2):
                        b, w, fw = nb()
                        n_upe[0] += 1
                        ups.append(dict(k="k", s=s, half=half, b=b, bw=w, fw=fw,
                                        upe=n_upe[0]))
                        freemap[b] = ("upe", n_upe[0])
                    for kc in range(8):
                        b, w, fw = nb()
                        n_upe[0] += 1
                        ups.append(dict(k="v", s=s, kc=kc, b=b, bw=w, fw=fw,
                                        upe=n_upe[0]))
                        freemap[b] = ("upe", n_upe[0])
                    sops = []
                    for c in range(8):
                        C = 8 * s + c
                        for h in range(4):
                            b, w, fw = nb()
                            n_sop[0] += 1
                            masked = C >= n - 8
                            if masked:
                                n_msk[0] += 1
                            sops.append(dict(s=s, C=C, h=h, b=b, bw=w, fw=fw,
                                             masked=masked, isop=n_sop[0],
                                             imsk=n_msk[0],
                                             slot=(n_sop[0] - 1) % 8))
                            freemap[b] = ("pexp", n_sop[0])
                    segs.append(dict(ups=ups, sops=sops,
                                     war_pvd=pvd_hist[-1],
                                     upe_done=n_upe[0]))
                    pvd_hist.append(n_sop[0])
                ofw = [freemap[hh] for hh in range(4)]
                lfw = freemap[4]
                olast = [0] * 4
                for op in segs[-1]["sops"]:
                    if op["C"] == n - 1:
                        olast[op["h"]] = op["isop"]
                lbwv = None
                nw[4] += 1
                lbwv = nw[4]
                bcb = []
                for hh in range(4):
                    b, w, fw = nb()
                    n_osc[0] += 1
                    bcb.append(dict(b=b, bw=w, fw=fw, oi=n_osc[0]))
                    freemap[b] = ("oscd", n_osc[0])
                    freemap[hh] = ("oscd", n_osc[0])
                n_swp[0] += 1
                freemap[4] = ("lse", n_swp[0])
                tg.append(dict(segs=segs, ofw=ofw, lfw=lfw, olast=olast,
                               lbw=lbwv, bcb=bcb, swp=n_swp[0]))
            sched.append(tg)

        def wait_fw(e, fw):
            if fw is not None:
                e.wait_ge(sb.hnd(fw[0]), fw[1])

        def emit_stores(e, t):
            for g in range(8):
                for hh in range(4):
                    oi = 32 * t + 4 * g + hh
                    e.wait_ge(sb.hnd("oscd"), oi + 1)
                    h_, _ = sb.bump(f"ost{oi % 2}", 16)
                    r0 = (4 * g + hh) * 128
                    e.dma_start(out=o_sc[r0:r0 + 128, 512 * t:512 * t + 512],
                                in_=osb[oi % 2][:]).then_inc(h_, 16)

        def f_sync(e):
            e.wait_ge(sb.hnd("pj"), b_pj3)
            e.wait_ge(sb.hnd("kvsp"), b_kvsp)
            h_, _ = sb.bump("kv_ld", 16)
            e.dma_start(out=kcmpT[:].rearrange("p (c n) -> p c n", c=2),
                        in_=kv_sc[0:256, :].rearrange("(c p) n -> p c n", p=128)).then_inc(h_, 16)
            h_, _ = sb.bump("kv_ld", 16)
            e.dma_start(out=vcmpT[:].rearrange("p (c n) -> p c n", c=2),
                        in_=kv_sc[256:512, :].rearrange("(c p) n -> p c n", p=128)).then_inc(h_, 16)
            e.wait_ge(sb.hnd("qst0"), b_qst[0])
            e.wait_ge(sb.hnd("qst1"), b_qst[1])
            for t in range(NT):
                if t >= 2:
                    e.wait_ge(sb.hnd("tdone"), t - 1)
                h_, _ = sb.bump(f"qtl{t % 2}", 16)
                e.dma_start(out=qbuf[t % 2][:].rearrange("p (c n) -> p c n", c=32),
                            in_=q_sc[4096 * t:4096 * (t + 1), :]
                            .rearrange("(c p) n -> p c n", p=128)).then_inc(h_, 16)
                if t >= 2:
                    e.wait_ge(sb.hnd("mdone"), t - 1)
                h_, _ = sb.bump(f"mld{t % 2}", 16)
                e.dma_start(out=mbuf[t % 2][:].rearrange("p (c n) -> p c n", c=8),
                            in_=masks[8 * 128 * t:8 * 128 * (t + 1), :]
                            .rearrange("(c p) n -> p c n", p=128)).then_inc(h_, 16)
                if t >= 1:
                    emit_stores(e, t - 1)
            emit_stores(e, NT - 1)
            e.wait_ge(sb.hnd("ost0"), sb.val("ost0"))
            e.wait_ge(sb.hnd("ost1"), sb.val("ost1"))

        def f_pool(e):
            e.wait_ge(sb.hnd("consts"), CONSTS_DONE)
            e.memset(selt[:], 1.0)
            e.drain()
            e.affine_select(out=selt[:], in_=selt[:], pattern=[[1, 512]],
                            compare_op=ALU.is_ge, fill=0.0, base=0,
                            channel_multiplier=-128)
            e.drain()
            h_, _ = sb.bump("selc")
            e.affine_select(out=selt[:], in_=selt[:], pattern=[[-1, 512]],
                            compare_op=ALU.is_ge, fill=0.0, base=127,
                            channel_multiplier=128).then_inc(h_, 1)

        def emit_lpv(e, n, ent, op):
            C, h = op["C"], op["h"]
            if op["masked"]:
                e.wait_ge(sb.hnd("pmask"), op["imsk"])
            else:
                e.wait_ge(sb.hnd("pexp"), op["isop"])
            first = (C == 0 and h == 0)
            firsth = (C == 0)
            last = (C == n - 1 and h == 3)
            lasth = (C == n - 1)
            pb = pbuf[:, 512 * op["slot"]:512 * op["slot"] + 512]
            if firsth:
                wait_fw(e, ent["ofw"][h])
            if first:
                wait_fw(e, ent["lfw"])
            inst = e.matmul(PS[4][0:4, :], inds[:, 4 * h:4 * h + 4], pb,
                            start=first, stop=last)
            if last:
                h_, _ = sb.bump("bk4w")
                inst.then_inc(h_, 1)
            inst2 = e.matmul(PS[h][:], vseg[:, 128 * (C % 8):128 * (C % 8) + 128],
                             pb, start=firsth, stop=lasth)
            h_, _ = sb.bump("pvd")
            inst2.then_inc(h_, 1)

        def f_pe(e):
            e.wait_ge(sb.hnd("qev"), b_qev)
            e.wait_ge(sb.hnd("kv_ld"), 32)
            e.wait_ge(sb.hnd("selc"), 1)
            for t in range(NT):
                e.wait_ge(sb.hnd(f"qtl{t % 2}"), 16 * (t // 2 + 1))
                n = SWEEPS[t]
                for g in range(8):
                    ent = sched[t][g]
                    for seg in ent["segs"]:
                        for op in seg["ups"]:
                            b = op["b"]
                            wait_fw(e, op["fw"])
                            if op["k"] == "k":
                                s, half = op["s"], op["half"]
                                for cc in range(2):
                                    inst = e.matmul(
                                        PS[b][:],
                                        kupb[:, 1024 * cc + 128 * g:1024 * cc + 128 * g + 128],
                                        kcmpT[:, 4096 * cc + 1024 * s + 512 * half:
                                              4096 * cc + 1024 * s + 512 * half + 512],
                                        start=(cc == 0), stop=(cc == 1))
                            else:
                                s, kc = op["s"], op["kc"]
                                for cc in range(2):
                                    inst = e.matmul(
                                        PS[b][:, 0:128],
                                        vcmpT[:, 4096 * cc + 1024 * s + 128 * kc:
                                              4096 * cc + 1024 * s + 128 * kc + 128],
                                        vupb[:, 1024 * cc + 128 * g:1024 * cc + 128 * g + 128],
                                        start=(cc == 0), stop=(cc == 1))
                            h_, _ = sb.bump(f"bk{b}w")
                            inst.then_inc(h_, 1)
                        pend = []
                        for i, op in enumerate(seg["sops"]):
                            b = op["b"]
                            wait_fw(e, op["fw"])
                            if i == 0:
                                e.wait_ge(sb.hnd("upe"), seg["upe_done"])
                            inst = e.matmul(
                                PS[b][:],
                                kseg[:, 128 * (op["C"] % 8):128 * (op["C"] % 8) + 128],
                                qbuf[t % 2][:, 512 * (4 * g + op["h"]):512 * (4 * g + op["h"]) + 512],
                                start=True, stop=True)
                            h_, _ = sb.bump(f"bk{b}w")
                            inst.then_inc(h_, 1)
                            pend.append(op)
                            if len(pend) > 2:
                                emit_lpv(e, n, ent, pend.pop(0))
                        for op in pend:
                            emit_lpv(e, n, ent, op)
                    e.wait_ge(sb.hnd("lrc"), 8 * t + g + 1)
                    for hh in range(4):
                        bc = ent["bcb"][hh]
                        wait_fw(e, bc["fw"])
                        inst = e.matmul(PS[bc["b"]][:], selt[:, 128 * hh:128 * hh + 128],
                                        lrec[:], start=True, stop=True)
                        h_, _ = sb.bump(f"bk{bc['b']}w")
                        inst.then_inc(h_, 1)
                    h_, _ = sb.bump("bcd")
                    e.drain().then_inc(h_, 1)
                h_, _ = sb.bump("tdone")
                e.drain().then_inc(h_, 1)

        def f_act(e):
            for t in range(NT):
                n = SWEEPS[t]
                for g in range(8):
                    ent = sched[t][g]
                    for seg in ent["segs"]:
                        for op in seg["sops"]:
                            b = op["b"]
                            e.wait_ge(sb.hnd(f"bk{b}w"), op["bw"])
                            if op["isop"] > 8:
                                e.wait_ge(sb.hnd("pvd"), op["isop"] - 8)
                            pb = pbuf[:, 512 * op["slot"]:512 * op["slot"] + 512]
                            h1, _ = sb.bump("pexp")
                            e.activation(out=pb, in_=PS[b][:], func=AF.Exp,
                                         bias=cst[:, 2:3],
                                         scale=1.0).then_inc(h1, 1)
                    e.wait_ge(sb.hnd("lse"), 8 * t + g + 1)
                    e.drain()
                    if 8 * t + g > 0:
                        e.wait_ge(sb.hnd("bcd"), 8 * t + g)
                    h_, _ = sb.bump("lr1")
                    e.activation(out=lrec[:], in_=lsb[:], func=AF.Ln,
                                 scale=1.0).then_inc(h_, 1)
                    e.drain()
                    h_, _ = sb.bump("lrc")
                    e.activation(out=lrec[:], in_=lrec[:], func=AF.Exp,
                                 scale=cst[0:4, 4:5]).then_inc(h_, 1)

        def f_vec(e):
            oi = 0
            for t in range(NT):
                n = SWEEPS[t]
                e.wait_ge(sb.hnd(f"mld{t % 2}"), 16 * (t // 2 + 1))
                for g in range(8):
                    ent = sched[t][g]
                    for seg in ent["segs"]:
                        if seg["war_pvd"] > 0:
                            e.wait_ge(sb.hnd("pvd"), seg["war_pvd"])
                        for op in seg["ups"]:
                            b = op["b"]
                            e.wait_ge(sb.hnd(f"bk{b}w"), op["bw"])
                            h2, _ = sb.bump("upe")
                            if op["k"] == "k":
                                e.tensor_copy(
                                    out=kseg[:, 512 * op["half"]:512 * op["half"] + 512],
                                    in_=PS[b][:]).then_inc(h2, 1)
                            else:
                                e.tensor_copy(
                                    out=vseg[:, 128 * op["kc"]:128 * op["kc"] + 128],
                                    in_=PS[b][:, 0:128]).then_inc(h2, 1)
                        for op in seg["sops"]:
                            if not op["masked"]:
                                continue
                            e.wait_ge(sb.hnd("pexp"), op["isop"])
                            pb = pbuf[:, 512 * op["slot"]:512 * op["slot"] + 512]
                            ci = op["C"] - (n - 8)
                            h_, _ = sb.bump("pmask")
                            e.tensor_tensor(out=pb, in0=pb,
                                            in1=mbuf[t % 2][:, 512 * ci:512 * ci + 512],
                                            op=ALU.mult).then_inc(h_, 1)
                    e.wait_ge(sb.hnd("bk4w"), ent["lbw"])
                    if 8 * t + g > 0:
                        e.wait_ge(sb.hnd("lr1"), 8 * t + g)
                    e.drain()
                    h2, _ = sb.bump("lse")
                    e.tensor_copy(out=lsb[:], in_=PS[4][0:4, :]).then_inc(h2, 1)
                    for hh in range(4):
                        bc = ent["bcb"][hh]
                        e.wait_ge(sb.hnd(f"bk{bc['b']}w"), bc["bw"])
                        e.drain()
                        e.tensor_copy(out=recb[:], in_=PS[bc["b"]][:])
                        e.wait_ge(sb.hnd("pvd"), ent["olast"][hh])
                        if oi >= 2:
                            e.wait_ge(sb.hnd(f"ost{oi % 2}"), 16 * (oi // 2))
                        e.drain()
                        h2, _ = sb.bump("oscd")
                        e.tensor_tensor(out=osb[oi % 2][:], in0=PS[hh][:],
                                        in1=recb[:],
                                        op=ALU.mult).then_inc(h2, 1)
                        oi += 1
                h_, _ = sb.bump("mdone")
                e.drain().then_inc(h_, 1)

        def f_store(e):
            if "o" in dbg_outs:
                e.wait_ge(sb.hnd("ost0"), sb.val("ost0"))
                e.wait_ge(sb.hnd("ost1"), sb.val("ost1"))
                h_, _ = sb.bump("dbg", 16)
                e.dma_start(out=dbg_outs["o"][:, :].rearrange("(c p) n -> p c n", p=128),
                            in_=o_sc[:, :].rearrange("(c p) n -> p c n", p=128)).then_inc(h_, 16)
                e.wait_ge(sb.hnd("dbg"), sb.val("dbg"))

        blk.sync(f_sync)
        blk.gpsimd(f_pool)
        blk.tensor(f_pe)
        blk.scalar(f_act)
        blk.vector(f_vec)
        blk.scalar(f_store)
        phase_join(["tensor", "vector", "scalar", "gpsimd"])
        p3.close()

    ph_p3()
    kv_es.close()
    if NPH <= 3:
        es.close()
        return nc


    def ph_g3():
        g3 = ExitStack()
        otb = g3.enter_context(nc.sbuf_tensor("otb", [128, 32 * 1024], BF16))
        wop = [g3.enter_context(nc.sbuf_tensor(f"wop{i}", [128, 32 * 256], F32)) for i in range(2)]
        wopb = [g3.enter_context(nc.sbuf_tensor(f"wopb{i}", [128, 32 * 256], BF16)) for i in range(2)]
        aostg = [g3.enter_context(nc.sbuf_tensor(f"aostg{i}", [128, 256], F32)) for i in range(2)]

        b_pj = sb.val("pj")
        b_kvld = sb.val("kv_ld")
        b_pnl = [sb.val("pnl0"), sb.val("pnl1")]
        b_cast = sb.val("pan_cast")
        b_mm = sb.val("qmm")
        b_ev = sb.val("qev")
        b_st = [sb.val("qst0"), sb.val("qst1")]
        b_ost = [sb.val("ost0"), sb.val("ost1")]

        def f_sync(e):
            e.wait_ge(sb.hnd("pj"), b_pj)
            e.wait_ge(sb.hnd("ost0"), b_ost[0])
            e.wait_ge(sb.hnd("ost1"), b_ost[1])
            for rg in range(2):
                if rg == 1:
                    e.wait_ge(sb.hnd("qev"), b_ev + 128)
                h_, _ = sb.bump("kv_ld", 16)
                e.dma_start(out=otb[:].rearrange("p (c n) -> p c n", c=32),
                            in_=o_sc[:, 1024 * rg:1024 * (rg + 1)]
                            .rearrange("(c p) n -> p c n", p=128)).then_inc(h_, 16)
                for db in range(16):
                    i = 16 * rg + db
                    if i >= 2:
                        e.wait_ge(sb.hnd("pan_cast"), b_cast + i - 1)
                    h_, _ = sb.bump(f"pnl{i % 2}", 16)
                    e.dma_start(out=wop[i % 2][:].rearrange("p (c n) -> p c n", c=32),
                                in_=WoT[:, 256 * db:256 * (db + 1)]
                                .rearrange("(c p) n -> p c n", p=128)).then_inc(h_, 16)

        def f_pool(e):
            for i in range(32):
                e.wait_ge(sb.hnd(f"pnl{i % 2}"), b_pnl[i % 2] + 16 * (i // 2 + 1))
                if i >= 2:
                    e.wait_ge(sb.hnd("qmm"), b_mm + 8 * (i - 1))
                h_, _ = sb.bump("pan_cast")
                e.tensor_copy(out=wopb[i % 2][:], in_=wop[i % 2][:]).then_inc(h_, 1)

        def f_pe(e):
            for rg in range(2):
                e.wait_ge(sb.hnd("kv_ld"), b_kvld + 16 * (rg + 1))
                for db in range(16):
                    i = 16 * rg + db
                    e.wait_ge(sb.hnd("pan_cast"), b_cast + i + 1)
                    for rs in range(8):
                        nn = 8 * i + rs
                        bb = nn % 2
                        if nn >= 2:
                            e.wait_ge(sb.hnd("qev"), b_ev + nn - 1)
                        for c in range(32):
                            inst = e.matmul(
                                PS[bb][:, 0:256],
                                otb[:, 1024 * c + 128 * rs:1024 * c + 128 * rs + 128],
                                wopb[i % 2][:, 256 * c:256 * (c + 1)],
                                start=(c == 0), stop=(c == 31))
                        h_, _ = sb.bump("qmm")
                        inst.then_inc(h_, 1)

        def f_vec(e):
            for nn in range(256):
                e.wait_ge(sb.hnd("qmm"), b_mm + nn + 1)
                if nn >= 2:
                    e.wait_ge(sb.hnd(f"qst{nn % 2}"), b_st[nn % 2] + 16 * (nn // 2))
                h_, _ = sb.bump("qev")
                e.tensor_copy(out=aostg[nn % 2][:],
                              in_=PS[nn % 2][:, 0:256]).then_inc(h_, 1)

        def f_store(e):
            for rg in range(2):
                for db in range(16):
                    for rs in range(8):
                        nn = 128 * rg + 8 * db + rs
                        e.wait_ge(sb.hnd("qev"), b_ev + nn + 1)
                        h_, _ = sb.bump(f"qst{nn % 2}", 16)
                        r0 = 1024 * rg + 128 * rs
                        e.dma_start(out=ao_sc[r0:r0 + 128, 256 * db:256 * (db + 1)],
                                    in_=aostg[nn % 2][:]).then_inc(h_, 16)
            e.wait_ge(sb.hnd("qst0"), sb.val("qst0"))
            e.wait_ge(sb.hnd("qst1"), sb.val("qst1"))

        blk.sync(f_sync)
        blk.gpsimd(f_pool)
        blk.tensor(f_pe)
        blk.vector(f_vec)
        blk.scalar(f_store)
        phase_join(["tensor", "vector", "gpsimd", "scalar"])
        g3.close()

    ph_g3()
    if NPH <= 4:
        es.close()
        return nc

    def ph_p3b():
        p4 = ExitStack()
        aor = [p4.enter_context(nc.sbuf_tensor(f"aor{i}", [128, DM], F32)) for i in range(2)]
        xor_ = [p4.enter_context(nc.sbuf_tensor(f"xor{i}", [128, DM], F32)) for i in range(2)]
        t1b = [p4.enter_context(nc.sbuf_tensor(f"t1b{i}", [128, DM], F32)) for i in range(2)]
        x1b = [p4.enter_context(nc.sbuf_tensor(f"x1b{i}", [128, DM], F32)) for i in range(2)]
        ln2b = [p4.enter_context(nc.sbuf_tensor(f"l2b{i}", [128, DM], BF16)) for i in range(2)]
        sqd = p4.enter_context(nc.sbuf_tensor("p4sqd", [128, DM], BF16))
        ln2stg = [p4.enter_context(nc.sbuf_tensor(f"ln2stg{i}", [128, 32 * 128], BF16)) for i in range(2)]

        NJ = 16
        b_pj = sb.val("pj")
        b = {k: sb.val(k) for k in ("qtl0", "qtl1", "xld0", "xld1",
                                    "sq", "sqB", "var", "varB", "rstd", "rstdB",
                                    "negmr", "negmrB", "apply", "applyB",
                                    "sq2", "sq2B", "var2", "var2B",
                                    "rstd2", "rstd2B", "negmr2", "negmr2B",
                                    "ap2", "ap2B", "x1d", "tr",
                                    "trev", "ost0", "ost1", "mld0", "mld1",
                                    "qst0", "qst1")}

        def emit_stores_j(e, j):
            p = j % 2
            e.wait_ge(sb.hnd("x1d"), b["x1d"] + j + 1)
            h_, _ = sb.bump(f"ost{p}", 16)
            e.dma_start(out=x1_sc[128 * j:128 * (j + 1), :],
                        in_=x1b[p][:]).then_inc(h_, 16)
            e.wait_ge(sb.hnd("trev"), b["trev"] + 4 * (j + 1))
            h_, _ = sb.bump(f"mld{p}", 16)
            l2v = ln2_sc.rearrange("(c p) n -> p c n", p=128)
            srcv = ln2stg[p][:, :].rearrange("p (c r) -> p c r", r=128)
            e.dma_start(out=l2v[:, :, 128 * j:128 * j + 128],
                        in_=srcv).then_inc(h_, 16)

        def f_sync(e):
            e.wait_ge(sb.hnd("pj"), b_pj)
            e.wait_ge(sb.hnd("qst0"), b["qst0"])
            e.wait_ge(sb.hnd("qst1"), b["qst1"])
            for j in range(NJ):
                p = j % 2
                e.wait_ge(sb.hnd(_ps("apply", p)),
                          b[_ps("apply", p)] + (j // 2 if j >= 2 else 0))
                h_, _ = sb.bump(f"qtl{j % 2}", 16)
                e.dma_start(out=aor[j % 2][:],
                            in_=ao_sc[128 * j:128 * (j + 1), :]).then_inc(h_, 16)
                e.wait_ge(sb.hnd("x1d"), b["x1d"] + max(j - 1, 0))
                h_, _ = sb.bump(f"xld{j % 2}", 16)
                e.dma_start(out=xor_[j % 2][:],
                            in_=x_own[128 * j:128 * (j + 1), :]).then_inc(h_, 16)
                if j >= 1:
                    emit_stores_j(e, j - 1)
            emit_stores_j(e, NJ - 1)
            e.wait_ge(sb.hnd("ost0"), sb.val("ost0"))
            e.wait_ge(sb.hnd("ost1"), sb.val("ost1"))
            e.wait_ge(sb.hnd("mld0"), sb.val("mld0"))
            e.wait_ge(sb.hnd("mld1"), sb.val("mld1"))

        def f_vec(e):
            for j in range(NJ):
                p = j % 2
                jp = j // 2 + 1
                o = 16 * p
                if j >= 2:
                    e.drain()
                e.wait_ge(sb.hnd(f"qtl{p}"), b[f"qtl{p}"] + 16 * jp)
                e.tensor_reduce(out=stt[:, o:o + 1], in_=aor[p][:],
                                axis=AX.X, op=ALU.add)
                e.drain()
                e.wait_ge(sb.hnd(_ps("sq", p)), b[_ps("sq", p)] + jp)
                if j >= 2:
                    e.wait_ge(sb.hnd(_ps("rstd", p)), b[_ps("rstd", p)] + jp - 1)
                e.tensor_scalar(out=stt[:, o + 2:o + 3], in0=stt[:, o:o + 1],
                                scalar1=stt[:, o:o + 1], scalar2=float(1.0 / DM),
                                op0=ALU.mult, op1=ALU.mult)
                e.drain()
                h_, _ = sb.bump(_ps("var", p))
                e.tensor_tensor(out=stt[:, o + 2:o + 3], in0=stt[:, o + 1:o + 2],
                                in1=stt[:, o + 2:o + 3], op=ALU.subtract).then_inc(h_, 1)
                e.wait_ge(sb.hnd(_ps("rstd", p)), b[_ps("rstd", p)] + jp)
                e.drain()
                h_, _ = sb.bump(_ps("negmr", p))
                e.tensor_scalar(out=stt[:, o + 4:o + 5], in0=stt[:, o:o + 1],
                                scalar1=stt[:, o + 3:o + 4], scalar2=float(-1.0 / DM),
                                op0=ALU.mult, op1=ALU.mult).then_inc(h_, 1)
                e.wait_ge(sb.hnd(_ps("apply", p)), b[_ps("apply", p)] + jp)
                e.wait_ge(sb.hnd(f"xld{p}"), b[f"xld{p}"] + 16 * jp)
                if j >= 2:
                    e.wait_ge(sb.hnd(_ps("ap2", p)), b[_ps("ap2", p)] + jp - 1)
                    e.wait_ge(sb.hnd(f"ost{p}"), b[f"ost{p}"] + 16 * (j // 2))
                e.drain()
                h_, _ = sb.bump("x1d")
                e.tensor_tensor(out=x1b[p][:], in0=t1b[p][:], in1=xor_[p][:],
                                op=ALU.add).then_inc(h_, 1)
                e.drain()
                e.tensor_reduce(out=stt[:, o + 5:o + 6], in_=x1b[p][:],
                                axis=AX.X, op=ALU.add)
                e.drain()
                e.wait_ge(sb.hnd(_ps("sq2", p)), b[_ps("sq2", p)] + jp)
                if j >= 2:
                    e.wait_ge(sb.hnd(_ps("rstd2", p)), b[_ps("rstd2", p)] + jp - 1)
                e.tensor_scalar(out=stt[:, o + 7:o + 8], in0=stt[:, o + 5:o + 6],
                                scalar1=stt[:, o + 5:o + 6], scalar2=float(1.0 / DM),
                                op0=ALU.mult, op1=ALU.mult)
                e.drain()
                h_, _ = sb.bump(_ps("var2", p))
                e.tensor_tensor(out=stt[:, o + 7:o + 8], in0=stt[:, o + 6:o + 7],
                                in1=stt[:, o + 7:o + 8], op=ALU.subtract).then_inc(h_, 1)
                e.wait_ge(sb.hnd(_ps("rstd2", p)), b[_ps("rstd2", p)] + jp)
                e.drain()
                h_, _ = sb.bump(_ps("negmr2", p))
                e.tensor_scalar(out=stt[:, o + 9:o + 10], in0=stt[:, o + 5:o + 6],
                                scalar1=stt[:, o + 8:o + 9], scalar2=float(-1.0 / DM),
                                op0=ALU.mult, op1=ALU.mult).then_inc(h_, 1)
                for fb in range(4):
                    e.wait_ge(sb.hnd("tr"), b["tr"] + 4 * j + fb + 1)
                    if fb == 0 and j >= 2:
                        e.wait_ge(sb.hnd(f"mld{p}"), b[f"mld{p}"] + 16 * (j // 2))
                    bb = fb % 2
                    src = PS[bb][:].bitcast(BF16)[:, 0:1024]
                    dst = ln2stg[p][:, :].rearrange("p (c r) -> p c r", r=128)
                    h_, _ = sb.bump("trev")
                    e.tensor_copy(out=dst[:, 8 * fb:8 * fb + 8, :],
                                  in_=src.rearrange("p (c r) -> p c r", r=128)).then_inc(h_, 1)

        def f_act(e):
            for j in range(NJ):
                p = j % 2
                jp = j // 2 + 1
                o = 16 * p
                if j >= 2:
                    e.drain()
                e.wait_ge(sb.hnd(f"qtl{p}"), b[f"qtl{p}"] + 16 * jp)
                if j >= 2:
                    e.wait_ge(sb.hnd(_ps("var", p)), b[_ps("var", p)] + jp - 1)
                h_, _ = sb.bump(_ps("sq", p))
                e.activation(out=sqd[:], in_=aor[p][:], func=AF.Square,
                             accum_out=stt[:, o + 1:o + 2]).then_inc(h_, 1)
                e.wait_ge(sb.hnd(_ps("var", p)), b[_ps("var", p)] + jp)
                e.activation(out=stt[:, o + 2:o + 3], in_=stt[:, o + 2:o + 3], func=AF.Ln,
                             bias=cst[:, 0:1], scale=1.0)
                e.drain()
                h_, _ = sb.bump(_ps("rstd", p))
                e.activation(out=stt[:, o + 3:o + 4], in_=stt[:, o + 2:o + 3], func=AF.Exp,
                             bias=cst[:, 1:2], scale=cst[:, 3:4]).then_inc(h_, 1)
                e.wait_ge(sb.hnd(_ps("negmr", p)), b[_ps("negmr", p)] + jp)
                if j >= 2:
                    e.wait_ge(sb.hnd("x1d"), b["x1d"] + j - 1)
                e.drain()
                h_, _ = sb.bump(_ps("apply", p))
                e.activation(out=t1b[p][:], in_=aor[p][:], func=AF.Identity,
                             bias=stt[:, o + 4:o + 5], scale=stt[:, o + 3:o + 4]).then_inc(h_, 1)
                e.drain()
                e.wait_ge(sb.hnd("x1d"), b["x1d"] + j + 1)
                if j >= 2:
                    e.wait_ge(sb.hnd(_ps("var2", p)), b[_ps("var2", p)] + jp - 1)
                h_, _ = sb.bump(_ps("sq2", p))
                e.activation(out=sqd[:], in_=x1b[p][:], func=AF.Square,
                             accum_out=stt[:, o + 6:o + 7]).then_inc(h_, 1)
                e.wait_ge(sb.hnd(_ps("var2", p)), b[_ps("var2", p)] + jp)
                e.activation(out=stt[:, o + 7:o + 8], in_=stt[:, o + 7:o + 8], func=AF.Ln,
                             bias=cst[:, 0:1], scale=1.0)
                e.drain()
                h_, _ = sb.bump(_ps("rstd2", p))
                e.activation(out=stt[:, o + 8:o + 9], in_=stt[:, o + 7:o + 8], func=AF.Exp,
                             bias=cst[:, 1:2], scale=cst[:, 3:4]).then_inc(h_, 1)
                e.wait_ge(sb.hnd(_ps("negmr2", p)), b[_ps("negmr2", p)] + jp)
                e.drain()
                if j >= 2:
                    e.wait_ge(sb.hnd("tr"), b["tr"] + 4 * (j - 1))
                h_, _ = sb.bump(_ps("ap2", p))
                e.activation(out=ln2b[p][:], in_=x1b[p][:], func=AF.Identity,
                             bias=stt[:, o + 9:o + 10], scale=stt[:, o + 8:o + 9]).then_inc(h_, 1)

        def f_pe(e):
            for j in range(NJ):
                p = j % 2
                e.wait_ge(sb.hnd(_ps("ap2", p)), b[_ps("ap2", p)] + j // 2 + 1)
                for fb in range(4):
                    bb = fb % 2
                    nfill = 4 * j + fb
                    if nfill >= 2:
                        e.wait_ge(sb.hnd("trev"), b["trev"] + nfill - 1)
                    for t8 in range(8):
                        c = 8 * fb + t8
                        inst = e.transpose(
                            PS[bb][:].bitcast(BF16)[:, 128 * t8:128 * (t8 + 1)],
                            ln2b[p][:, 128 * c:128 * (c + 1)], ident[:])
                    h_, _ = sb.bump("tr")
                    inst.then_inc(h_, 1)

        blk.sync(f_sync)
        blk.vector(f_vec)
        blk.scalar(f_act)
        blk.tensor(f_pe)
        phase_join(["tensor", "vector", "scalar"])
        p4.close()

    ph_p3b()
    if NPH <= 5:
        es.close()
        return nc


    g45 = ExitStack()
    dA_sb = g45.enter_context(nc.sbuf_tensor("dA_sb", [16, 2048], BF16))

    def ph_g4():
        g4 = ExitStack()
        ln2TF = g4.enter_context(nc.sbuf_tensor("ln2TF", [128, 32 * NROW], BF16))
        gp = [g4.enter_context(nc.sbuf_tensor(f"gp{i}", [128, 32 * 64], F32)) for i in range(2)]
        gpb = [g4.enter_context(nc.sbuf_tensor(f"gpb{i}", [128, 32 * 128], BF16)) for i in range(3)]
        gsil = [g4.enter_context(nc.sbuf_tensor(f"gsil{i}", [128, 2048], BF16)) for i in range(2)]
        h_sb = [g4.enter_context(nc.sbuf_tensor(f"hsb{i}", [128, 2048], BF16)) for i in range(2)]
        gATb = g4.enter_context(nc.sbuf_tensor("gATb", [128, 32 * 16], BF16))
        uATb = g4.enter_context(nc.sbuf_tensor("uATb", [128, 32 * 16], BF16))
        dATb = g4.enter_context(nc.sbuf_tensor("dATb", [128, NHC * 16], BF16))
        gA_sb = g4.enter_context(nc.sbuf_tensor("gA_sb", [16, 2048], BF16))
        uA_sb = g4.enter_context(nc.sbuf_tensor("uA_sb", [16, 2048], BF16))
        gBst = [g4.enter_context(nc.sbuf_tensor(f"gBst{i}", [16, 128], F32)) for i in range(2)]
        uBst = [g4.enter_context(nc.sbuf_tensor(f"uBst{i}", [16, 128], F32)) for i in range(2)]
        gBb = [g4.enter_context(nc.sbuf_tensor(f"gBb{i}", [16, 128], BF16)) for i in range(2)]
        uBb = [g4.enter_context(nc.sbuf_tensor(f"uBb{i}", [16, 128], BF16)) for i in range(2)]

        b_pj = sb.val("pj")
        b = {k: sb.val(k) for k in ("wkkup", "wk_cast", "kv_ld", "ymm", "yev",
                                    "pnl0", "pnl1", "pan_cast", "qmm", "qev",
                                    "qst0", "qst1", "pvd", "gsl", "hm",
                                    "mld0", "mld1", "wkld0", "wkld1",
                                    "wkld2", "wkld3")}

        def f_sync(e):
            e.wait_ge(sb.hnd("pj"), b_pj)
            for q in range(4):
                e.wait_ge(sb.hnd(f"wkld{q}"), b[f"wkld{q}"])
            e.wait_ge(sb.hnd("mld0"), b["mld0"])
            e.wait_ge(sb.hnd("mld1"), b["mld1"])
            h_, _ = sb.bump("kv_ld", 16)
            e.dma_start(out=ln2TF[:].rearrange("p (c n) -> p c n", c=32),
                        in_=ln2_sc[:, :].rearrange("(c p) n -> p c n", p=128)).then_inc(h_, 16)
            for i, (ncol, srcw) in enumerate([
                    (512, gateAT), (512, upAT), (NHC * 16, downAT)]):
                if i >= 2:
                    e.wait_ge(sb.hnd("wk_cast"), b["wk_cast"] + i - 1)
                h_, _ = sb.bump(f"wkld{i % 2}", 16)
                e.dma_start(out=gp[i % 2][:, 0:ncol].rearrange("p (c n) -> p c n", n=16),
                            in_=srcw.rearrange("(c p) n -> p c n", p=128)).then_inc(h_, 16)
            for c in range(NHC):
                for w in range(2):
                    i = 2 * c + w
                    wt = gateT if w == 0 else upT
                    for hf in range(2):
                        hi = 2 * i + hf
                        if hi < 2:
                            e.wait_ge(sb.hnd("wk_cast"), b["wk_cast"] + 3)
                        if hi >= 2:
                            e.wait_ge(sb.hnd("pan_cast"), b["pan_cast"] + hi - 1)
                        h_, _ = sb.bump(f"pnl{hi % 2}", 16)
                        e.dma_start(out=gp[hi % 2][:].rearrange("p (c n) -> p c n", c=32),
                                    in_=wt[:, 128 * c + 64 * hf:128 * c + 64 * (hf + 1)]
                                    .rearrange("(c p) n -> p c n", p=128)).then_inc(h_, 16)
                if c >= 2:
                    e.wait_ge(sb.hnd("wk_cast"), b["wk_cast"] + 3 + 2 * (c - 1))
                h_, _ = sb.bump(f"wkld{c % 2}", 16)
                e.dma_start(out=gBst[c % 2][:],
                            in_=gateBT[:, 128 * c:128 * (c + 1)]).then_inc(h_, 16)
                h_, _ = sb.bump(f"wkld{2 + c % 2}", 16)
                e.dma_start(out=uBst[c % 2][:],
                            in_=upBT[:, 128 * c:128 * (c + 1)]).then_inc(h_, 16)
                if c >= 1:
                    cc = c - 1
                    e.wait_ge(sb.hnd("hm"), b["hm"] + 4 * (cc + 1))
                    h_, _ = sb.bump(f"qst{cc % 2}", 16)
                    hv = h_sc.rearrange("(a b p) n -> p a b n", a=16, b=NHC)
                    e.dma_start(out=hv[:, :, cc, :],
                                in_=h_sb[cc % 2][:].rearrange("p (a r) -> p a r", a=16)).then_inc(h_, 16)
            cc = NHC - 1
            e.wait_ge(sb.hnd("hm"), b["hm"] + 4 * (cc + 1))
            h_, _ = sb.bump(f"qst{cc % 2}", 16)
            hv = h_sc.rearrange("(a b p) n -> p a b n", a=16, b=NHC)
            e.dma_start(out=hv[:, :, cc, :],
                        in_=h_sb[cc % 2][:].rearrange("p (a r) -> p a r", a=16)).then_inc(h_, 16)
            e.wait_ge(sb.hnd("qst0"), sb.val("qst0"))
            e.wait_ge(sb.hnd("qst1"), sb.val("qst1"))

        def f_pool(e):
            for i, (dsb, ncol) in enumerate([(gATb, 512), (uATb, 512),
                                             (dATb, NHC * 16)]):
                e.wait_ge(sb.hnd(f"wkld{i % 2}"), b[f"wkld{i % 2}"] + 16 * (i // 2 + 1))
                h_, _ = sb.bump("wk_cast")
                e.tensor_copy(out=dsb[:, 0:ncol], in_=gp[i % 2][:, 0:ncol]).then_inc(h_, 1)
            for c in range(NHC):
                for w in range(2):
                    i = 2 * c + w
                    for hf in range(2):
                        hi = 2 * i + hf
                        e.wait_ge(sb.hnd(f"pnl{hi % 2}"), b[f"pnl{hi % 2}"] + 16 * (hi // 2 + 1))
                        if i >= 3:
                            e.wait_ge(sb.hnd("qmm"), b["qmm"] + 4 * (i - 2))
                        h_, _ = sb.bump("pan_cast")
                        dstv = gpb[i % 3][:, :].rearrange("p (c h n) -> p c h n",
                                                          h=2, n=64)
                        e.tensor_copy(out=dstv[:, :, hf, :],
                                      in_=gp[hi % 2][:].rearrange("p (c n) -> p c n",
                                                                  n=64)).then_inc(h_, 1)
                aoff = 32 if c % 2 == 0 else 16
                e.wait_ge(sb.hnd(f"wkld{c % 2}"), b[f"wkld{c % 2}"] + aoff + 16 * (c // 2 + 1))
                e.wait_ge(sb.hnd(f"wkld{2 + c % 2}"), b[f"wkld{2 + c % 2}"] + 16 * (c // 2 + 1))
                if c >= 2:
                    e.wait_ge(sb.hnd("qmm"), b["qmm"] + 8 * (c - 1))
                h_, _ = sb.bump("wk_cast")
                e.tensor_copy(out=gBb[c % 2][:], in_=gBst[c % 2][:]).then_inc(h_, 1)
                h_, _ = sb.bump("wk_cast")
                e.tensor_copy(out=uBb[c % 2][:], in_=uBst[c % 2][:]).then_inc(h_, 1)

        def f_pe(e):
            e.wait_ge(sb.hnd("kv_ld"), b["kv_ld"] + 16)
            e.wait_ge(sb.hnd("wk_cast"), b["wk_cast"] + 3)
            for wi, ATb in enumerate([gATb, uATb]):
                for rb in range(4):
                    nn = 4 * wi + rb
                    bb = 4 + nn % 2
                    if nn >= 2:
                        e.wait_ge(sb.hnd("yev"), b["yev"] + nn - 1)
                    for c in range(32):
                        inst = e.matmul(PS[bb][0:16, :],
                                        ATb[:, 16 * c:16 * (c + 1)],
                                        ln2TF[:, NROW * c + 512 * rb:NROW * c + 512 * (rb + 1)],
                                        start=(c == 0), stop=(c == 31))
                    h_, _ = sb.bump("ymm")
                    inst.then_inc(h_, 1)
            pendDA = []
            for c in range(NHC):
                for w in range(2):
                    i = 2 * c + w
                    e.wait_ge(sb.hnd("pan_cast"), b["pan_cast"] + 2 * (i + 1))
                    e.wait_ge(sb.hnd("wk_cast"), b["wk_cast"] + 3 + 2 * (c + 1))
                    Bt = gBb[c % 2] if w == 0 else uBb[c % 2]
                    Asb = gA_sb if w == 0 else uA_sb
                    for rb in range(4):
                        bb = (2 * w) + rb % 2
                        nn = 8 * c + 4 * w + rb
                        if c >= 1 or rb >= 2:
                            if w == 0:
                                e.wait_ge(sb.hnd("gsl"), b["gsl"] + 4 * c + rb - 1)
                            else:
                                e.wait_ge(sb.hnd("hm"), b["hm"] + 4 * c + rb - 1)
                        if w == 0 and c == 0 and rb < 2:
                            e.wait_ge(sb.hnd("yev"), b["yev"] + 8)
                        for cc in range(32):
                            inst = e.matmul(
                                PS[bb][:],
                                gpb[i % 3][:, 128 * cc:128 * (cc + 1)],
                                ln2TF[:, NROW * cc + 512 * rb:NROW * cc + 512 * (rb + 1)],
                                start=(cc == 0), stop=False)
                        e.wait_ge(sb.hnd("yev"), b["yev"] + (1 if w == 0 else 2) * 4)
                        inst = e.matmul(PS[bb][:], Bt[:, :],
                                        Asb[:, 512 * rb:512 * (rb + 1)],
                                        start=False, stop=True)
                        h_, _ = sb.bump("qmm")
                        inst.then_inc(h_, 1)
                    if w == 0 and len(pendDA) > 1:
                        cd = pendDA.pop(0)
                        for rb in range(4):
                            e.wait_ge(sb.hnd("hm"), b["hm"] + 4 * cd + rb + 1)
                            inst = e.matmul(PS[4 + rb][0:16, :],
                                            dATb[:, 16 * cd:16 * (cd + 1)],
                                            h_sb[cd % 2][:, 512 * rb:512 * (rb + 1)],
                                            start=(cd == 0), stop=(cd == NHC - 1))
                            h_, _ = sb.bump("pvd")
                            inst.then_inc(h_, 1)
                pendDA.append(c)
            for cd in pendDA:
                for rb in range(4):
                    e.wait_ge(sb.hnd("hm"), b["hm"] + 4 * cd + rb + 1)
                    inst = e.matmul(PS[4 + rb][0:16, :],
                                    dATb[:, 16 * cd:16 * (cd + 1)],
                                    h_sb[cd % 2][:, 512 * rb:512 * (rb + 1)],
                                    start=(cd == 0), stop=(cd == NHC - 1))
                    h_, _ = sb.bump("pvd")
                    inst.then_inc(h_, 1)

        def f_act(e):
            for c in range(NHC):
                for rb in range(4):
                    nn = 8 * c + rb
                    e.wait_ge(sb.hnd("qmm"), b["qmm"] + 8 * c + rb + 1)
                    if c >= 2:
                        e.wait_ge(sb.hnd("hm"), b["hm"] + 4 * (c - 2) + rb + 1)
                    h_, _ = sb.bump("gsl")
                    e.activation(out=gsil[c % 2][:, 512 * rb:512 * (rb + 1)],
                                 in_=PS[rb % 2][:], func=AF.Silu).then_inc(h_, 1)

        def f_vec(e):
            for wi, Asb in enumerate([gA_sb, uA_sb]):
                for rb in range(4):
                    nn = 4 * wi + rb
                    bb = 4 + nn % 2
                    e.wait_ge(sb.hnd("ymm"), b["ymm"] + nn + 1)
                    h_, _ = sb.bump("yev")
                    e.tensor_copy(out=Asb[:, 512 * rb:512 * (rb + 1)],
                                  in_=PS[bb][0:16, :]).then_inc(h_, 1)
            for c in range(NHC):
                for rb in range(4):
                    e.wait_ge(sb.hnd("qmm"), b["qmm"] + 8 * c + 4 + rb + 1)
                    e.wait_ge(sb.hnd("gsl"), b["gsl"] + 4 * c + rb + 1)
                    if c >= 2:
                        e.wait_ge(sb.hnd(f"qst{c % 2}"), b[f"qst{c % 2}"] + 16 * (c // 2))
                        e.wait_ge(sb.hnd("pvd"), b["pvd"] + 4 * (c - 1))
                    h_, _ = sb.bump("hm")
                    e.tensor_tensor(out=h_sb[c % 2][:, 512 * rb:512 * (rb + 1)],
                                    in0=PS[2 + rb % 2][:],
                                    in1=gsil[c % 2][:, 512 * rb:512 * (rb + 1)],
                                    op=ALU.mult).then_inc(h_, 1)
            e.wait_ge(sb.hnd("pvd"), b["pvd"] + 4 * NHC)
            for rb in range(4):
                h_, _ = sb.bump("yev")
                e.tensor_copy(out=dA_sb[:, 512 * rb:512 * (rb + 1)],
                              in_=PS[4 + rb][0:16, :]).then_inc(h_, 1)

        blk.sync(f_sync)
        blk.gpsimd(f_pool)
        blk.tensor(f_pe)
        blk.scalar(f_act)
        blk.vector(f_vec)
        phase_join(["tensor", "vector", "scalar", "gpsimd"])
        g4.close()

    ph_g4()
    if NPH <= 6:
        g45.close()
        es.close()
        return nc

    def ph_g5():
        g5 = ExitStack()
        panelb = g5.enter_context(nc.sbuf_tensor("panelb", [128, NHC * 512], BF16))
        pn4 = [g5.enter_context(nc.sbuf_tensor(f"pn4{i}", [128, 4 * 512], F32)) for i in range(4)]
        ht = [g5.enter_context(nc.sbuf_tensor(f"ht{i}", [128, NHC * 128], BF16)) for i in range(2)]
        dstg = [g5.enter_context(nc.sbuf_tensor(f"dstg{i}", [128, 512], F32)) for i in range(2)]
        dBTb = g5.enter_context(nc.sbuf_tensor("dBTb", [16, DM], BF16))
        smw2 = g5.enter_context(nc.sbuf_tensor("g5smw", [16, DM], F32))

        b_pj = sb.val("pj")
        b = {k: sb.val(k) for k in ("wkkup", "wk_cast", "pan_cast", "qmm", "qev",
                                    "qst0", "qst1", "qtl0", "qtl1", "gsl",
                                    "wkld0", "wkld1", "wkld2", "wkld3", "dbc")}

        PIECES = [(4 * k, min(4 * k + 4, NHC)) for k in range((NHC + 3) // 4)]
        NPC = len(PIECES)

        def f_sync(e):
            e.wait_ge(sb.hnd("pj"), b_pj)
            for q in range(4):
                e.wait_ge(sb.hnd(f"wkld{q}"), b[f"wkld{q}"])
            h_, _ = sb.bump("wkkup", 16)
            e.dma_start(out=smw2[:, :], in_=downBT_[:, :]).then_inc(h_, 16)
            for d in range(8):
                for k, (c0, c1) in enumerate(PIECES):
                    i = NPC * d + k
                    if i >= 4:
                        e.wait_ge(sb.hnd("pan_cast"), b["pan_cast"] + i - 3)
                    h_, _ = sb.bump(f"wkld{i % 4}", 16)
                    e.dma_start(out=pn4[i % 4][:, 0:512 * (c1 - c0)]
                                .rearrange("p (c n) -> p c n", n=512),
                                in_=downT[128 * c0:128 * c1, 512 * d:512 * (d + 1)]
                                .rearrange("(c p) n -> p c n", p=128)).then_inc(h_, 16)

        def f_pool(e):
            e.wait_ge(sb.hnd("wkkup"), b["wkkup"] + 16)
            h_, _ = sb.bump("wk_cast")
            e.tensor_copy(out=dBTb[:, :], in_=smw2[:, :]).then_inc(h_, 1)
            for d in range(8):
                for k, (c0, c1) in enumerate(PIECES):
                    i = NPC * d + k
                    e.wait_ge(sb.hnd(f"wkld{i % 4}"), b[f"wkld{i % 4}"] + 16 * (i // 4 + 1))
                    if d >= 1:
                        e.wait_ge(sb.hnd("gsl"), b["gsl"] + 11 * (d - 1) + min((c1 - 1) // 8 + 1, 11))
                    h_, _ = sb.bump("pan_cast")
                    e.tensor_copy(out=panelb[:, 512 * c0:512 * c1],
                                  in_=pn4[i % 4][:, 0:512 * (c1 - c0)]).then_inc(h_, 1)
                h_, _ = sb.bump("dbc")
                e.drain().then_inc(h_, 1)

        def f_pe(e):
            e.wait_ge(sb.hnd("wk_cast"), b["wk_cast"] + 1)
            for d in range(8):
                for rs in range(16):
                    n = 16 * d + rs
                    bb = n % 2
                    e.wait_ge(sb.hnd(f"qtl{n % 2}"), b[f"qtl{n % 2}"] + 16 * (n // 2 + 1))
                    if n >= 2:
                        e.wait_ge(sb.hnd("qev"), b["qev"] + n - 1)
                    if rs == 1:
                        e.wait_ge(sb.hnd("dbc"), b["dbc"] + d + 1)
                    for c in range(NHC):
                        if rs == 0 and c % 4 == 0:
                            e.wait_ge(sb.hnd("pan_cast"), b["pan_cast"] + NPC * d + c // 4 + 1)
                        inst = e.matmul(PS[bb][:],
                                        ht[n % 2][:, 128 * c:128 * (c + 1)],
                                        panelb[:, 512 * c:512 * (c + 1)],
                                        start=(c == 0), stop=False)
                        if rs == 15 and (c % 8 == 7 or c == NHC - 1):
                            h_, _ = sb.bump("gsl")
                            inst.then_inc(h_, 1)
                    inst = e.matmul(PS[bb][:], dA_sb[:, 128 * rs:128 * (rs + 1)],
                                    dBTb[:, 512 * d:512 * (d + 1)],
                                    start=False, stop=True)
                    h_, _ = sb.bump("qmm")
                    inst.then_inc(h_, 1)

        def f_vec(e):
            for d in range(8):
                for rs in range(16):
                    n = 16 * d + rs
                    e.wait_ge(sb.hnd("qmm"), b["qmm"] + n + 1)
                    if n >= 2:
                        e.wait_ge(sb.hnd(f"qst{n % 2}"), b[f"qst{n % 2}"] + 16 * (n // 2))
                    h_, _ = sb.bump("qev")
                    e.tensor_copy(out=dstg[n % 2][:], in_=PS[n % 2][:]).then_inc(h_, 1)

        def emit_ht(e, n):
            rs = n % 16
            if n >= 2:
                e.wait_ge(sb.hnd("qmm"), b["qmm"] + n - 1)
            h_, _ = sb.bump(f"qtl{n % 2}", 16)
            e.dma_start(out=ht[n % 2][:].rearrange("p (c n) -> p c n", c=NHC),
                        in_=h_sc[NHC * 128 * rs:NHC * 128 * (rs + 1), :]
                        .rearrange("(c p) n -> p c n", p=128)).then_inc(h_, 16)

        def f_store(e):
            e.wait_ge(sb.hnd("pj"), b_pj)
            e.wait_ge(sb.hnd("qst0"), b["qst0"])
            e.wait_ge(sb.hnd("qst1"), b["qst1"])
            emit_ht(e, 0)
            emit_ht(e, 1)
            for d in range(8):
                for rs in range(16):
                    n = 16 * d + rs
                    if n + 2 < 128:
                        emit_ht(e, n + 2)
                    e.wait_ge(sb.hnd("qev"), b["qev"] + n + 1)
                    h_, _ = sb.bump(f"qst{n % 2}", 16)
                    e.dma_start(out=d_sc[128 * rs:128 * (rs + 1), 512 * d:512 * (d + 1)],
                                in_=dstg[n % 2][:]).then_inc(h_, 16)
            e.wait_ge(sb.hnd("qst0"), sb.val("qst0"))
            e.wait_ge(sb.hnd("qst1"), sb.val("qst1"))

        blk.sync(f_sync)
        blk.gpsimd(f_pool)
        blk.tensor(f_pe)
        blk.vector(f_vec)
        blk.scalar(f_store)
        phase_join(["tensor", "vector", "gpsimd", "scalar"])
        g5.close()

    ph_g5()
    g45.close()
    if NPH <= 7:
        es.close()
        return nc

    def ph_p7():
        p7 = ExitStack()
        dr = [p7.enter_context(nc.sbuf_tensor(f"p7d{i}", [128, DM], F32)) for i in range(2)]
        x1r = [p7.enter_context(nc.sbuf_tensor(f"p7x{i}", [128, DM], F32)) for i in range(2)]
        t2b = [p7.enter_context(nc.sbuf_tensor(f"p7t{i}", [128, DM], F32)) for i in range(2)]
        outb = [p7.enter_context(nc.sbuf_tensor(f"p7o{i}", [128, DM], F32)) for i in range(2)]
        sqd = p7.enter_context(nc.sbuf_tensor("p7sqd", [128, DM], BF16))

        NJ = 16
        b_pj = sb.val("pj")
        b = {k: sb.val(k) for k in ("xld0", "xld1", "qtl0", "qtl1",
                                    "sq", "sqB", "var", "varB", "rstd", "rstdB",
                                    "negmr", "negmrB", "apply", "applyB", "x1d",
                                    "ost0", "ost1", "qst0", "qst1")}

        def emit_out_j(e, j):
            p = j % 2
            e.wait_ge(sb.hnd("x1d"), b["x1d"] + j + 1)
            h_, _ = sb.bump(f"ost{p}", 16)
            e.dma_start(out=out[128 * j:128 * (j + 1), :],
                        in_=outb[p][:]).then_inc(h_, 16)

        def f_sync(e):
            e.wait_ge(sb.hnd("pj"), b_pj)
            e.wait_ge(sb.hnd("qst0"), b["qst0"])
            e.wait_ge(sb.hnd("qst1"), b["qst1"])
            for j in range(NJ):
                p = j % 2
                e.wait_ge(sb.hnd(_ps("apply", p)),
                          b[_ps("apply", p)] + (j // 2 if j >= 2 else 0))
                h_, _ = sb.bump(f"xld{j % 2}", 16)
                e.dma_start(out=dr[j % 2][:],
                            in_=d_sc[128 * j:128 * (j + 1), :]).then_inc(h_, 16)
                e.wait_ge(sb.hnd("x1d"), b["x1d"] + max(j - 1, 0))
                h_, _ = sb.bump(f"qtl{j % 2}", 16)
                e.dma_start(out=x1r[j % 2][:],
                            in_=x1_sc[128 * j:128 * (j + 1), :]).then_inc(h_, 16)
                if j >= 1:
                    emit_out_j(e, j - 1)
            emit_out_j(e, NJ - 1)

        def f_vec(e):
            base = dict(b); base["xld"] = [b["xld0"], b["xld1"]]
            for j in range(NJ):
                p = j % 2
                stats_vec_j(e, j, p, dr[p][:], base)
                e.wait_ge(sb.hnd(_ps("apply", p)), b[_ps("apply", p)] + j // 2 + 1)
                e.wait_ge(sb.hnd(f"qtl{p}"), b[f"qtl{p}"] + 16 * (j // 2 + 1))
                if j >= 2:
                    e.wait_ge(sb.hnd(f"ost{p}"), b[f"ost{p}"] + 16 * (j // 2))
                e.drain()
                h_, _ = sb.bump("x1d")
                e.tensor_tensor(out=outb[p][:], in0=t2b[p][:], in1=x1r[p][:],
                                op=ALU.add).then_inc(h_, 1)

        def f_act(e):
            base = dict(b); base["xld"] = [b["xld0"], b["xld1"]]
            sqd_ref[0] = sqd
            for j in range(NJ):
                p = j % 2
                ew = ("x1d", b["x1d"] + j - 1) if j >= 2 else None
                stats_act_j(e, j, p, dr[p][:], t2b[p][:], base,
                            extra_tr_wait=None, extra_wait=ew)

        blk.sync(f_sync)
        blk.vector(f_vec)
        blk.scalar(f_act)

        def f_final(e):
            e.wait_ge(sb.hnd("ost0"), sb.val("ost0"))
            e.wait_ge(sb.hnd("ost1"), sb.val("ost1"))
        blk.sync(f_final)
        p7.close()

    ph_p7()

    dbg_map = {"ao": ao_sc, "x1": x1_sc, "ln2": ln2_sc, "h": h_sc, "d": d_sc}
    todump = [nm for nm in _DBG.get("dumps", []) if nm in dbg_map]
    if todump:
        def f_dbg(e):
            e.wait_ge(sb.hnd("pj"), sb.val("pj"))
            e.wait_ge(sb.hnd("ost0"), sb.val("ost0"))
            e.wait_ge(sb.hnd("ost1"), sb.val("ost1"))
            for nm in todump:
                h_, _ = sb.bump("dbg", 16)
                sc = dbg_map[nm]
                e.dma_start(out=dbg_outs[nm][:, :].rearrange("(c p) n -> p c n", p=128),
                            in_=sc[:, :].rearrange("(c p) n -> p c n", p=128)).then_inc(h_, 16)
            e.wait_ge(sb.hnd("dbg"), sb.val("dbg"))
        blk.scalar(f_dbg)

    es.close()
    return nc


_PERM = (np.arange(8)[:, None] + 8 * np.arange(512)[None, :]).reshape(-1)


def _build_masks(starts):
    m = np.zeros((NT, 8, 128, TQ), np.float32)
    for t in range(NT):
        n = SWEEPS[t]
        r0 = starts[t]
        for ci in range(8):
            c = n - 8 + ci
            k = 128 * c + np.arange(128)[:, None]
            q = r0 + np.arange(TQ)[None, :]
            m[t, ci] = (k <= q)
    return m.reshape(NT * 8 * 128, TQ).astype(ml_dtypes.bfloat16)


def kernel(**inp):
    x = np.ascontiguousarray(np.asarray(inp["L0_attn_hidden_states"], np.float32))
    B = x.shape[0]

    def T(a):
        return np.ascontiguousarray(np.asarray(a, np.float32).T)

    Wkd = np.asarray(inp["L0_attn_Wk_down"], np.float32)
    Wvd = np.asarray(inp["L0_attn_Wv_down"], np.float32)
    wkv = np.empty((512, DM), np.float32)
    for h in range(8):
        wkv[64 * h:64 * h + 32] = Wkd[32 * h:32 * h + 32]
        wkv[64 * h + 32:64 * h + 64] = Wvd[32 * h:32 * h + 32]

    weights = {
        "WqT": T(inp["L0_attn_Wq"]),
        "WkvT": np.ascontiguousarray(wkv.T),
        "WkupT": T(inp["L0_attn_Wk_up"]),
        "WvupT": T(inp["L0_attn_Wv_up"]),
        "WoT": T(inp["L0_attn_Wo"]),
        "gateT": T(inp["L0_ffn_gate_w"]),
        "upT": T(inp["L0_ffn_up_w"]),
        "downT": T(inp["L0_ffn_down_w"]),
        "gateAT": T(inp["L0_ffn_gate_a"]),
        "gateBT": T(inp["L0_ffn_gate_b"]),
        "upAT": T(inp["L0_ffn_up_a"]),
        "upBT": T(inp["L0_ffn_up_b"]),
        "downAT": T(inp["L0_ffn_down_a"]),
        "downBT": T(inp["L0_ffn_down_b"]),
    }
    mask_p = _build_masks(ROWS_P)
    mask_q = _build_masks(ROWS_Q)

    nc = build_program()
    in_maps = []
    for c in range(8):
        b = c // 2
        starts = ROWS_P if c % 2 == 0 else ROWS_Q
        xo = np.concatenate([x[b, s:s + TQ] for s in starts], axis=0)
        m = {"xb_perm": np.ascontiguousarray(x[b][_PERM]),
             "x_own": np.ascontiguousarray(xo),
             "masks": mask_p if c % 2 == 0 else mask_q}
        m.update(weights)
        in_maps.append(m)

    res = run_bass_kernel_spmd(nc, in_maps, core_ids=list(range(8)))
    outp = np.empty((B, S, DM), np.float32)
    for c in range(8):
        b = c // 2
        starts = ROWS_P if c % 2 == 0 else ROWS_Q
        oc = np.asarray(res.results[c]["out"], np.float32)
        for t, s in enumerate(starts):
            outp[b, s:s + TQ] = oc[TQ * t:TQ * (t + 1)]
    return outp
